# version 05
# baseline (speedup 1.0000x reference)
import numpy as np
from contextlib import ExitStack

import concourse.bass as bass
import concourse.mybir as mybir
from concourse.bass_utils import run_bass_kernel_spmd

F32 = mybir.dt.float32
BF16 = mybir.dt.bfloat16
ALU = mybir.AluOpType
AF = mybir.ActivationFunctionType
AX = mybir.AxisListType

D = 1024
DIN = 2048
EH = 1024
NST = 16
RNK = 64
TSEQ = 4096
TT = 512
HT = 256
NCH = TSEQ // TT
KC = D // 128
DC = EH // 128
MCXZ = 2 * EH // 128
NKO = DIN // 128
HF = 4096
HKC = HF // 128
DEBUG_TAPS = False
EPS1 = 1e-5
EPS2 = 1e-6
EPSF = 1e-5
AMEPS = 1e-6
RSQ2 = 0.7071067811865476
PAIRS = [[0, 1], [2, 3], [4, 5], [6, 7]]


def build_nc():
    nc = bass.Bass("TRN2", target_bir_lowering=False)

    def ein(name, shape, dtype=F32):
        return nc.dram_tensor(name, list(shape), dtype, kind="ExternalInput").ap()

    x_in = ein("x", (TSEQ, D))
    w_inT = ein("w_inT", (D, 2 * EH))
    ln_g = ein("ln_g", (128, KC))
    ln_b = ein("ln_b", (128, KC))
    conv_w = ein("conv_w", (128, DC * 4))
    conv_b = ein("conv_b", (128, DC))
    w_xT = ein("w_xT", (EH, 96))
    w_dtT = ein("w_dtT", (RNK, EH))
    b_dt = ein("b_dt", (128, DC))
    a_log = ein("a_log", (128, DC * NST))
    dp_in = ein("dp", (128, DC))
    w_outT = ein("w_outT", (DIN, D))
    w_fcT = ein("w_fcT", (D, HF))
    lnw_f = ein("lnw_f", (128, HKC))
    lnb_f = ein("lnb_f", (128, HKC))
    w_projT = ein("w_projT", (HF, D))
    ident = ein("ident", (128, 128))
    x_own = ein("x_own", (NCH, HT, D))

    y_out = nc.dram_tensor("y", [NCH, HT, D], F32, kind="ExternalOutput").ap()
    taps = {}
    if DEBUG_TAPS:
        for nm, shape, dt_ in (("t_hf", (128, KC * TT // 2), F32),
                               ("t_xi", (128, DC, TT + 4), F32),
                               ("t_sz", (128, DC, TT), F32),
                               ("t_u", (128, DC, TT), F32),
                               ("t_dt", (128, DC, TT), F32),
                               ("t_xdbl", (96, TT), F32),
                               ("t_yacc", (128, DC, TT), F32),
                               ("t_y2h", (128, DC * TT // 2), F32),
                               ("t_om", (128, 2, HT), F32),
                               ("t_epim", (128, 2, D), F32),
                               ("t_xs1", (128, 2, D), F32),
                               ("t_thr", (128, 4), F32),
                               ("t_wfc", (128, HF // 2), F32),
                               ("t_wpj", (128, HKC * D // 2), F32),
                               ("t_nswbo", (128, 16), F32),
                               ("t_xlnf", (128, KC * HT // 2), F32),
                               ("t_gbf", (128, HKC * HT // 2), F32),
                               ("t_rows", (1, 2 * HT), F32),
                               ("t_fotok", (128, 2, D), F32)):
            taps[nm] = nc.dram_tensor(nm, list(shape), dt_, kind="ExternalOutput").ap()

    e1_in = nc.dram_tensor("e1_in", [NCH, 96, TT], F32, kind="Internal").ap()
    e1_out = nc.dram_tensor("e1_out", [NCH, 96, TT], F32, kind="Internal").ap()
    e2_in = nc.dram_tensor("e2_in", [NCH, EH, TT], BF16, kind="Internal").ap()
    e2_out = nc.dram_tensor("e2_out", [NCH, DIN, TT], BF16, kind="Internal").ap()
    wout_bf = nc.dram_tensor("wout_bf", [NKO, 128, D], BF16, kind="Internal").ap()
    xs1_sp = nc.dram_tensor("xs1_sp", [NCH, HT, D], F32, kind="Internal").ap()
    row_dram = nc.dram_tensor("row_dram", [NCH, 2, HT], F32, kind="Internal").ap()
    thr_dram = nc.dram_tensor("thr_dram", [1, 4], F32, kind="Internal").ap()

    ctx = ExitStack()
    ctxM = ExitStack()
    sb = lambda st, name, shape, dtype=F32: st.enter_context(
        nc.sbuf_tensor(name, list(shape), dtype))
    psum = lambda name, shape: ctx.enter_context(
        nc.psum_tensor(name, list(shape), F32))
    sem = lambda name: ctx.enter_context(nc.semaphore(name))

    a_sb = sb(ctx, "a_sb", (128, DC * NST))
    g_sb = sb(ctx, "g_sb", (128, KC))
    lnb_sb = sb(ctx, "lnb_sb", (128, KC))
    convw_sb = sb(ctx, "convw_sb", (128, DC * 4))
    convb_sb = sb(ctx, "convb_sb", (128, DC))
    bdt_sb = sb(ctx, "bdt_sb", (128, DC))
    dp_sb = sb(ctx, "dp_sb", (128, DC))
    winb_sb = sb(ctx, "winb_sb", (128, MCXZ))
    id_f = sb(ctx, "id_f", (128, 128))
    id_b = sb(ctx, "id_b", (128, 128), BF16)
    ones_f = sb(ctx, "ones_f", (128, 1))
    ones_b = sb(ctx, "ones_b", (128, 1), BF16)
    thr_sb = sb(ctx, "thr_sb", (128, 4))
    lnwf_sb = sb(ctx, "lnwf_sb", (128, HKC))
    lnbf_sb = sb(ctx, "lnbf_sb", (128, HKC))
    lnwf_b = sb(ctx, "lnwf_b", (128, HKC), BF16)
    lnbf_b = sb(ctx, "lnbf_b", (128, HKC), BF16)
    gs_sb = sb(ctx, "gs_sb", (128, 2))
    tiny = sb(ctx, "tiny", (1, 8))
    st6 = sb(ctx, "st6", (128, 12))
    st2 = sb(ctx, "st2", (128, 2))
    rs_sb = sb(ctx, "rs_sb", (128, 4))
    cst = sb(ctx, "cst", (128, 6))

    w_in_sb = sb(ctxM, "w_in_sb", (128, KC, 2 * EH), BF16)
    w_x_sb = sb(ctxM, "w_x_sb", (128, KC * 96), BF16)
    w_dt_sb = sb(ctxM, "w_dt_sb", (RNK, EH), BF16)
    big = sb(ctxM, "big", (128, 2 * D))
    xtile = sb(ctxM, "xtile", (128, D))
    h_tok = sb(ctxM, "h_tok", (128, D), BF16)
    h_f = sb(ctxM, "h_f", (128, KC, TT), BF16)
    xi = sb(ctxM, "xi", (128, DC, TT + 4))
    sz = sb(ctxM, "sz", (128, DC, TT))
    u_sb = sb(ctxM, "u_sb", (128, DC, TT))
    ubf = sb(ctxM, "ubf", (128, TT), BF16)
    dt_sb = sb(ctxM, "dt_sb", (128, DC, TT))
    dtu_sb = sb(ctxM, "dtu_sb", (128, DC, TT))
    dtin_bf = sb(ctxM, "dtin_bf", (RNK, TT), BF16)
    sp1 = sb(ctxM, "sp1", (128, TT))
    sp2 = sb(ctxM, "sp2", (128, TT))
    sp3 = sb(ctxM, "sp3", (128, TT))
    bbc = sb(ctxM, "bbc", (128, 2, TT))
    cbc = sb(ctxM, "cbc", (128, 2, TT))
    da_scr = sb(ctxM, "da_scr", (128, TT))
    db_scr = sb(ctxM, "db_scr", (128, TT))
    hn_scr = sb(ctxM, "hn_scr", (128, TT))
    yp_scr = sb(ctxM, "yp_scr", (128, TT))
    yacc = sb(ctxM, "yacc", (128, DC, TT))
    h_carry = sb(ctxM, "h_carry", (128, DC * NST))
    y2h = sb(ctxM, "y2h", (128, DC, TT), BF16)
    y2kc = sb(ctxM, "y2kc", (128, 2, HT), BF16)
    wo_st = sb(ctxM, "wo_st", (128, 2, D), BF16)
    epi = sb(ctxM, "epi", (128, D))
    epim = sb(ctxM, "epim", (128, 2, D))

    pm = [psum(f"pm{i}", (128, TT)) for i in range(8)]

    s_tp = sem("s_tp")
    s_ld = sem("s_ld"); s_gl = sem("s_gl"); s_mm = sem("s_mm")
    s_sc = sem("s_sc"); s_ve = sem("s_ve"); s_cc = sem("s_cc"); s_st = sem("s_st")
    s_x = sem("s_x"); s_xd = sem("s_xd"); s_bb = sem("s_bb"); s_cb = sem("s_cb")
    s_wo = sem("s_wo"); s_y2 = sem("s_y2"); s_wi = sem("s_wi"); s_pc = sem("s_pc")
    s_rw = sem("s_rw")
    S = {"ld": s_ld, "gl": s_gl, "mm": s_mm, "sc": s_sc, "ve": s_ve,
         "cc": s_cc, "st": s_st, "x": s_x, "xd": s_xd, "bb": s_bb, "cb": s_cb,
         "wo": s_wo, "y2": s_y2, "wi": s_wi, "pc": s_pc, "rw": s_rw}
    c = {k: 0 for k in S}

    block = ctx.enter_context(nc.Block())

    DMA_KEYS = ("ld", "gl", "st", "x", "xd", "bb", "cb", "wo", "y2", "wi",
                "pc", "rw")
    def inc(instr, key, n=None):
        if n is None:
            n = 16 if key in DMA_KEYS else 1
        instr.then_inc(S[key], n)
        c[key] += n
        return c[key]

    tapcnt = [0]
    def tap(name, src_ap, gates=()):
        if not DEBUG_TAPS:
            return
        @block.scalar
        def _(s, name=name, src_ap=src_ap, gates=tuple(gates)):
            for key, val in gates:
                s.wait_ge(S[key], val)
            s.dma_start(out=taps[name], in_=src_ap).then_inc(s_tp, 16)
            tapcnt[0] += 16
            s.wait_ge(s_tp, tapcnt[0])



    @block.sync
    def _(sync):
        for dst, srcap in ((id_f, ident), (g_sb, ln_g), (lnb_sb, ln_b),
                           (convw_sb, conv_w), (convb_sb, conv_b),
                           (bdt_sb, b_dt), (dp_sb, dp_in),
                           (lnwf_sb, lnw_f), (lnbf_sb, lnb_f)):
            inc(sync.dma_start(out=dst[:], in_=srcap), "pc")
        inc(sync.dma_start(out=sp1[0:RNK, 0:TT], in_=w_dtT[:, 0:TT]), "pc")
        inc(sync.dma_start(out=sp2[0:RNK, 0:TT], in_=w_dtT[:, TT:EH]), "pc")
        inc(sync.dma_start(out=big[:, 0:KC * 96].rearrange("p (k c) -> p k c", k=KC),
                           in_=w_xT.rearrange("(k p) c -> p k c", p=128)), "pc")
        inc(sync.dma_start(out=big[:, 1024:1024 + DC * NST], in_=a_log), "pc")
    ld_consts = c["pc"]

    @block.vector
    def _(v):
        v.wait_ge(s_pc, ld_consts)
        inc(v.memset(ones_f[:], 1.0), "ve")
        inc(v.memset(cst[:, 0:1], EPS1), "ve")
        inc(v.memset(cst[:, 1:2], EPS2), "ve")
        inc(v.memset(cst[:, 2:3], 1.0), "ve")
        inc(v.memset(cst[:, 3:4], -1.0), "ve")
        inc(v.memset(cst[:, 4:5], RSQ2), "ve")
        inc(v.memset(ones_b[:], 1.0), "ve")
        inc(v.memset(h_carry[:], 0.0), "ve")
        inc(v.memset(xi[:, :, 0:4].opt(), 0.0), "ve")
        inc(v.tensor_copy(out=id_b[:], in_=id_f[:]), "ve")
        inc(v.tensor_copy(out=w_dt_sb[:, 0:TT], in_=sp1[0:RNK, 0:TT]), "ve")
        inc(v.tensor_copy(out=w_dt_sb[:, TT:EH], in_=sp2[0:RNK, 0:TT]), "ve")
        inc(v.tensor_copy(out=w_x_sb[:], in_=big[:, 0:KC * 96]), "ve")
        inc(v.tensor_copy(out=lnwf_b[:], in_=lnwf_sb[:]), "ve")
        inc(v.tensor_copy(out=lnbf_b[:], in_=lnbf_sb[:]), "ve")
    ve_consts = c["ve"]

    @block.scalar
    def _(s):
        s.wait_ge(s_pc, ld_consts)
        inc(s.activation(out=a_sb[:], in_=big[:, 1024:1024 + DC * NST], func=AF.Exp), "sc")
    sc_aexp = c["sc"]
    @block.vector
    def _(v):
        v.wait_ge(s_sc, sc_aexp)
        inc(v.tensor_scalar(out=a_sb[:], in0=a_sb[:], scalar1=-1.0, scalar2=None,
                            op0=ALU.mult), "ve")
    ve_aneg = c["ve"]

    win_cast_at = {}
    for kc in range(KC):
        buf = big[:, 0:2048]
        gate = win_cast_at.get(kc - 1)
        @block.sync
        def _(sync, buf=buf, kc=kc, gate=gate):
            if kc == 0:
                sync.wait_ge(s_ve, ve_consts)
                sync.wait_ge(s_sc, sc_aexp)
            if gate is not None:
                sync.wait_ge(s_sc, gate)
            inc(sync.dma_start(out=buf, in_=w_inT[kc * 128:(kc + 1) * 128, :]), "wi")
        ld_here = c["wi"]
        @block.tensor
        def _(t, kc=kc, buf=buf, ld_here=ld_here, ve_gate=c["ve"]):
            t.wait_ge(s_wi, ld_here)
            if kc > 0:
                t.wait_ge(s_ve, ve_gate)
            for mc in range(MCXZ):
                inc(t.matmul(pm[6][:, mc:mc + 1], buf[:, mc * 128:(mc + 1) * 128],
                             lnb_sb[:, kc:kc + 1], start=True, stop=True), "mm")
        mm_here = c["mm"]
        @block.vector
        def _(v, kc=kc, mm_here=mm_here):
            v.wait_ge(s_mm, mm_here)
            if kc == 0:
                inc(v.tensor_copy(out=winb_sb[:], in_=pm[6][:, 0:MCXZ]), "ve")
            else:
                inc(v.tensor_tensor(out=winb_sb[:], in0=winb_sb[:],
                                    in1=pm[6][:, 0:MCXZ], op=ALU.add), "ve")
            v.wait_ge(s_ve, c["ve"])
        @block.scalar
        def _(s, kc=kc, buf=buf, mm_here=mm_here):
            s.wait_ge(s_mm, mm_here)
            inc(s.activation(out=w_in_sb[:, kc, :], in_=buf, func=AF.Copy,
                             scale=g_sb[:, kc:kc + 1]), "sc")
        win_cast_at[kc] = c["sc"]
    sc_winb = c["sc"]

    wout_st_at = {}
    for kc in range(NKO):
        buf = big[:, (kc % 2) * 1024:(kc % 2) * 1024 + 1024]
        gate = wout_st_at.get(kc - 2)
        @block.gpsimd
        def _(g, kc=kc, buf=buf, gate=gate):
            if kc == 0:
                g.wait_ge(s_sc, win_cast_at[KC - 1])
            if gate is not None:
                g.wait_ge(s_st, gate)
            inc(g.dma_start(out=buf, in_=w_outT[kc * 128:(kc + 1) * 128, :]), "gl")
        gl_here = c["gl"]
        @block.vector
        def _(v, kc=kc, buf=buf, gl_here=gl_here, ve_gate=c["ve"]):
            v.wait_ge(s_gl, gl_here)
            inc(v.tensor_copy(out=wo_st[:, kc % 2, :], in_=buf), "ve")
        ve_here = c["ve"]
        @block.sync
        def _(sync, kc=kc, ve_here=ve_here):
            sync.wait_ge(s_ve, ve_here)
            inc(sync.dma_start(out=wout_bf[kc], in_=wo_st[:, kc % 2, :]), "st")
        wout_st_at[kc] = c["st"]

    @block.vector
    def _(v):
        inc(v.memset(gs_sb[:], 0.0), "ve")
    gamma_prev = c["ve"]
    gl_gamma_last = c["gl"]
    for wi, (wT, nk, cols) in enumerate(((w_fcT, KC, HF), (w_projT, HKC, D))):
        nh = 2 if cols > 2048 else 1
        hw = cols // nh
        for kc in range(nk):
            for hh in range(nh):
                @block.gpsimd
                def _(g, wT=wT, kc=kc, hh=hh, hw=hw, gate=gamma_prev):
                    g.wait_ge(s_ve, gate)
                    inc(g.dma_start(out=big[:, 0:hw],
                                    in_=wT[kc * 128:(kc + 1) * 128,
                                           hh * hw:(hh + 1) * hw]), "gl")
                gl_here = c["gl"]
                @block.vector
                def _(v, wi=wi, hw=hw, gl_here=gl_here):
                    v.wait_ge(s_gl, gl_here)
                    inc(v.tensor_reduce(out=st2[:, 0:1], in_=big[:, 0:hw], axis=AX.X,
                                        op=ALU.add, apply_absolute_value=True), "ve")
                    v.wait_ge(s_ve, c["ve"])
                    inc(v.tensor_tensor(out=gs_sb[:, wi:wi + 1], in0=gs_sb[:, wi:wi + 1],
                                        in1=st2[:, 0:1], op=ALU.add), "ve")
                gamma_prev = c["ve"]
    gl_gamma_last = c["gl"]
    ve_gamma = c["ve"]
    @block.tensor
    def _(t):
        t.wait_ge(s_ve, ve_gamma)
        inc(t.matmul(pm[7][0:1, 0:1], gs_sb[:, 0:1], ones_f[:], start=True, stop=True), "mm")
        inc(t.matmul(pm[7][0:1, 1:2], gs_sb[:, 1:2], ones_f[:], start=True, stop=True), "mm")
    mm_gam = c["mm"]
    @block.vector
    def _(v):
        v.wait_ge(s_mm, mm_gam)
        inc(v.tensor_scalar(out=tiny[0:1, 0:1], in0=pm[7][0:1, 0:1],
                            scalar1=0.5 / (HF * D), scalar2=0.5 * AMEPS,
                            op0=ALU.mult, op1=ALU.add), "ve")
        v.wait_ge(s_ve, c["ve"])
        inc(v.tensor_scalar(out=tiny[0:1, 1:2], in0=tiny[0:1, 0:1],
                            scalar1=-1.0, scalar2=None, op0=ALU.mult), "ve")
        inc(v.tensor_scalar(out=tiny[0:1, 2:3], in0=pm[7][0:1, 1:2],
                            scalar1=0.5 / (HF * D), scalar2=0.5 * AMEPS,
                            op0=ALU.mult, op1=ALU.add), "ve")
        v.wait_ge(s_ve, c["ve"])
        inc(v.tensor_scalar(out=tiny[0:1, 3:4], in0=tiny[0:1, 2:3],
                            scalar1=-1.0, scalar2=None, op0=ALU.mult), "ve")
    ve_thr = c["ve"]
    @block.sync
    def _(sync):
        sync.wait_ge(s_ve, ve_thr)
        inc(sync.dma_start(out=thr_dram, in_=tiny[0:1, 0:4]), "st")
    st_thr = c["st"]
    @block.sync
    def _(sync):
        sync.wait_ge(s_st, st_thr)
        inc(sync.dma_start(out=thr_sb[:], in_=thr_dram.to_broadcast([128, 4])), "pc")
    ld_thr = c["pc"]

    for k in range(NCH):
        for tt in range(4):
            @block.sync
            def _(sync, k=k, tt=tt, gate_ve=c["ve"]):
                sync.wait_ge(s_ve, gate_ve)
                inc(sync.dma_start(out=xtile[:],
                                   in_=x_in[k * TT + tt * 128: k * TT + (tt + 1) * 128, :]), "x")
            ld_x = c["x"]
            mtile = big[:, (tt % 2) * D:(tt % 2 + 1) * D]
            @block.vector
            def _(v, ld_x=ld_x, k=k, tt=tt, gl_gate=gl_gamma_last):
                v.wait_ge(s_x, ld_x)
                if k == 0 and tt == 0:
                    v.wait_ge(s_ve, ve_aneg)
                    v.wait_ge(s_gl, gl_gate)
                inc(v.bn_stats(out=st6[:, 0:6], in_=xtile[:, 0:512]), "ve")
                inc(v.bn_stats(out=st6[:, 6:12], in_=xtile[:, 512:1024]), "ve")
                v.wait_ge(s_ve, c["ve"])
                inc(v.bn_aggr(out=st2[:], in_=st6[:]), "ve")
            ve_stat1 = c["ve"]
            @block.scalar
            def _(s, ve_stat1=ve_stat1):
                s.wait_ge(s_ve, ve_stat1)
                inc(s.activation(out=rs_sb[:, 0:1], in_=st2[:, 1:2], func=AF.Sqrt,
                                 bias=cst[:, 0:1]), "sc")
            sc_r1 = c["sc"]
            @block.vector
            def _(v, mtile=mtile, sc_r1=sc_r1):
                v.wait_ge(s_sc, sc_r1)
                inc(v.reciprocal(out=rs_sb[:, 0:1], in_=rs_sb[:, 0:1]), "ve")
                _ = v.wait_ge(s_ve, c["ve"])
                inc(v.tensor_scalar(out=mtile, in0=xtile[:], scalar1=st2[:, 0:1],
                                    scalar2=rs_sb[:, 0:1], op0=ALU.subtract,
                                    op1=ALU.mult), "ve")
                inc(v.bn_stats(out=st6[:, 0:6], in_=mtile[:, 0:512]), "ve")
                inc(v.bn_stats(out=st6[:, 6:12], in_=mtile[:, 512:1024]), "ve")
                v.wait_ge(s_ve, c["ve"])
                inc(v.bn_aggr(out=st2[:], in_=st6[:]), "ve")
            ve_stat2 = c["ve"]
            @block.scalar
            def _(s, ve_stat2=ve_stat2):
                s.wait_ge(s_ve, ve_stat2)
                inc(s.activation(out=rs_sb[:, 1:2], in_=st2[:, 1:2], func=AF.Sqrt,
                                 bias=cst[:, 1:2]), "sc")
            sc_r2 = c["sc"]
            @block.vector
            def _(v, mtile=mtile, sc_r2=sc_r2):
                v.wait_ge(s_sc, sc_r2)
                inc(v.reciprocal(out=rs_sb[:, 1:2], in_=rs_sb[:, 1:2]), "ve")
                _ = v.wait_ge(s_ve, c["ve"])
                inc(v.tensor_scalar(out=h_tok[:], in0=mtile, scalar1=st2[:, 0:1],
                                    scalar2=rs_sb[:, 1:2], op0=ALU.subtract,
                                    op1=ALU.mult), "ve")
            ve_h = c["ve"]
            @block.tensor
            def _(t, tt=tt, ve_h=ve_h, sc_gate=c["sc"], ve_gate=c["ve"]):
                t.wait_ge(s_ve, max(ve_h, ve_gate))
                t.wait_ge(s_sc, sc_gate)
                pmb = pm[4][:].bitcast(BF16)
                for kc in range(KC):
                    inc(t.matmul(pmb[:, kc * 128:(kc + 1) * 128],
                                 h_tok[:, kc * 128:(kc + 1) * 128], id_b[:],
                                 start=True, stop=True, is_transpose=True), "mm")
            mm_tr = c["mm"]
            @block.scalar
            def _(s, tt=tt, mm_tr=mm_tr):
                s.wait_ge(s_mm, mm_tr)
                inc(s.activation(out=h_f[:, :, tt * 128:(tt + 1) * 128],
                                 in_=pm[4][:].bitcast(BF16).rearrange(
                                     "p (kc t) -> p kc t", kc=KC),
                                 func=AF.Copy), "sc")
        sc_hf = c["sc"]
        if k == 0:
            tap("t_hf", h_f[:].rearrange("p a b -> p (a b)").bitcast(F32),
                gates=[("sc", c["sc"])])

        for mc in range(MCXZ):
            bank = pm[mc % 4]
            @block.tensor
            def _(t, mc=mc, bank=bank, sc_gate=c["sc"], ve_gate=c["ve"]):
                t.wait_ge(s_sc, sc_gate)
                t.wait_ge(s_ve, ve_gate)
                for kc in range(KC):
                    inc(t.matmul(bank[:], w_in_sb[:, kc, mc * 128:(mc + 1) * 128],
                                 h_f[:, kc, :], start=(kc == 0), stop=(kc == KC - 1)),
                        "mm")
            mm_mc = c["mm"]
            @block.scalar
            def _(s, mc=mc, bank=bank, mm_mc=mm_mc, ve_gate=c["ve"]):
                s.wait_ge(s_mm, mm_mc)
                if mc == 0:
                    s.wait_ge(s_ve, ve_gate)
                if mc < DC:
                    inc(s.activation(out=xi[:, mc, 4:4 + TT], in_=bank[:],
                                     func=AF.Identity,
                                     bias=winb_sb[:, mc:mc + 1]), "sc")
                else:
                    inc(s.activation(out=sz[:, mc - DC, :], in_=bank[:], func=AF.Silu,
                                     bias=winb_sb[:, mc:mc + 1]), "sc")
        sc_xz = c["sc"]
        if k == 0:
            tap("t_xi", xi[:], gates=[("sc", c["sc"])])
            tap("t_sz", sz[:], gates=[("sc", c["sc"])])

        for dc in range(DC):
            @block.vector
            def _(v, dc=dc, sc_xz=sc_xz):
                v.wait_ge(s_sc, sc_xz)
                inc(v.tensor_scalar(out=sp1[:], in0=xi[:, dc, 1:1 + TT],
                                    scalar1=convw_sb[:, dc * 4:dc * 4 + 1],
                                    scalar2=convb_sb[:, dc:dc + 1],
                                    op0=ALU.mult, op1=ALU.add), "ve")
                inc(v.scalar_tensor_tensor(out=sp2[:], in0=xi[:, dc, 2:2 + TT],
                                           scalar=convw_sb[:, dc * 4 + 1:dc * 4 + 2],
                                           in1=sp1[:], op0=ALU.mult, op1=ALU.add), "ve")
                inc(v.scalar_tensor_tensor(out=sp1[:], in0=xi[:, dc, 3:3 + TT],
                                           scalar=convw_sb[:, dc * 4 + 2:dc * 4 + 3],
                                           in1=sp2[:], op0=ALU.mult, op1=ALU.add), "ve")
                inc(v.scalar_tensor_tensor(out=sp2[:], in0=xi[:, dc, 4:4 + TT],
                                           scalar=convw_sb[:, dc * 4 + 3:dc * 4 + 4],
                                           in1=sp1[:], op0=ALU.mult, op1=ALU.add), "ve")
                inc(v.tensor_copy(out=xi[:, dc, 1:4], in_=xi[:, dc, 1 + TT:4 + TT]), "ve")
            ve_cpre = c["ve"]
            @block.scalar
            def _(s, dc=dc, ve_cpre=ve_cpre):
                s.wait_ge(s_ve, ve_cpre)
                inc(s.activation(out=u_sb[:, dc, :], in_=sp2[:], func=AF.Silu), "sc")
            sc_u = c["sc"]
            @block.vector
            def _(v, dc=dc, sc_u=sc_u, mm_gate=c["mm"]):
                v.wait_ge(s_sc, sc_u)
                if dc > 0:
                    v.wait_ge(s_mm, mm_gate)
                inc(v.tensor_copy(out=ubf[:], in_=u_sb[:, dc, :]), "ve")
            ve_ubf = c["ve"]
            @block.tensor
            def _(t, dc=dc, ve_ubf=ve_ubf, sc_gate=c["sc"]):
                t.wait_ge(s_ve, ve_ubf)
                if dc == 0:
                    t.wait_ge(s_sc, sc_gate)
                inc(t.matmul(pm[6][0:96, :], w_x_sb[:, dc * 96:(dc + 1) * 96],
                             ubf[:], start=(dc == 0), stop=(dc == DC - 1)),
                    "mm")
        mm_xp = c["mm"]
        @block.vector
        def _(v, mm_xp=mm_xp):
            v.wait_ge(s_mm, mm_xp)
            inc(v.tensor_copy(out=sp1[0:96, :], in_=pm[6][0:96, :]), "ve")
        ve_xp = c["ve"]
        @block.sync
        def _(sync, k=k, ve_xp=ve_xp):
            sync.wait_ge(s_ve, ve_xp)
            inc(sync.dma_start(out=e1_in[k], in_=sp1[0:96, :]), "st")
        st_e1 = c["st"]
        @block.gpsimd
        def _(g, k=k, st_e1=st_e1):
            g.wait_ge(s_st, st_e1)
            inc(g.collective_compute("AllReduce", ALU.add, replica_groups=PAIRS,
                                     ins=[e1_in[k]], outs=[e1_out[k]]), "cc")
        cc1 = c["cc"]
        @block.sync
        def _(sync, k=k, cc1=cc1):
            sync.wait_ge(s_cc, cc1)
            inc(sync.dma_start(out=sp1[0:RNK, :], in_=e1_out[k][0:RNK, :]), "xd")
        ld_xdbl = c["xd"]
        if k == 0:
            tap("t_xdbl", sp1[0:96, :], gates=[("xd", c["xd"])])

        @block.vector
        def _(v, ld_xdbl=ld_xdbl):
            v.wait_ge(s_xd, ld_xdbl)
            inc(v.tensor_copy(out=dtin_bf[:], in_=sp1[0:RNK, :]), "ve")
        ve_dtin = c["ve"]
        for dc in range(DC):
            bank = pm[dc % 2]
            @block.tensor
            def _(t, dc=dc, bank=bank, ve_dtin=ve_dtin, ve_gate=c["ve"], sc_gate=c["sc"]):
                t.wait_ge(s_ve, max(ve_dtin, ve_gate))
                t.wait_ge(s_sc, sc_gate)
                inc(t.matmul(bank[:], w_dt_sb[:, dc * 128:(dc + 1) * 128],
                             dtin_bf[:], start=True, stop=True), "mm")
            mm_dt = c["mm"]
            @block.vector
            def _(v, dc=dc, bank=bank, mm_dt=mm_dt):
                v.wait_ge(s_mm, mm_dt)
                inc(v.tensor_scalar(out=dt_sb[:, dc, :], in0=bank[:],
                                    scalar1=bdt_sb[:, dc:dc + 1],
                                    scalar2=0.0, op0=ALU.add, op1=ALU.max), "ve")
                inc(v.tensor_scalar(out=dtu_sb[:, dc, :], in0=bank[:],
                                    scalar1=bdt_sb[:, dc:dc + 1],
                                    scalar2=-1.0, op0=ALU.add, op1=ALU.mult), "ve")
                inc(v.scalar_tensor_tensor(out=dtu_sb[:, dc, :],
                                           in0=bank[:],
                                           scalar=bdt_sb[:, dc:dc + 1],
                                           in1=dtu_sb[:, dc, :],
                                           op0=ALU.add, op1=ALU.max), "ve")
        ve_av = c["ve"]
        @block.scalar
        def _(s, ve_av=ve_av):
            s.wait_ge(s_ve, ve_av)
            inc(s.activation(out=big[:, 0:2048],
                             in_=dtu_sb[:, 0:4, :].rearrange("p a b -> p (a b)"),
                             func=AF.Exp, scale=cst[:, 3:4]), "sc")
            inc(s.activation(out=sp2[:],
                             in_=dtu_sb[:, 4, :], func=AF.Exp, scale=cst[:, 3:4]), "sc")
            inc(s.activation(out=sp3[:],
                             in_=dtu_sb[:, 5, :], func=AF.Exp, scale=cst[:, 3:4]), "sc")
            inc(s.activation(out=sp1[:],
                             in_=dtu_sb[:, 6, :], func=AF.Exp, scale=cst[:, 3:4]), "sc")
            inc(s.activation(out=h_tok[:].bitcast(F32),
                             in_=dtu_sb[:, 7, :], func=AF.Exp, scale=cst[:, 3:4]), "sc")
            inc(s.activation(out=dtu_sb[:, 0:4, :].rearrange("p a b -> p (a b)"),
                             in_=big[:, 0:2048], func=AF.Ln, bias=cst[:, 2:3]), "sc")
            inc(s.activation(out=dtu_sb[:, 4, :], in_=sp2[:], func=AF.Ln,
                             bias=cst[:, 2:3]), "sc")
            inc(s.activation(out=dtu_sb[:, 5, :], in_=sp3[:], func=AF.Ln,
                             bias=cst[:, 2:3]), "sc")
            inc(s.activation(out=dtu_sb[:, 6, :], in_=sp1[:], func=AF.Ln,
                             bias=cst[:, 2:3]), "sc")
            inc(s.activation(out=dtu_sb[:, 7, :], in_=h_tok[:].bitcast(F32)[:, 0:TT],
                             func=AF.Ln, bias=cst[:, 2:3]), "sc")
        sc_sp = c["sc"]
        for dc in range(DC):
            @block.vector
            def _(v, dc=dc, sc_sp=sc_sp):
                if dc == 0:
                    v.wait_ge(s_sc, sc_sp)
                inc(v.tensor_tensor(out=dt_sb[:, dc, :], in0=dt_sb[:, dc, :],
                                    in1=dtu_sb[:, dc, :], op=ALU.add), "ve")
                inc(v.tensor_tensor(out=dtu_sb[:, dc, :], in0=dt_sb[:, dc, :],
                                    in1=u_sb[:, dc, :], op=ALU.mult), "ve")
        ve_dt = c["ve"]
        if k == 0:
            tap("t_dt", dt_sb[:], gates=[("ve", c["ve"])])
            tap("t_u", u_sb[:], gates=[("ve", c["ve"])])

        for n in range(NST):
            @block.sync
            def _(sync, k=k, n=n, ve_gate=c["ve"]):
                if n >= 2:
                    sync.wait_ge(s_ve, ve_gate)
                inc(sync.dma_start(out=bbc[:, n % 2, :],
                                   in_=e1_out[k][64 + n:65 + n, :].to_broadcast([128, TT])), "bb")
                inc(sync.dma_start(out=cbc[:, n % 2, :],
                                   in_=e1_out[k][80 + n:81 + n, :].to_broadcast([128, TT])), "cb")
            ld_bc = (c["bb"], c["cb"])
            for dc in range(DC):
                idx = dc * NST + n
                @block.scalar
                def _(s, dc=dc, n=n, idx=idx, ve_dt=ve_dt, ve_gate=c["ve"]):
                    s.wait_ge(s_ve, max(ve_dt, ve_gate))
                    inc(s.activation(out=da_scr[:], in_=dt_sb[:, dc, :],
                                     func=AF.Exp, scale=a_sb[:, idx:idx + 1]), "sc")
                sc_da = c["sc"]
                @block.vector
                def _(v, dc=dc, n=n, idx=idx, sc_da=sc_da, ld_bc=ld_bc):
                    v.wait_ge(s_sc, sc_da)
                    v.wait_ge(s_bb, ld_bc[0])
                    v.wait_ge(s_cb, ld_bc[1])
                    inc(v.tensor_tensor(out=db_scr[:], in0=dtu_sb[:, dc, :],
                                        in1=bbc[:, n % 2, :], op=ALU.mult), "ve")
                    inc(v.tensor_tensor_scan(out=hn_scr[:], data0=da_scr[:],
                                             data1=db_scr[:],
                                             initial=h_carry[:, idx:idx + 1],
                                             op0=ALU.mult, op1=ALU.add), "ve")
                    inc(v.tensor_copy(out=h_carry[:, idx:idx + 1],
                                      in_=hn_scr[:, TT - 1:TT]), "ve")
                    if n == 0:
                        inc(v.tensor_tensor(out=yacc[:, dc, :], in0=hn_scr[:],
                                            in1=cbc[:, n % 2, :], op=ALU.mult), "ve")
                    else:
                        inc(v.tensor_tensor(out=yp_scr[:], in0=hn_scr[:],
                                            in1=cbc[:, n % 2, :], op=ALU.mult), "ve")
                        inc(v.tensor_tensor(out=yacc[:, dc, :], in0=yacc[:, dc, :],
                                            in1=yp_scr[:], op=ALU.add), "ve")

        if k == 0:
            tap("t_yacc", yacc[:], gates=[("ve", c["ve"])])
        for dc in range(DC):
            @block.vector
            def _(v, dc=dc):
                inc(v.scalar_tensor_tensor(out=yp_scr[:], in0=u_sb[:, dc, :],
                                           scalar=dp_sb[:, dc:dc + 1],
                                           in1=yacc[:, dc, :], op0=ALU.mult,
                                           op1=ALU.add), "ve")
                inc(v.tensor_tensor(out=y2h[:, dc, :], in0=yp_scr[:],
                                    in1=sz[:, dc, :], op=ALU.mult), "ve")
        ve_y2 = c["ve"]
        if k == 0:
            tap("t_y2h", y2h[:].rearrange("p a b -> p (a b)").bitcast(F32),
                gates=[("ve", c["ve"])])
        @block.sync
        def _(sync, k=k, ve_y2=ve_y2):
            sync.wait_ge(s_ve, ve_y2)
            inc(sync.dma_start(out=e2_in[k].rearrange("(dc p) t -> p dc t", p=128),
                               in_=y2h[:]), "st")
        st_e2 = c["st"]
        @block.gpsimd
        def _(g, k=k, st_e2=st_e2):
            g.wait_ge(s_st, st_e2)
            inc(g.collective_compute("AllGather", ALU.bypass, replica_groups=PAIRS,
                                     ins=[e2_in[k]], outs=[e2_out[k]]), "cc")
        cc2 = c["cc"]

        y2full = h_f[:].rearrange("p a b -> p (a b)").rearrange(
            "p (kc t) -> p kc t", kc=NKO)
        qkey = "ld" if k % 2 == 0 else "gl"
        mm_before_wo = c["mm"]
        def emit_y2_load(eng, k=k, cc2=cc2, mm_before=mm_before_wo):
            eng.wait_ge(s_cc, cc2)
            eng.wait_ge(s_mm, mm_before)
            par = eng.snap(eng.partition_id() % 2, min_val=0, max_val=1)
            inc(eng.dma_start(
                out=y2full,
                in_=e2_out[k].rearrange("(kc p) t -> p kc t", p=128)[:, :, bass.ts(par, HT)]),
                "y2")
        if k % 2 == 0:
            block.sync(emit_y2_load)
        else:
            block.gpsimd(emit_y2_load)
        ld_y2 = c["y2"]
        for kc in range(NKO):
            @block.sync
            def _(sync, kc=kc, mm_gate=c["mm"]):
                if kc >= 2:
                    sync.wait_ge(s_mm, mm_gate)
                inc(sync.dma_start(out=wo_st[:, kc % 2, :], in_=wout_bf[kc]), "wo")
            ld_wo = c["wo"]
            @block.tensor
            def _(t, kc=kc, ld_wo=ld_wo, ld_y2=ld_y2, sc_gate=c["sc"], ve_gate=c["ve"], qkey=qkey):
                t.wait_ge(s_wo, ld_wo)
                if kc == 0:
                    t.wait_ge(s_y2, ld_y2)
                    t.wait_ge(s_sc, sc_gate)
                    t.wait_ge(s_ve, ve_gate)
                for mc in range(8):
                    inc(t.matmul(pm[mc][:, 0:HT],
                                 wo_st[:, kc % 2, mc * 128:(mc + 1) * 128],
                                 y2full[:, kc, :], start=(kc == 0),
                                 stop=(kc == NKO - 1)), "mm")
        mm_wo = c["mm"]

        for ht in range(2):
            @block.sync
            def _(sync, k=k, ht=ht, gate_ve=c["ve"]):
                sync.wait_ge(s_ve, gate_ve)
                inc(sync.dma_start(out=xtile[:],
                                   in_=x_own[k][ht * 128:(ht + 1) * 128, :]), "x")
            ld_xo = c["x"]
            @block.vector
            def _(v, ht=ht, ld_xo=ld_xo):
                v.wait_ge(s_x, ld_xo)
                inc(v.bn_stats(out=st6[:, 0:6], in_=xtile[:, 0:512]), "ve")
                inc(v.bn_stats(out=st6[:, 6:12], in_=xtile[:, 512:1024]), "ve")
                v.wait_ge(s_ve, c["ve"])
                inc(v.bn_aggr(out=st2[:], in_=st6[:]), "ve")
            ve_so = c["ve"]
            @block.scalar
            def _(s, ve_so=ve_so):
                s.wait_ge(s_ve, ve_so)
                inc(s.activation(out=rs_sb[:, 0:1], in_=st2[:, 1:2], func=AF.Sqrt,
                                 bias=cst[:, 0:1]), "sc")
            sc_ro = c["sc"]
            @block.vector
            def _(v, ht=ht, sc_ro=sc_ro):
                v.wait_ge(s_sc, sc_ro)
                inc(v.reciprocal(out=rs_sb[:, 0:1], in_=rs_sb[:, 0:1]), "ve")
                _ = v.wait_ge(s_ve, c["ve"])
                inc(v.tensor_scalar(out=epim[:, ht, :], in0=xtile[:],
                                    scalar1=st2[:, 0:1], scalar2=rs_sb[:, 0:1],
                                    op0=ALU.subtract, op1=ALU.mult), "ve")
        for mc in range(8):
            @block.scalar
            def _(s, mc=mc, mm_wo=mm_wo, ve_gate=c["ve"]):
                s.wait_ge(s_mm, mm_wo)
                inc(s.activation(out=sp3[:, (mc % 2) * HT:(mc % 2 + 1) * HT],
                                 in_=pm[mc][:, 0:HT], func=AF.Copy), "sc")
            sc_om = c["sc"]
            @block.tensor
            def _(t, mc=mc, sc_om=sc_om, ve_gate=c["ve"]):
                t.wait_ge(s_sc, sc_om)
                if mc >= 2:
                    t.wait_ge(s_ve, ve_gate)
                for ht in range(2):
                    inc(t.matmul(pm[mc][:, HT + ht * 128: HT + (ht + 1) * 128],
                                 sp3[:, (mc % 2) * HT + ht * 128:(mc % 2) * HT + (ht + 1) * 128],
                                 id_f[:], start=True, stop=True, is_transpose=True), "mm")
            mm_omt = c["mm"]
            @block.vector
            def _(v, mc=mc, mm_omt=mm_omt):
                v.wait_ge(s_mm, mm_omt)
                for ht in range(2):
                    sl = epim[:, ht, mc * 128:(mc + 1) * 128]
                    inc(v.tensor_tensor(out=sl, in0=sl,
                                        in1=pm[mc][:, HT + ht * 128: HT + (ht + 1) * 128],
                                        op=ALU.add), "ve")
        if k == 0:
            tap("t_epim", epim[:], gates=[("ve", c["ve"])])
        for ht in range(2):
            @block.sync
            def _(sync, k=k, ht=ht, ve_gate=c["ve"]):
                sync.wait_ge(s_ve, ve_gate)
                inc(sync.dma_start(out=xtile[:],
                                   in_=x_own[k][ht * 128:(ht + 1) * 128, :]), "x")
            ld_xr = c["x"]
            @block.vector
            def _(v, ht=ht):
                inc(v.bn_stats(out=st6[:, 0:6], in_=epim[:, ht, 0:512]), "ve")
                inc(v.bn_stats(out=st6[:, 6:12], in_=epim[:, ht, 512:1024]), "ve")
                v.wait_ge(s_ve, c["ve"])
                inc(v.bn_aggr(out=st2[:], in_=st6[:]), "ve")
            ve_st3 = c["ve"]
            @block.scalar
            def _(s, ve_st3=ve_st3):
                s.wait_ge(s_ve, ve_st3)
                inc(s.activation(out=rs_sb[:, 2:3], in_=st2[:, 1:2], func=AF.Sqrt,
                                 bias=cst[:, 0:1]), "sc")
            sc_r3 = c["sc"]
            @block.vector
            def _(v, ht=ht, sc_r3=sc_r3, ld_xr=ld_xr):
                v.wait_ge(s_sc, sc_r3)
                v.wait_ge(s_x, ld_xr)
                inc(v.reciprocal(out=rs_sb[:, 2:3], in_=rs_sb[:, 2:3]), "ve")
                _ = v.wait_ge(s_ve, c["ve"])
                inc(v.tensor_scalar(out=epi[:], in0=epim[:, ht, :],
                                    scalar1=st2[:, 0:1],
                                    scalar2=rs_sb[:, 2:3], op0=ALU.subtract,
                                    op1=ALU.mult), "ve")
                inc(v.tensor_tensor(out=epi[:], in0=epi[:], in1=xtile[:],
                                    op=ALU.add), "ve")
            ve_xs1 = c["ve"]
            if k == 0:
                tap_xs1_ht = ht
                if DEBUG_TAPS:
                    @block.scalar
                    def _(s, ht=ht, gv=c["ve"]):
                        s.wait_ge(s_ve, gv)
                        s.dma_start(out=taps["t_xs1"][:, ht, :], in_=epi[:]).then_inc(s_tp, 16)
                        tapcnt[0] += 16
                        s.wait_ge(s_tp, tapcnt[0])
            @block.sync
            def _(sync, k=k, ht=ht, ve_xs1=ve_xs1):
                sync.wait_ge(s_ve, ve_xs1)
                inc(sync.dma_start(out=xs1_sp[k][ht * 128:(ht + 1) * 128, :],
                                   in_=epi[:]), "st")

    cM = dict(c)
    ctxM.close()

    ctxF = ExitStack()
    sbF = lambda name, shape, dtype=F32: ctxF.enter_context(
        nc.sbuf_tensor(name, list(shape), dtype))
    wfc_sb = sbF("wfc_sb", (128, KC, HF), BF16)
    wpj_sb = sbF("wpj_sb", (128, HKC, D), BF16)
    fstage = sbF("fstage", (128, 2048))
    qs1 = sbF("qs1", (128, 2048))
    qs2 = sbF("qs2", (128, 2048))
    wq_t = sbF("wq_t", (128, D), BF16)
    g_bf = sbF("g_bf", (128, HKC, HT), BF16)
    g2_scr = sbF("g2_scr", (128, HT), BF16)
    ebuf = sbF("ebuf", (128, HT))
    rowsc = sbF("rowsc", (1, 2 * HT))
    sa_bc = sbF("sa_bc", (128, HT))
    nb_bc = sbF("nb_bc", (128, HT))
    nsw_sb = sbF("nsw_sb", (128, 8))
    bo_sb = sbF("bo_sb", (128, 8))
    fo = sbF("fo", (128, 2, HT))
    fo_tok = sbF("fo_tok", (128, 2, D))
    xs1c = sbF("xs1c", (128, 2, D))
    xln_b = sbF("xln_b", (128, D), BF16)
    xlnf = sbF("xlnf", (128, KC, HT), BF16)

    ALLK = tuple(S.keys())
    @block.sync
    def _(sync):
        for key in ALLK:
            sync.wait_ge(S[key], cM[key])
    @block.gpsimd
    def _(g):
        for key in ALLK:
            g.wait_ge(S[key], cM[key])
    @block.tensor
    def _(t):
        for key in ALLK:
            t.wait_ge(S[key], cM[key])
    @block.scalar
    def _(s):
        for key in ALLK:
            s.wait_ge(S[key], cM[key])
    @block.vector
    def _(v):
        for key in ALLK:
            v.wait_ge(S[key], cM[key])

    for kc in range(KC):
        for hh in range(2):
            @block.gpsimd
            def _(g, kc=kc, hh=hh, ve_gate=c["ve"]):
                g.wait_ge(s_ve, ve_gate)
                inc(g.dma_start(out=fstage[:],
                                in_=w_fcT[kc * 128:(kc + 1) * 128,
                                          hh * 2048:(hh + 1) * 2048]), "gl")
            gl_q = c["gl"]
            @block.vector
            def _(v, kc=kc, hh=hh, gl_q=gl_q):
                v.wait_ge(s_gl, gl_q)
                inc(v.tensor_scalar(out=qs1[:], in0=fstage[:], scalar1=thr_sb[:, 0:1],
                                    scalar2=None, op0=ALU.is_ge), "ve")
                inc(v.tensor_scalar(out=qs2[:], in0=fstage[:], scalar1=thr_sb[:, 1:2],
                                    scalar2=None, op0=ALU.is_le), "ve")
                inc(v.tensor_tensor(out=wfc_sb[:, kc, hh * 2048:(hh + 1) * 2048],
                                    in0=qs1[:], in1=qs2[:], op=ALU.subtract), "ve")

    for kc2 in range(HKC):
        @block.gpsimd
        def _(g, kc2=kc2, ve_gate=c["ve"]):
            g.wait_ge(s_ve, ve_gate)
            inc(g.dma_start(out=fstage[:, 0:D],
                            in_=w_projT[kc2 * 128:(kc2 + 1) * 128, :]), "gl")
        gl_q = c["gl"]
        @block.vector
        def _(v, kc2=kc2, gl_q=gl_q, mm_gate=c["mm"]):
            v.wait_ge(s_gl, gl_q)
            if kc2 > 0:
                v.wait_ge(s_mm, mm_gate)
            inc(v.tensor_scalar(out=qs1[:, 0:D], in0=fstage[:, 0:D],
                                scalar1=thr_sb[:, 2:3], scalar2=None, op0=ALU.is_ge), "ve")
            inc(v.tensor_scalar(out=qs2[:, 0:D], in0=fstage[:, 0:D],
                                scalar1=thr_sb[:, 3:4], scalar2=None, op0=ALU.is_le), "ve")
            inc(v.tensor_tensor(out=wq_t[:], in0=qs1[:, 0:D], in1=qs2[:, 0:D],
                                op=ALU.subtract), "ve")
        ve_wq = c["ve"]
        @block.tensor
        def _(t, kc2=kc2, ve_wq=ve_wq):
            t.wait_ge(s_ve, ve_wq)
            for mc in range(8):
                inc(t.matmul(pm[6][:, mc:mc + 1], wq_t[:, mc * 128:(mc + 1) * 128],
                             lnbf_b[:, kc2:kc2 + 1], start=True, stop=True), "mm")
                inc(t.matmul(pm[7][:, mc:mc + 1], wq_t[:, mc * 128:(mc + 1) * 128],
                             lnwf_b[:, kc2:kc2 + 1], start=True, stop=True), "mm")
        mm_bo = c["mm"]
        @block.vector
        def _(v, kc2=kc2, mm_bo=mm_bo):
            v.wait_ge(s_mm, mm_bo)
            if kc2 == 0:
                inc(v.tensor_copy(out=bo_sb[:], in_=pm[6][:, 0:8]), "ve")
                inc(v.tensor_copy(out=nsw_sb[:], in_=pm[7][:, 0:8]), "ve")
            else:
                inc(v.tensor_tensor(out=bo_sb[:], in0=bo_sb[:],
                                    in1=pm[6][:, 0:8], op=ALU.add), "ve")
                inc(v.tensor_tensor(out=nsw_sb[:], in0=nsw_sb[:],
                                    in1=pm[7][:, 0:8], op=ALU.add), "ve")
            v.wait_ge(s_ve, c["ve"])
            inc(v.tensor_scalar(out=wpj_sb[:, kc2, :], in0=wq_t[:],
                                scalar1=lnwf_sb[:, kc2:kc2 + 1], scalar2=None,
                                op0=ALU.mult), "ve")
    @block.vector
    def _(v):
        inc(v.tensor_scalar(out=nsw_sb[:], in0=nsw_sb[:], scalar1=-1.0,
                            scalar2=None, op0=ALU.mult), "ve")
        v.wait_ge(s_ve, c["ve"])
    ve_bo = c["ve"]
    @block.scalar
    def _(s, ve_bo=ve_bo):
        s.wait_ge(s_ve, ve_bo)
    sc_bo = c["sc"]

    if DEBUG_TAPS:
        tap("t_thr", thr_sb[:], gates=[("pc", c["pc"])])
        tap("t_wfc", wfc_sb[:, 0, :].bitcast(F32), gates=[("ve", c["ve"])])
        tap("t_wpj", wpj_sb[:].rearrange("p a b -> p (a b)").bitcast(F32),
            gates=[("ve", c["ve"])])
        if DEBUG_TAPS:
            @block.scalar
            def _(s, gv=c["sc"]):
                s.wait_ge(s_sc, gv)
                s.dma_start(out=taps["t_nswbo"][:, 0:8], in_=nsw_sb[:]).then_inc(s_tp, 16)
                tapcnt[0] += 16
                s.wait_ge(s_tp, tapcnt[0])
        @block.scalar
        def _(s, gv=c["sc"]):
            s.wait_ge(s_sc, gv)
            s.dma_start(out=taps["t_nswbo"][:, 8:16], in_=bo_sb[:]).then_inc(s_tp, 16)
            tapcnt[0] += 16
            s.wait_ge(s_tp, tapcnt[0])

    for h in range(NCH):
        @block.sync
        def _(sync, h=h, ve_gate=c["ve"]):
            if h == 0:
                sync.wait_ge(s_sc, sc_bo)
            sync.wait_ge(s_ve, ve_gate)
            inc(sync.dma_start(out=xs1c[:],
                               in_=xs1_sp[h].rearrange("(a p) d -> p a d", p=128)), "x")
        ld_xs1 = c["x"]
        for ht in range(2):
            @block.vector
            def _(v, ht=ht, ld_xs1=ld_xs1):
                v.wait_ge(s_x, ld_xs1)
                inc(v.bn_stats(out=st6[:, 0:6], in_=xs1c[:, ht, 0:512]), "ve")
                inc(v.bn_stats(out=st6[:, 6:12], in_=xs1c[:, ht, 512:1024]), "ve")
                v.wait_ge(s_ve, c["ve"])
                inc(v.bn_aggr(out=st2[:], in_=st6[:]), "ve")
            ve_sf = c["ve"]
            @block.scalar
            def _(s, ve_sf=ve_sf):
                s.wait_ge(s_ve, ve_sf)
                inc(s.activation(out=rs_sb[:, 0:1], in_=st2[:, 1:2], func=AF.Sqrt,
                                 bias=cst[:, 0:1]), "sc")
            sc_rf = c["sc"]
            @block.vector
            def _(v, ht=ht, sc_rf=sc_rf):
                v.wait_ge(s_sc, sc_rf)
                inc(v.reciprocal(out=rs_sb[:, 0:1], in_=rs_sb[:, 0:1]), "ve")
                _ = v.wait_ge(s_ve, c["ve"])
                inc(v.tensor_scalar(out=xln_b[:], in0=xs1c[:, ht, :],
                                    scalar1=st2[:, 0:1], scalar2=rs_sb[:, 0:1],
                                    op0=ALU.subtract, op1=ALU.mult), "ve")
            ve_xln = c["ve"]
            @block.tensor
            def _(t, ht=ht, ve_xln=ve_xln, sc_gate=c["sc"]):
                t.wait_ge(s_ve, ve_xln)
                t.wait_ge(s_sc, sc_gate)
                pmb = pm[4][:].bitcast(BF16)
                for kc in range(KC):
                    inc(t.matmul(pmb[:, kc * 128:(kc + 1) * 128],
                                 xln_b[:, kc * 128:(kc + 1) * 128], id_b[:],
                                 start=True, stop=True, is_transpose=True), "mm")
            mm_trf = c["mm"]
            @block.scalar
            def _(s, ht=ht, mm_trf=mm_trf):
                s.wait_ge(s_mm, mm_trf)
                inc(s.activation(out=xlnf[:, :, ht * 128:(ht + 1) * 128],
                                 in_=pm[4][:].bitcast(BF16).rearrange(
                                     "p (kc t) -> p kc t", kc=KC),
                                 func=AF.Copy), "sc")
        sc_xlnf = c["sc"]
        if h == 0:
            tap("t_xlnf", xlnf[:].rearrange("p a b -> p (a b)").bitcast(F32),
                gates=[("sc", c["sc"])])

        for mc in range(HKC):
            bank = pm[mc % 4]
            @block.tensor
            def _(t, mc=mc, bank=bank, sc_xlnf=sc_xlnf, sc_gate=c["sc"], ve_gate=c["ve"]):
                t.wait_ge(s_sc, sc_gate)
                t.wait_ge(s_ve, ve_gate)
                for kc in range(KC):
                    inc(t.matmul(bank[:, 0:HT], wfc_sb[:, kc, mc * 128:(mc + 1) * 128],
                                 xlnf[:, kc, :], start=(kc == 0), stop=(kc == KC - 1)),
                        "mm")
            mm_fc = c["mm"]
            @block.scalar
            def _(s, mc=mc, bank=bank, mm_fc=mm_fc):
                s.wait_ge(s_mm, mm_fc)
                inc(s.activation(out=ebuf[:], in_=bank[:, 0:HT], func=AF.Erf,
                                 scale=cst[:, 4:5]), "sc")
            sc_erf = c["sc"]
            @block.vector
            def _(v, mc=mc, bank=bank, sc_erf=sc_erf):
                v.wait_ge(s_sc, sc_erf)
                inc(v.scalar_tensor_tensor(out=g_bf[:, mc, :], in0=ebuf[:],
                                           scalar=1.0, in1=bank[:, 0:HT],
                                           op0=ALU.add, op1=ALU.mult), "ve")
            @block.vector
            def _(v, mc=mc):
                inc(v.tensor_tensor(out=g2_scr[:], in0=g_bf[:, mc, :],
                                    in1=g_bf[:, mc, :], op=ALU.mult), "ve")
            ve_g = c["ve"]
            @block.tensor
            def _(t, mc=mc, ve_g=ve_g):
                t.wait_ge(s_ve, ve_g)
                inc(t.matmul(pm[6][0:1, 0:HT], ones_b[:], g_bf[:, mc, :],
                             start=(mc == 0), stop=(mc == HKC - 1)), "mm")
                inc(t.matmul(pm[7][0:1, 0:HT], ones_b[:], g2_scr[:],
                             start=(mc == 0), stop=(mc == HKC - 1)), "mm")
        mm_st = c["mm"]
        if h == 0:
            tap("t_gbf", g_bf[:].rearrange("p a b -> p (a b)").bitcast(F32),
                gates=[("ve", c["ve"]), ("sc", c["sc"])])
        @block.vector
        def _(v, mm_st=mm_st):
            v.wait_ge(s_mm, mm_st)
            inc(v.tensor_scalar(out=rowsc[0:1, 0:HT], in0=pm[6][0:1, 0:HT],
                                scalar1=0.5 / HF, scalar2=None, op0=ALU.mult), "ve")
            inc(v.tensor_scalar(out=rowsc[0:1, HT:2 * HT], in0=pm[7][0:1, 0:HT],
                                scalar1=0.25 / HF, scalar2=None, op0=ALU.mult), "ve")
            v.wait_ge(s_ve, c["ve"])
            inc(v.tensor_tensor(out=ebuf[0:1, 0:HT], in0=rowsc[0:1, 0:HT],
                                in1=rowsc[0:1, 0:HT], op=ALU.mult), "ve")
            v.wait_ge(s_ve, c["ve"])
            inc(v.tensor_tensor(out=ebuf[0:1, 0:HT], in0=rowsc[0:1, HT:2 * HT],
                                in1=ebuf[0:1, 0:HT], op=ALU.subtract), "ve")
        ve_var = c["ve"]
        @block.scalar
        def _(s, ve_var=ve_var):
            s.wait_ge(s_ve, ve_var)
            inc(s.activation(out=ebuf[0:1, 0:HT], in_=ebuf[0:1, 0:HT],
                             func=AF.Sqrt, bias=cst[0:1, 0:1]), "sc")
        sc_rstdf = c["sc"]
        @block.vector
        def _(v, sc_rstdf=sc_rstdf):
            v.wait_ge(s_sc, sc_rstdf)
            inc(v.reciprocal(out=ebuf[0:1, 0:HT], in_=ebuf[0:1, 0:HT]), "ve")
            v.wait_ge(s_ve, c["ve"])
            inc(v.tensor_tensor(out=rowsc[0:1, 0:HT], in0=rowsc[0:1, 0:HT],
                                in1=ebuf[0:1, 0:HT], op=ALU.mult), "ve")
            inc(v.tensor_scalar(out=rowsc[0:1, HT:2 * HT], in0=ebuf[0:1, 0:HT],
                                scalar1=0.5, scalar2=None, op0=ALU.mult), "ve")
        ve_rows = c["ve"]
        @block.sync
        def _(sync, h=h, ve_rows=ve_rows):
            sync.wait_ge(s_ve, ve_rows)
            inc(sync.dma_start(out=row_dram[h], in_=rowsc[0:1, :].rearrange(
                "o (a t) -> o a t", a=2)), "st")
        st_rows = c["st"]
        @block.sync
        def _(sync, h=h, st_rows=st_rows, ve_gate=c["ve"]):
            sync.wait_ge(s_st, st_rows)
            inc(sync.dma_start(out=nb_bc[:], in_=row_dram[h][0:1, :].to_broadcast([128, HT])), "rw")
            inc(sync.dma_start(out=sa_bc[:], in_=row_dram[h][1:2, :].to_broadcast([128, HT])), "rw")
        ld_rows = c["rw"]

        if h == 0:
            tap("t_rows", rowsc[0:1, :], gates=[("ve", c["ve"])])
        for mc2 in range(8):
            bank = pm[mc2 % 4]
            @block.tensor
            def _(t, mc2=mc2, bank=bank, ve_gate=c["ve"], sc_gate=c["sc"]):
                t.wait_ge(s_ve, ve_gate)
                t.wait_ge(s_sc, sc_gate)
                for kc2 in range(HKC):
                    inc(t.matmul(bank[:, 0:HT], wpj_sb[:, kc2, mc2 * 128:(mc2 + 1) * 128],
                                 g_bf[:, kc2, :], start=(kc2 == 0),
                                 stop=(kc2 == HKC - 1)), "mm")
            mm_pj = c["mm"]
            @block.vector
            def _(v, mc2=mc2, bank=bank, mm_pj=mm_pj, ld_rows=ld_rows):
                v.wait_ge(s_mm, mm_pj)
                v.wait_ge(s_rw, ld_rows)
                inc(v.tensor_tensor(out=fo[:, mc2 % 2, :], in0=bank[:, 0:HT],
                                    in1=sa_bc[:], op=ALU.mult), "ve")
                inc(v.scalar_tensor_tensor(out=fo[:, mc2 % 2, :], in0=nb_bc[:],
                                           scalar=nsw_sb[:, mc2:mc2 + 1],
                                           in1=fo[:, mc2 % 2, :], op0=ALU.mult,
                                           op1=ALU.add), "ve")
                inc(v.tensor_scalar(out=fo[:, mc2 % 2, :], in0=fo[:, mc2 % 2, :],
                                    scalar1=bo_sb[:, mc2:mc2 + 1], scalar2=None,
                                    op0=ALU.add), "ve")
            ve_fo = c["ve"]
            @block.tensor
            def _(t, mc2=mc2, bank=bank, ve_fo=ve_fo):
                t.wait_ge(s_ve, ve_fo)
                for ht in range(2):
                    inc(t.matmul(bank[:, HT + ht * 128:HT + (ht + 1) * 128],
                                 fo[:, mc2 % 2, ht * 128:(ht + 1) * 128], id_f[:],
                                 start=True, stop=True, is_transpose=True), "mm")
            mm_fot = c["mm"]
            @block.scalar
            def _(s, mc2=mc2, bank=bank, mm_fot=mm_fot):
                s.wait_ge(s_mm, mm_fot)
                for ht in range(2):
                    inc(s.activation(out=fo_tok[:, ht, mc2 * 128:(mc2 + 1) * 128],
                                     in_=bank[:, HT + ht * 128:HT + (ht + 1) * 128],
                                     func=AF.Copy), "sc")
        sc_fot = c["sc"]
        if h == 0:
            tap("t_fotok", fo_tok[:], gates=[("sc", c["sc"])])
        for ht in range(2):
            @block.vector
            def _(v, ht=ht, sc_fot=sc_fot):
                v.wait_ge(s_sc, sc_fot)
                inc(v.bn_stats(out=st6[:, 0:6], in_=fo_tok[:, ht, 0:512]), "ve")
                inc(v.bn_stats(out=st6[:, 6:12], in_=fo_tok[:, ht, 512:1024]), "ve")
                v.wait_ge(s_ve, c["ve"])
                inc(v.bn_aggr(out=st2[:], in_=st6[:]), "ve")
            ve_s4 = c["ve"]
            @block.scalar
            def _(s, ve_s4=ve_s4):
                s.wait_ge(s_ve, ve_s4)
                inc(s.activation(out=rs_sb[:, 3:4], in_=st2[:, 1:2], func=AF.Sqrt,
                                 bias=cst[:, 0:1]), "sc")
            sc_r4 = c["sc"]
            @block.vector
            def _(v, ht=ht, sc_r4=sc_r4):
                v.wait_ge(s_sc, sc_r4)
                inc(v.reciprocal(out=rs_sb[:, 3:4], in_=rs_sb[:, 3:4]), "ve")
                _ = v.wait_ge(s_ve, c["ve"])
                inc(v.tensor_scalar(out=fstage[:, 0:D], in0=fo_tok[:, ht, :],
                                    scalar1=st2[:, 0:1], scalar2=rs_sb[:, 3:4],
                                    op0=ALU.subtract, op1=ALU.mult), "ve")
                inc(v.tensor_tensor(out=fstage[:, D:2 * D], in0=fstage[:, 0:D],
                                    in1=xs1c[:, ht, :], op=ALU.add), "ve")
            ve_fin = c["ve"]
            @block.sync
            def _(sync, h=h, ht=ht, ve_fin=ve_fin):
                sync.wait_ge(s_ve, ve_fin)
                inc(sync.dma_start(out=y_out[h][ht * 128:(ht + 1) * 128, :],
                                   in_=fstage[:, D:2 * D]), "st")

    st_total = c["st"]
    @block.sync
    def _(sync):
        sync.wait_ge(s_st, st_total)

    ctxF.close()
    ctx.close()
    return nc


_NC_CACHE = None


def kernel(**inputs):
    global _NC_CACHE
    f32 = np.float32
    x = np.ascontiguousarray(inputs["L0_attn_x"], dtype=f32)
    W_in = np.asarray(inputs["L0_attn_W_in"], f32)
    ln_g = np.asarray(inputs["L0_attn_ln_g"], f32)
    ln_b = np.asarray(inputs["L0_attn_ln_b"], f32)
    conv_w = np.asarray(inputs["L0_attn_conv_w"], f32)
    conv_b = np.asarray(inputs["L0_attn_conv_b"], f32)
    W_x = np.asarray(inputs["L0_attn_W_x"], f32)
    W_dt = np.asarray(inputs["L0_attn_W_dt"], f32)
    b_dt = np.asarray(inputs["L0_attn_b_dt"], f32)
    A_log = np.asarray(inputs["L0_attn_A_log"], f32)
    Dp = np.asarray(inputs["L0_attn_Dp"], f32)
    W_out = np.asarray(inputs["L0_attn_W_out"], f32)
    W_fc = np.asarray(inputs["L0_ffn_W_fc"], f32)
    lnw = np.asarray(inputs["L0_ffn_ln_w"], f32)
    lnb = np.asarray(inputs["L0_ffn_ln_b"], f32)
    W_proj = np.asarray(inputs["L0_ffn_W_proj"], f32)

    CC = np.ascontiguousarray
    def dlay(vec, nk):
        return CC(vec.reshape(nk, 128).T)

    in_maps = []
    for core in range(8):
        p, q = core // 2, core % 2
        ch = slice(q * EH, (q + 1) * EH)
        w_inT_slice = CC(np.concatenate([W_in[ch], W_in[DIN + q * EH: DIN + (q + 1) * EH]],
                                        axis=0).T)
        m = {
            "x": CC(x[p]),
            "w_inT": w_inT_slice,
            "ln_g": dlay(ln_g, KC),
            "ln_b": dlay(ln_b, KC),
            "conv_w": CC(conv_w[ch, 0, :].reshape(DC, 128, 4).transpose(1, 0, 2).reshape(128, DC * 4)),
            "conv_b": dlay(conv_b[ch], DC),
            "w_xT": CC(W_x[:, ch].T),
            "w_dtT": CC(W_dt[ch].T),
            "b_dt": dlay(b_dt[ch], DC),
            "a_log": CC(A_log[ch].reshape(DC, 128, NST).transpose(1, 0, 2).reshape(128, DC * NST)),
            "dp": dlay(Dp[ch], DC),
            "w_outT": CC(W_out.T),
            "w_fcT": CC(W_fc.T),
            "lnw_f": dlay(lnw, HKC),
            "lnb_f": dlay(lnb, HKC),
            "w_projT": CC(W_proj.T),
            "ident": np.eye(128, dtype=f32),
            "x_own": CC(x[p].reshape(NCH, 2, HT, D)[:, q]),
        }
        in_maps.append(m)

    if _NC_CACHE is None:
        _NC_CACHE = build_nc()
    nc = _NC_CACHE
    res = run_bass_kernel_spmd(nc, in_maps, core_ids=list(range(8)))

    out = np.empty((4, TSEQ, D), dtype=f32)
    for core in range(8):
        p, q = core // 2, core % 2
        yc = res.results[core]["y"]
        for k in range(NCH):
            t0 = k * TT + q * HT
            out[p, t0:t0 + HT, :] = yc[k]
    if DEBUG_TAPS:
        kernel.LAST_TAPS = [dict(r) for r in res.results]
    return out


# version 06
# speedup vs baseline: 1.0945x; 1.0945x over previous
import numpy as np
from contextlib import ExitStack

import concourse.bass as bass
import concourse.mybir as mybir
from concourse.bass_utils import run_bass_kernel_spmd

F32 = mybir.dt.float32
BF16 = mybir.dt.bfloat16
ALU = mybir.AluOpType
AF = mybir.ActivationFunctionType
AX = mybir.AxisListType

D = 1024
DIN = 2048
EH = 1024
NST = 16
RNK = 64
TSEQ = 4096
TT = 512
HT = 256
NCH = TSEQ // TT
KC = D // 128
DC = EH // 128
MCXZ = 2 * EH // 128
NKO = DIN // 128
HF = 4096
HKC = HF // 128
DEBUG_TAPS = False
EPS1 = 1e-5
EPS2 = 1e-6
EPSF = 1e-5
AMEPS = 1e-6
RSQ2 = 0.7071067811865476
PAIRS = [[0, 1], [2, 3], [4, 5], [6, 7]]


def build_nc():
    nc = bass.Bass("TRN2", target_bir_lowering=False)

    def ein(name, shape, dtype=F32):
        return nc.dram_tensor(name, list(shape), dtype, kind="ExternalInput").ap()

    x_in = ein("x", (TSEQ, D))
    w_inT = ein("w_inT", (D, 2 * EH))
    ln_g = ein("ln_g", (128, KC))
    ln_b = ein("ln_b", (128, KC))
    conv_w = ein("conv_w", (128, DC * 4))
    conv_b = ein("conv_b", (128, DC))
    w_xT = ein("w_xT", (EH, 96))
    w_dtT = ein("w_dtT", (RNK, EH))
    b_dt = ein("b_dt", (128, DC))
    a_log = ein("a_log", (128, DC * NST))
    dp_in = ein("dp", (128, DC))
    w_outT = ein("w_outT", (DIN, D))
    w_fcT = ein("w_fcT", (D, HF))
    lnw_f = ein("lnw_f", (128, HKC))
    lnb_f = ein("lnb_f", (128, HKC))
    w_projT = ein("w_projT", (HF, D))
    ident = ein("ident", (128, 128))
    x_own = ein("x_own", (NCH, HT, D))

    y_out = nc.dram_tensor("y", [NCH, HT, D], F32, kind="ExternalOutput").ap()
    taps = {}
    if DEBUG_TAPS:
        for nm, shape, dt_ in (("t_hf", (128, KC * TT // 2), F32),
                               ("t_xi", (128, DC, TT + 4), F32),
                               ("t_sz", (128, DC, TT), F32),
                               ("t_u", (128, DC, TT), F32),
                               ("t_dt", (128, DC, TT), F32),
                               ("t_xdbl", (96, TT), F32),
                               ("t_yacc", (128, DC, TT), F32),
                               ("t_y2h", (128, DC * TT // 2), F32),
                               ("t_om", (128, 2, HT), F32),
                               ("t_epim", (128, 2, D), F32),
                               ("t_xs1", (128, 2, D), F32),
                               ("t_thr", (128, 4), F32),
                               ("t_wfc", (128, HF // 2), F32),
                               ("t_wpj", (128, HKC * D // 2), F32),
                               ("t_nswbo", (128, 16), F32),
                               ("t_xlnf", (128, KC * HT // 2), F32),
                               ("t_gbf", (128, HKC * HT // 2), F32),
                               ("t_rows", (1, 2 * HT), F32),
                               ("t_fotok", (128, 2, D), F32)):
            taps[nm] = nc.dram_tensor(nm, list(shape), dt_, kind="ExternalOutput").ap()

    e1_in = nc.dram_tensor("e1_in", [NCH, 96, TT], F32, kind="Internal").ap()
    e1_out = nc.dram_tensor("e1_out", [NCH, 96, TT], F32, kind="Internal").ap()
    e2_in = nc.dram_tensor("e2_in", [NCH, EH, TT], BF16, kind="Internal").ap()
    e2_out = nc.dram_tensor("e2_out", [NCH, DIN, TT], BF16, kind="Internal").ap()
    wout_bf = nc.dram_tensor("wout_bf", [NKO, 128, D], BF16, kind="Internal").ap()
    xs1_sp = nc.dram_tensor("xs1_sp", [NCH, HT, D], F32, kind="Internal").ap()
    row_dram = nc.dram_tensor("row_dram", [NCH, 2, HT], F32, kind="Internal").ap()
    thr_dram = nc.dram_tensor("thr_dram", [1, 4], F32, kind="Internal").ap()

    ctx = ExitStack()
    ctxM = ExitStack()
    sb = lambda st, name, shape, dtype=F32: st.enter_context(
        nc.sbuf_tensor(name, list(shape), dtype))
    psum = lambda name, shape: ctx.enter_context(
        nc.psum_tensor(name, list(shape), F32))
    sem = lambda name: ctx.enter_context(nc.semaphore(name))

    a_sb = sb(ctx, "a_sb", (128, DC * NST))
    g_sb = sb(ctx, "g_sb", (128, KC))
    lnb_sb = sb(ctx, "lnb_sb", (128, KC))
    convw_sb = sb(ctx, "convw_sb", (128, DC * 4))
    convb_sb = sb(ctx, "convb_sb", (128, DC))
    bdt_sb = sb(ctx, "bdt_sb", (128, DC))
    dp_sb = sb(ctx, "dp_sb", (128, DC))
    winb_sb = sb(ctx, "winb_sb", (128, MCXZ))
    id_f = sb(ctx, "id_f", (128, 128))
    id_b = sb(ctx, "id_b", (128, 128), BF16)
    ones_f = sb(ctx, "ones_f", (128, 1))
    ones_b = sb(ctx, "ones_b", (128, 1), BF16)
    thr_sb = sb(ctx, "thr_sb", (128, 4))
    lnwf_sb = sb(ctx, "lnwf_sb", (128, HKC))
    lnbf_sb = sb(ctx, "lnbf_sb", (128, HKC))
    lnwf_b = sb(ctx, "lnwf_b", (128, HKC), BF16)
    lnbf_b = sb(ctx, "lnbf_b", (128, HKC), BF16)
    gs_sb = sb(ctx, "gs_sb", (128, 2))
    tiny = sb(ctx, "tiny", (1, 8))
    st6 = sb(ctx, "st6", (128, 12))
    st2 = sb(ctx, "st2", (128, 2))
    rs_sb = sb(ctx, "rs_sb", (128, 4))
    cst = sb(ctx, "cst", (128, 6))

    w_in_sb = sb(ctxM, "w_in_sb", (128, KC, 2 * EH), BF16)
    w_x_sb = sb(ctxM, "w_x_sb", (128, KC * 96), BF16)
    w_dt_sb = sb(ctxM, "w_dt_sb", (RNK, EH), BF16)
    big = sb(ctxM, "big", (128, 2 * D))
    xtile = sb(ctxM, "xtile", (128, D))
    h_tok = sb(ctxM, "h_tok", (128, D), BF16)
    h_f = sb(ctxM, "h_f", (128, KC, TT), BF16)
    xi = sb(ctxM, "xi", (128, DC, TT + 4))
    sz = sb(ctxM, "sz", (128, DC, TT))
    u_sb = sb(ctxM, "u_sb", (128, DC, TT))
    ubf = sb(ctxM, "ubf", (128, TT), BF16)
    dt_sb = sb(ctxM, "dt_sb", (128, DC, TT))
    dtu_sb = sb(ctxM, "dtu_sb", (128, DC, TT))
    dtin_bf = sb(ctxM, "dtin_bf", (RNK, TT), BF16)
    sp1 = sb(ctxM, "sp1", (128, TT))
    sp2 = sb(ctxM, "sp2", (128, TT))
    sp3 = sb(ctxM, "sp3", (128, TT))
    bbc = sb(ctxM, "bbc", (128, 2, TT))
    cbc = sb(ctxM, "cbc", (128, 2, TT))
    da2 = sb(ctxM, "da2", (128, 2, TT + 1))
    db2 = sb(ctxM, "db2", (128, 2, 2, TT + 1))
    yp_scr = sb(ctxM, "yp_scr", (128, TT))
    yacc = sb(ctxM, "yacc", (128, DC, TT))
    h_carry = sb(ctxM, "h_carry", (128, DC * NST))
    y2h = sb(ctxM, "y2h", (128, DC, TT), BF16)
    y2kc = sb(ctxM, "y2kc", (128, 2, HT), BF16)
    wo_st = sb(ctxM, "wo_st", (128, 2, D), BF16)
    epim = sb(ctxM, "epim", (128, 2, D))

    pm = [psum(f"pm{i}", (128, TT)) for i in range(8)]

    s_tp = sem("s_tp")
    s_ld = sem("s_ld"); s_gl = sem("s_gl"); s_mm = sem("s_mm")
    s_sc = sem("s_sc"); s_ve = sem("s_ve"); s_cc = sem("s_cc"); s_st = sem("s_st")
    s_gp = sem("s_gp")
    s_x = sem("s_x"); s_xd = sem("s_xd"); s_bb = sem("s_bb"); s_cb = sem("s_cb")
    s_wo = sem("s_wo"); s_y2 = sem("s_y2"); s_wi = sem("s_wi"); s_pc = sem("s_pc")
    s_rw = sem("s_rw")
    S = {"ld": s_ld, "gl": s_gl, "mm": s_mm, "sc": s_sc, "ve": s_ve,
         "cc": s_cc, "st": s_st, "x": s_x, "xd": s_xd, "bb": s_bb, "cb": s_cb,
         "wo": s_wo, "y2": s_y2, "wi": s_wi, "pc": s_pc, "rw": s_rw, "gp": s_gp}
    c = {k: 0 for k in S}

    block = ctx.enter_context(nc.Block())

    DMA_KEYS = ("ld", "gl", "st", "x", "xd", "bb", "cb", "wo", "y2", "wi",
                "pc", "rw")
    def inc(instr, key, n=None):
        if n is None:
            n = 16 if key in DMA_KEYS else 1
        instr.then_inc(S[key], n)
        c[key] += n
        return c[key]

    tapcnt = [0]
    def tap(name, src_ap, gates=()):
        if not DEBUG_TAPS:
            return
        @block.scalar
        def _(s, name=name, src_ap=src_ap, gates=tuple(gates)):
            for key, val in gates:
                s.wait_ge(S[key], val)
            s.dma_start(out=taps[name], in_=src_ap).then_inc(s_tp, 16)
            tapcnt[0] += 16
            s.wait_ge(s_tp, tapcnt[0])



    @block.sync
    def _(sync):
        for dst, srcap in ((id_f, ident), (g_sb, ln_g), (lnb_sb, ln_b),
                           (convw_sb, conv_w), (convb_sb, conv_b),
                           (bdt_sb, b_dt), (dp_sb, dp_in),
                           (lnwf_sb, lnw_f), (lnbf_sb, lnb_f)):
            inc(sync.dma_start(out=dst[:], in_=srcap), "pc")
        inc(sync.dma_start(out=sp1[0:RNK, 0:TT], in_=w_dtT[:, 0:TT]), "pc")
        inc(sync.dma_start(out=sp2[0:RNK, 0:TT], in_=w_dtT[:, TT:EH]), "pc")
        inc(sync.dma_start(out=big[:, 0:KC * 96].rearrange("p (k c) -> p k c", k=KC),
                           in_=w_xT.rearrange("(k p) c -> p k c", p=128)), "pc")
        inc(sync.dma_start(out=big[:, 1024:1024 + DC * NST], in_=a_log), "pc")
    ld_consts = c["pc"]

    @block.vector
    def _(v):
        v.wait_ge(s_pc, ld_consts)
        inc(v.memset(ones_f[:], 1.0), "ve")
        inc(v.memset(cst[:, 0:1], EPS1), "ve")
        inc(v.memset(cst[:, 1:2], EPS2), "ve")
        inc(v.memset(cst[:, 2:3], 1.0), "ve")
        inc(v.memset(cst[:, 3:4], -1.0), "ve")
        inc(v.memset(cst[:, 4:5], RSQ2), "ve")
        inc(v.memset(ones_b[:], 1.0), "ve")
        inc(v.memset(h_carry[:], 0.0), "ve")
        inc(v.memset(da2[:, :, 0:1].opt(), 0.0), "ve")
        inc(v.memset(xi[:, :, 0:4].opt(), 0.0), "ve")
        inc(v.tensor_copy(out=id_b[:], in_=id_f[:]), "ve")
        inc(v.tensor_copy(out=w_dt_sb[:, 0:TT], in_=sp1[0:RNK, 0:TT]), "ve")
        inc(v.tensor_copy(out=w_dt_sb[:, TT:EH], in_=sp2[0:RNK, 0:TT]), "ve")
        inc(v.tensor_copy(out=w_x_sb[:], in_=big[:, 0:KC * 96]), "ve")
        inc(v.tensor_copy(out=lnwf_b[:], in_=lnwf_sb[:]), "ve")
        inc(v.tensor_copy(out=lnbf_b[:], in_=lnbf_sb[:]), "ve")
    ve_consts = c["ve"]

    @block.scalar
    def _(s):
        s.wait_ge(s_pc, ld_consts)
        inc(s.activation(out=a_sb[:], in_=big[:, 1024:1024 + DC * NST], func=AF.Exp), "sc")
    sc_aexp = c["sc"]
    @block.vector
    def _(v):
        v.wait_ge(s_sc, sc_aexp)
        inc(v.tensor_scalar(out=a_sb[:], in0=a_sb[:], scalar1=-1.0, scalar2=None,
                            op0=ALU.mult), "ve")
    ve_aneg = c["ve"]

    win_cast_at = {}
    for kc in range(KC):
        buf = big[:, 0:2048]
        gate = win_cast_at.get(kc - 1)
        @block.sync
        def _(sync, buf=buf, kc=kc, gate=gate):
            if kc == 0:
                sync.wait_ge(s_ve, ve_consts)
                sync.wait_ge(s_sc, sc_aexp)
            if gate is not None:
                sync.wait_ge(s_sc, gate)
            inc(sync.dma_start(out=buf, in_=w_inT[kc * 128:(kc + 1) * 128, :]), "wi")
        ld_here = c["wi"]
        @block.tensor
        def _(t, kc=kc, buf=buf, ld_here=ld_here, ve_gate=c["ve"]):
            t.wait_ge(s_wi, ld_here)
            if kc > 0:
                t.wait_ge(s_ve, ve_gate)
            for mc in range(MCXZ):
                inc(t.matmul(pm[6][:, mc:mc + 1], buf[:, mc * 128:(mc + 1) * 128],
                             lnb_sb[:, kc:kc + 1], start=True, stop=True), "mm")
        mm_here = c["mm"]
        @block.vector
        def _(v, kc=kc, mm_here=mm_here):
            v.wait_ge(s_mm, mm_here)
            if kc == 0:
                inc(v.tensor_copy(out=winb_sb[:], in_=pm[6][:, 0:MCXZ]), "ve")
            else:
                inc(v.tensor_tensor(out=winb_sb[:], in0=winb_sb[:],
                                    in1=pm[6][:, 0:MCXZ], op=ALU.add), "ve")
            v.wait_ge(s_ve, c["ve"])
        @block.scalar
        def _(s, kc=kc, buf=buf, mm_here=mm_here):
            s.wait_ge(s_mm, mm_here)
            inc(s.activation(out=w_in_sb[:, kc, :], in_=buf, func=AF.Copy,
                             scale=g_sb[:, kc:kc + 1]), "sc")
        win_cast_at[kc] = c["sc"]
    sc_winb = c["sc"]

    wout_st_at = {}
    for kc in range(NKO):
        buf = big[:, (kc % 2) * 1024:(kc % 2) * 1024 + 1024]
        gate = wout_st_at.get(kc - 2)
        @block.gpsimd
        def _(g, kc=kc, buf=buf, gate=gate):
            if kc == 0:
                g.wait_ge(s_sc, win_cast_at[KC - 1])
            if gate is not None:
                g.wait_ge(s_st, gate)
            inc(g.dma_start(out=buf, in_=w_outT[kc * 128:(kc + 1) * 128, :]), "gl")
        gl_here = c["gl"]
        @block.vector
        def _(v, kc=kc, buf=buf, gl_here=gl_here, ve_gate=c["ve"]):
            v.wait_ge(s_gl, gl_here)
            inc(v.tensor_copy(out=wo_st[:, kc % 2, :], in_=buf), "ve")
        ve_here = c["ve"]
        @block.sync
        def _(sync, kc=kc, ve_here=ve_here):
            sync.wait_ge(s_ve, ve_here)
            inc(sync.dma_start(out=wout_bf[kc], in_=wo_st[:, kc % 2, :]), "st")
        wout_st_at[kc] = c["st"]

    @block.vector
    def _(v):
        inc(v.memset(gs_sb[:], 0.0), "ve")
    gamma_prev = c["ve"]
    gl_gamma_last = c["gl"]
    for wi, (wT, nk, cols) in enumerate(((w_fcT, KC, HF), (w_projT, HKC, D))):
        nh = 2 if cols > 2048 else 1
        hw = cols // nh
        for kc in range(nk):
            for hh in range(nh):
                @block.gpsimd
                def _(g, wT=wT, kc=kc, hh=hh, hw=hw, gate=gamma_prev):
                    g.wait_ge(s_ve, gate)
                    inc(g.dma_start(out=big[:, 0:hw],
                                    in_=wT[kc * 128:(kc + 1) * 128,
                                           hh * hw:(hh + 1) * hw]), "gl")
                gl_here = c["gl"]
                @block.vector
                def _(v, wi=wi, hw=hw, gl_here=gl_here):
                    v.wait_ge(s_gl, gl_here)
                    inc(v.tensor_reduce(out=st2[:, 0:1], in_=big[:, 0:hw], axis=AX.X,
                                        op=ALU.add, apply_absolute_value=True), "ve")
                    v.wait_ge(s_ve, c["ve"])
                    inc(v.tensor_tensor(out=gs_sb[:, wi:wi + 1], in0=gs_sb[:, wi:wi + 1],
                                        in1=st2[:, 0:1], op=ALU.add), "ve")
                gamma_prev = c["ve"]
    gl_gamma_last = c["gl"]
    ve_gamma = c["ve"]
    @block.tensor
    def _(t):
        t.wait_ge(s_ve, ve_gamma)
        inc(t.matmul(pm[7][0:1, 0:1], gs_sb[:, 0:1], ones_f[:], start=True, stop=True), "mm")
        inc(t.matmul(pm[7][0:1, 1:2], gs_sb[:, 1:2], ones_f[:], start=True, stop=True), "mm")
    mm_gam = c["mm"]
    @block.vector
    def _(v):
        v.wait_ge(s_mm, mm_gam)
        inc(v.tensor_scalar(out=tiny[0:1, 0:1], in0=pm[7][0:1, 0:1],
                            scalar1=0.5 / (HF * D), scalar2=0.5 * AMEPS,
                            op0=ALU.mult, op1=ALU.add), "ve")
        v.wait_ge(s_ve, c["ve"])
        inc(v.tensor_scalar(out=tiny[0:1, 1:2], in0=tiny[0:1, 0:1],
                            scalar1=-1.0, scalar2=None, op0=ALU.mult), "ve")
        inc(v.tensor_scalar(out=tiny[0:1, 2:3], in0=pm[7][0:1, 1:2],
                            scalar1=0.5 / (HF * D), scalar2=0.5 * AMEPS,
                            op0=ALU.mult, op1=ALU.add), "ve")
        v.wait_ge(s_ve, c["ve"])
        inc(v.tensor_scalar(out=tiny[0:1, 3:4], in0=tiny[0:1, 2:3],
                            scalar1=-1.0, scalar2=None, op0=ALU.mult), "ve")
    ve_thr = c["ve"]
    @block.sync
    def _(sync):
        sync.wait_ge(s_ve, ve_thr)
        inc(sync.dma_start(out=thr_dram, in_=tiny[0:1, 0:4]), "st")
    st_thr = c["st"]
    @block.sync
    def _(sync):
        sync.wait_ge(s_st, st_thr)
        inc(sync.dma_start(out=thr_sb[:], in_=thr_dram.to_broadcast([128, 4])), "pc")
    ld_thr = c["pc"]

    for k in range(NCH):
        for tt in range(4):
            @block.sync
            def _(sync, k=k, tt=tt, gate_ve=c["ve"]):
                sync.wait_ge(s_ve, gate_ve)
                inc(sync.dma_start(out=xtile[:],
                                   in_=x_in[k * TT + tt * 128: k * TT + (tt + 1) * 128, :]), "x")
            ld_x = c["x"]
            mtile = big[:, (tt % 2) * D:(tt % 2 + 1) * D]
            @block.vector
            def _(v, ld_x=ld_x, k=k, tt=tt, gl_gate=gl_gamma_last):
                v.wait_ge(s_x, ld_x)
                if k == 0 and tt == 0:
                    v.wait_ge(s_ve, ve_aneg)
                    v.wait_ge(s_gl, gl_gate)
                inc(v.bn_stats(out=st6[:, 0:6], in_=xtile[:, 0:512]), "ve")
                inc(v.bn_stats(out=st6[:, 6:12], in_=xtile[:, 512:1024]), "ve")
                v.wait_ge(s_ve, c["ve"])
                inc(v.bn_aggr(out=st2[:], in_=st6[:]), "ve")
            ve_stat1 = c["ve"]
            @block.scalar
            def _(s, ve_stat1=ve_stat1):
                s.wait_ge(s_ve, ve_stat1)
                inc(s.activation(out=rs_sb[:, 0:1], in_=st2[:, 1:2], func=AF.Sqrt,
                                 bias=cst[:, 0:1]), "sc")
            sc_r1 = c["sc"]
            @block.vector
            def _(v, mtile=mtile, sc_r1=sc_r1):
                v.wait_ge(s_sc, sc_r1)
                inc(v.reciprocal(out=rs_sb[:, 0:1], in_=rs_sb[:, 0:1]), "ve")
                _ = v.wait_ge(s_ve, c["ve"])
                inc(v.tensor_scalar(out=mtile, in0=xtile[:], scalar1=st2[:, 0:1],
                                    scalar2=rs_sb[:, 0:1], op0=ALU.subtract,
                                    op1=ALU.mult), "ve")
                inc(v.bn_stats(out=st6[:, 0:6], in_=mtile[:, 0:512]), "ve")
                inc(v.bn_stats(out=st6[:, 6:12], in_=mtile[:, 512:1024]), "ve")
                v.wait_ge(s_ve, c["ve"])
                inc(v.bn_aggr(out=st2[:], in_=st6[:]), "ve")
            ve_stat2 = c["ve"]
            @block.scalar
            def _(s, ve_stat2=ve_stat2):
                s.wait_ge(s_ve, ve_stat2)
                inc(s.activation(out=rs_sb[:, 1:2], in_=st2[:, 1:2], func=AF.Sqrt,
                                 bias=cst[:, 1:2]), "sc")
            sc_r2 = c["sc"]
            @block.vector
            def _(v, mtile=mtile, sc_r2=sc_r2):
                v.wait_ge(s_sc, sc_r2)
                inc(v.reciprocal(out=rs_sb[:, 1:2], in_=rs_sb[:, 1:2]), "ve")
                _ = v.wait_ge(s_ve, c["ve"])
                inc(v.tensor_scalar(out=h_tok[:], in0=mtile, scalar1=st2[:, 0:1],
                                    scalar2=rs_sb[:, 1:2], op0=ALU.subtract,
                                    op1=ALU.mult), "ve")
            ve_h = c["ve"]
            @block.tensor
            def _(t, tt=tt, ve_h=ve_h, sc_gate=c["sc"], ve_gate=c["ve"]):
                t.wait_ge(s_ve, max(ve_h, ve_gate))
                t.wait_ge(s_sc, sc_gate)
                pmb = pm[4][:].bitcast(BF16)
                for kc in range(KC):
                    inc(t.matmul(pmb[:, kc * 128:(kc + 1) * 128],
                                 h_tok[:, kc * 128:(kc + 1) * 128], id_b[:],
                                 start=True, stop=True, is_transpose=True), "mm")
            mm_tr = c["mm"]
            @block.scalar
            def _(s, tt=tt, mm_tr=mm_tr):
                s.wait_ge(s_mm, mm_tr)
                inc(s.activation(out=h_f[:, :, tt * 128:(tt + 1) * 128],
                                 in_=pm[4][:].bitcast(BF16).rearrange(
                                     "p (kc t) -> p kc t", kc=KC),
                                 func=AF.Copy), "sc")
        sc_hf = c["sc"]
        if k == 0:
            tap("t_hf", h_f[:].rearrange("p a b -> p (a b)").bitcast(F32),
                gates=[("sc", c["sc"])])

        for mc in range(MCXZ):
            bank = pm[mc % 4]
            @block.tensor
            def _(t, mc=mc, bank=bank, sc_gate=c["sc"], ve_gate=c["ve"]):
                t.wait_ge(s_sc, sc_gate)
                t.wait_ge(s_ve, ve_gate)
                for kc in range(KC):
                    inc(t.matmul(bank[:], w_in_sb[:, kc, mc * 128:(mc + 1) * 128],
                                 h_f[:, kc, :], start=(kc == 0), stop=(kc == KC - 1)),
                        "mm")
            mm_mc = c["mm"]
            @block.scalar
            def _(s, mc=mc, bank=bank, mm_mc=mm_mc, ve_gate=c["ve"]):
                s.wait_ge(s_mm, mm_mc)
                if mc == 0:
                    s.wait_ge(s_ve, ve_gate)
                if mc < DC:
                    inc(s.activation(out=xi[:, mc, 4:4 + TT], in_=bank[:],
                                     func=AF.Identity,
                                     bias=winb_sb[:, mc:mc + 1]), "sc")
                else:
                    inc(s.activation(out=sz[:, mc - DC, :], in_=bank[:], func=AF.Silu,
                                     bias=winb_sb[:, mc:mc + 1]), "sc")
        sc_xz = c["sc"]
        if k == 0:
            tap("t_xi", xi[:], gates=[("sc", c["sc"])])
            tap("t_sz", sz[:], gates=[("sc", c["sc"])])

        for dc in range(DC):
            @block.vector
            def _(v, dc=dc, sc_xz=sc_xz):
                v.wait_ge(s_sc, sc_xz)
                inc(v.tensor_scalar(out=sp1[:], in0=xi[:, dc, 1:1 + TT],
                                    scalar1=convw_sb[:, dc * 4:dc * 4 + 1],
                                    scalar2=convb_sb[:, dc:dc + 1],
                                    op0=ALU.mult, op1=ALU.add), "ve")
                inc(v.scalar_tensor_tensor(out=sp2[:], in0=xi[:, dc, 2:2 + TT],
                                           scalar=convw_sb[:, dc * 4 + 1:dc * 4 + 2],
                                           in1=sp1[:], op0=ALU.mult, op1=ALU.add), "ve")
                inc(v.scalar_tensor_tensor(out=sp1[:], in0=xi[:, dc, 3:3 + TT],
                                           scalar=convw_sb[:, dc * 4 + 2:dc * 4 + 3],
                                           in1=sp2[:], op0=ALU.mult, op1=ALU.add), "ve")
                inc(v.scalar_tensor_tensor(out=sp2[:], in0=xi[:, dc, 4:4 + TT],
                                           scalar=convw_sb[:, dc * 4 + 3:dc * 4 + 4],
                                           in1=sp1[:], op0=ALU.mult, op1=ALU.add), "ve")
                inc(v.tensor_copy(out=xi[:, dc, 1:4], in_=xi[:, dc, 1 + TT:4 + TT]), "ve")
            ve_cpre = c["ve"]
            @block.scalar
            def _(s, dc=dc, ve_cpre=ve_cpre):
                s.wait_ge(s_ve, ve_cpre)
                inc(s.activation(out=u_sb[:, dc, :], in_=sp2[:], func=AF.Silu), "sc")
            sc_u = c["sc"]
            @block.vector
            def _(v, dc=dc, sc_u=sc_u, mm_gate=c["mm"]):
                v.wait_ge(s_sc, sc_u)
                if dc > 0:
                    v.wait_ge(s_mm, mm_gate)
                inc(v.tensor_copy(out=ubf[:], in_=u_sb[:, dc, :]), "ve")
            ve_ubf = c["ve"]
            @block.tensor
            def _(t, dc=dc, ve_ubf=ve_ubf, sc_gate=c["sc"]):
                t.wait_ge(s_ve, ve_ubf)
                if dc == 0:
                    t.wait_ge(s_sc, sc_gate)
                inc(t.matmul(pm[6][0:96, :], w_x_sb[:, dc * 96:(dc + 1) * 96],
                             ubf[:], start=(dc == 0), stop=(dc == DC - 1)),
                    "mm")
        mm_xp = c["mm"]
        @block.vector
        def _(v, mm_xp=mm_xp):
            v.wait_ge(s_mm, mm_xp)
            inc(v.tensor_copy(out=sp1[0:96, :], in_=pm[6][0:96, :]), "ve")
        ve_xp = c["ve"]
        @block.sync
        def _(sync, k=k, ve_xp=ve_xp):
            sync.wait_ge(s_ve, ve_xp)
            inc(sync.dma_start(out=e1_in[k], in_=sp1[0:96, :]), "st")
        st_e1 = c["st"]
        @block.gpsimd
        def _(g, k=k, st_e1=st_e1):
            g.wait_ge(s_st, st_e1)
            inc(g.collective_compute("AllReduce", ALU.add, replica_groups=PAIRS,
                                     ins=[e1_in[k]], outs=[e1_out[k]]), "cc")
        cc1 = c["cc"]
        @block.sync
        def _(sync, k=k, cc1=cc1):
            sync.wait_ge(s_cc, cc1)
            inc(sync.dma_start(out=sp1[0:RNK, :], in_=e1_out[k][0:RNK, :]), "xd")
        ld_xdbl = c["xd"]
        if k == 0:
            tap("t_xdbl", sp1[0:96, :], gates=[("xd", c["xd"])])

        @block.vector
        def _(v, ld_xdbl=ld_xdbl):
            v.wait_ge(s_xd, ld_xdbl)
            inc(v.tensor_copy(out=dtin_bf[:], in_=sp1[0:RNK, :]), "ve")
        ve_dtin = c["ve"]
        for dc in range(DC):
            bank = pm[dc % 2]
            @block.tensor
            def _(t, dc=dc, bank=bank, ve_dtin=ve_dtin, ve_gate=c["ve"], sc_gate=c["sc"]):
                t.wait_ge(s_ve, max(ve_dtin, ve_gate))
                t.wait_ge(s_sc, sc_gate)
                inc(t.matmul(bank[:], w_dt_sb[:, dc * 128:(dc + 1) * 128],
                             dtin_bf[:], start=True, stop=True), "mm")
            mm_dt = c["mm"]
            @block.vector
            def _(v, dc=dc, bank=bank, mm_dt=mm_dt):
                v.wait_ge(s_mm, mm_dt)
                inc(v.tensor_scalar(out=dt_sb[:, dc, :], in0=bank[:],
                                    scalar1=bdt_sb[:, dc:dc + 1],
                                    scalar2=0.0, op0=ALU.add, op1=ALU.max), "ve")
                inc(v.tensor_scalar(out=dtu_sb[:, dc, :], in0=bank[:],
                                    scalar1=bdt_sb[:, dc:dc + 1],
                                    scalar2=-1.0, op0=ALU.add, op1=ALU.mult), "ve")
                inc(v.scalar_tensor_tensor(out=dtu_sb[:, dc, :],
                                           in0=bank[:],
                                           scalar=bdt_sb[:, dc:dc + 1],
                                           in1=dtu_sb[:, dc, :],
                                           op0=ALU.add, op1=ALU.max), "ve")
        ve_av = c["ve"]
        @block.scalar
        def _(s, ve_av=ve_av):
            s.wait_ge(s_ve, ve_av)
            inc(s.activation(out=big[:, 0:2048],
                             in_=dtu_sb[:, 0:4, :].rearrange("p a b -> p (a b)"),
                             func=AF.Exp, scale=cst[:, 3:4]), "sc")
            inc(s.activation(out=sp2[:],
                             in_=dtu_sb[:, 4, :], func=AF.Exp, scale=cst[:, 3:4]), "sc")
            inc(s.activation(out=sp3[:],
                             in_=dtu_sb[:, 5, :], func=AF.Exp, scale=cst[:, 3:4]), "sc")
            inc(s.activation(out=sp1[:],
                             in_=dtu_sb[:, 6, :], func=AF.Exp, scale=cst[:, 3:4]), "sc")
            inc(s.activation(out=h_tok[:].bitcast(F32),
                             in_=dtu_sb[:, 7, :], func=AF.Exp, scale=cst[:, 3:4]), "sc")
            inc(s.activation(out=dtu_sb[:, 0:4, :].rearrange("p a b -> p (a b)"),
                             in_=big[:, 0:2048], func=AF.Ln, bias=cst[:, 2:3]), "sc")
            inc(s.activation(out=dtu_sb[:, 4, :], in_=sp2[:], func=AF.Ln,
                             bias=cst[:, 2:3]), "sc")
            inc(s.activation(out=dtu_sb[:, 5, :], in_=sp3[:], func=AF.Ln,
                             bias=cst[:, 2:3]), "sc")
            inc(s.activation(out=dtu_sb[:, 6, :], in_=sp1[:], func=AF.Ln,
                             bias=cst[:, 2:3]), "sc")
            inc(s.activation(out=dtu_sb[:, 7, :], in_=h_tok[:].bitcast(F32)[:, 0:TT],
                             func=AF.Ln, bias=cst[:, 2:3]), "sc")
        sc_sp = c["sc"]
        for dc in range(DC):
            @block.vector
            def _(v, dc=dc, sc_sp=sc_sp):
                if dc == 0:
                    v.wait_ge(s_sc, sc_sp)
                inc(v.tensor_tensor(out=dt_sb[:, dc, :], in0=dt_sb[:, dc, :],
                                    in1=dtu_sb[:, dc, :], op=ALU.add), "ve")
                inc(v.tensor_tensor(out=dtu_sb[:, dc, :], in0=dt_sb[:, dc, :],
                                    in1=u_sb[:, dc, :], op=ALU.mult), "ve")
        ve_dt = c["ve"]
        if k == 0:
            tap("t_dt", dt_sb[:], gates=[("ve", c["ve"])])
            tap("t_u", u_sb[:], gates=[("ve", c["ve"])])

        for n in range(NST):
            @block.sync
            def _(sync, k=k, n=n, ve_gate=c["ve"], gp_gate=c["gp"]):
                if n >= 2:
                    sync.wait_ge(s_ve, ve_gate)
                    sync.wait_ge(s_gp, gp_gate)
                inc(sync.dma_start(out=bbc[:, n % 2, :],
                                   in_=e1_out[k][64 + n:65 + n, :].to_broadcast([128, TT])), "bb")
                inc(sync.dma_start(out=cbc[:, n % 2, :],
                                   in_=e1_out[k][80 + n:81 + n, :].to_broadcast([128, TT])), "cb")
            ld_bc = (c["bb"], c["cb"])
            for p_ in range(DC // 2):
                slot = (n * (DC // 2) + p_) % 2
                d0, d1 = 2 * p_, 2 * p_ + 1
                @block.scalar
                def _(s, n=n, d0=d0, d1=d1, ve_dt=ve_dt, ve_gate=c["ve"]):
                    s.wait_ge(s_ve, max(ve_dt, ve_gate))
                    inc(s.activation(out=da2[:, 0, 1:1 + TT], in_=dt_sb[:, d0, :],
                                     func=AF.Exp, scale=a_sb[:, d0 * NST + n:d0 * NST + n + 1]), "sc")
                    inc(s.activation(out=da2[:, 1, 1:1 + TT], in_=dt_sb[:, d1, :],
                                     func=AF.Exp, scale=a_sb[:, d1 * NST + n:d1 * NST + n + 1]), "sc")
                sc_da = c["sc"]
                @block.gpsimd
                def _(g, n=n, p_=p_, slot=slot, d0=d0, d1=d1, ld_bc=ld_bc,
                      ve_dt=ve_dt, ve_gate=c["ve"]):
                    g.wait_ge(s_bb, ld_bc[0])
                    g.wait_ge(s_ve, max(ve_dt, ve_gate))
                    inc(g.tensor_copy(out=db2[:, slot, 0, 0:1],
                                      in_=h_carry[:, d0 * NST + n:d0 * NST + n + 1]), "gp")
                    inc(g.tensor_copy(out=db2[:, slot, 1, 0:1],
                                      in_=h_carry[:, d1 * NST + n:d1 * NST + n + 1]), "gp")
                    inc(g.tensor_tensor(
                        out=db2[:, slot, :, 1:1 + TT],
                        in0=dtu_sb[:, d0:d0 + 2, :],
                        in1=bbc[:, n % 2, :].unsqueeze(1).to_broadcast([128, 2, TT]),
                        op=ALU.mult), "gp")
                sc_gp = c["gp"]
                @block.vector
                def _(v, n=n, slot=slot, d0=d0, d1=d1, sc_da=sc_da, sc_gp=sc_gp,
                      ld_bc=ld_bc):
                    v.wait_ge(s_sc, sc_da)
                    v.wait_ge(s_gp, sc_gp)
                    v.wait_ge(s_cb, ld_bc[1])
                    inc(v.tensor_tensor_scan(
                        out=db2[:, slot, :, :].rearrange("p a b -> p (a b)"),
                        data0=da2[:].rearrange("p a b -> p (a b)"),
                        data1=db2[:, slot, :, :].rearrange("p a b -> p (a b)"),
                        initial=0.0, op0=ALU.mult, op1=ALU.add), "ve")
                    inc(v.tensor_copy(out=h_carry[:, d0 * NST + n:d0 * NST + n + 1],
                                      in_=db2[:, slot, 0, TT:TT + 1]), "ve")
                    inc(v.tensor_copy(out=h_carry[:, d1 * NST + n:d1 * NST + n + 1],
                                      in_=db2[:, slot, 1, TT:TT + 1]), "ve")
                    inc(v.tensor_tensor(
                        out=db2[:, slot, :, 1:1 + TT],
                        in0=db2[:, slot, :, 1:1 + TT],
                        in1=cbc[:, n % 2, :].unsqueeze(1).to_broadcast([128, 2, TT]),
                        op=ALU.mult), "ve")
                ve_yp = c["ve"]
                @block.gpsimd
                def _(g, n=n, slot=slot, d0=d0, ve_yp=ve_yp):
                    g.wait_ge(s_ve, ve_yp)
                    if n == 0:
                        inc(g.tensor_copy(out=yacc[:, d0:d0 + 2, :],
                                          in_=db2[:, slot, :, 1:1 + TT]), "gp")
                    else:
                        inc(g.tensor_tensor(out=yacc[:, d0:d0 + 2, :],
                                            in0=yacc[:, d0:d0 + 2, :],
                                            in1=db2[:, slot, :, 1:1 + TT],
                                            op=ALU.add), "gp")

        if k == 0:
            tap("t_yacc", yacc[:], gates=[("ve", c["ve"])])
        gp_lat = c["gp"]
        for dc in range(DC):
            @block.vector
            def _(v, dc=dc, gp_lat=gp_lat):
                if dc == 0:
                    v.wait_ge(s_gp, gp_lat)
                inc(v.scalar_tensor_tensor(out=yp_scr[:], in0=u_sb[:, dc, :],
                                           scalar=dp_sb[:, dc:dc + 1],
                                           in1=yacc[:, dc, :], op0=ALU.mult,
                                           op1=ALU.add), "ve")
                inc(v.tensor_tensor(out=y2h[:, dc, :], in0=yp_scr[:],
                                    in1=sz[:, dc, :], op=ALU.mult), "ve")
        ve_y2 = c["ve"]
        if k == 0:
            tap("t_y2h", y2h[:].rearrange("p a b -> p (a b)").bitcast(F32),
                gates=[("ve", c["ve"])])
        @block.sync
        def _(sync, k=k, ve_y2=ve_y2):
            sync.wait_ge(s_ve, ve_y2)
            inc(sync.dma_start(out=e2_in[k].rearrange("(dc p) t -> p dc t", p=128),
                               in_=y2h[:]), "st")
        st_e2 = c["st"]
        @block.gpsimd
        def _(g, k=k, st_e2=st_e2):
            g.wait_ge(s_st, st_e2)
            inc(g.collective_compute("AllGather", ALU.bypass, replica_groups=PAIRS,
                                     ins=[e2_in[k]], outs=[e2_out[k]]), "cc")
        cc2 = c["cc"]

        y2full = h_f[:].rearrange("p a b -> p (a b)").rearrange(
            "p (kc t) -> p kc t", kc=NKO)
        qkey = "ld" if k % 2 == 0 else "gl"
        mm_before_wo = c["mm"]
        def emit_y2_load(eng, k=k, cc2=cc2, mm_before=mm_before_wo):
            eng.wait_ge(s_cc, cc2)
            eng.wait_ge(s_mm, mm_before)
            par = eng.snap(eng.partition_id() % 2, min_val=0, max_val=1)
            inc(eng.dma_start(
                out=y2full,
                in_=e2_out[k].rearrange("(kc p) t -> p kc t", p=128)[:, :, bass.ts(par, HT)]),
                "y2")
        if k % 2 == 0:
            block.sync(emit_y2_load)
        else:
            block.gpsimd(emit_y2_load)
        ld_y2 = c["y2"]
        for kc in range(NKO):
            @block.sync
            def _(sync, kc=kc, mm_gate=c["mm"]):
                if kc >= 2:
                    sync.wait_ge(s_mm, mm_gate)
                inc(sync.dma_start(out=wo_st[:, kc % 2, :], in_=wout_bf[kc]), "wo")
            ld_wo = c["wo"]
            @block.tensor
            def _(t, kc=kc, ld_wo=ld_wo, ld_y2=ld_y2, sc_gate=c["sc"], ve_gate=c["ve"], qkey=qkey):
                t.wait_ge(s_wo, ld_wo)
                if kc == 0:
                    t.wait_ge(s_y2, ld_y2)
                    t.wait_ge(s_sc, sc_gate)
                    t.wait_ge(s_ve, ve_gate)
                for mc in range(8):
                    inc(t.matmul(pm[mc][:, 0:HT],
                                 wo_st[:, kc % 2, mc * 128:(mc + 1) * 128],
                                 y2full[:, kc, :], start=(kc == 0),
                                 stop=(kc == NKO - 1)), "mm")
        mm_wo = c["mm"]

        for ht in range(2):
            @block.sync
            def _(sync, k=k, ht=ht, gate_ve=c["ve"]):
                sync.wait_ge(s_ve, gate_ve)
                inc(sync.dma_start(out=xtile[:],
                                   in_=x_own[k][ht * 128:(ht + 1) * 128, :]), "x")
            ld_xo = c["x"]
            @block.vector
            def _(v, ht=ht, ld_xo=ld_xo):
                v.wait_ge(s_x, ld_xo)
                inc(v.bn_stats(out=st6[:, 0:6], in_=xtile[:, 0:512]), "ve")
                inc(v.bn_stats(out=st6[:, 6:12], in_=xtile[:, 512:1024]), "ve")
                v.wait_ge(s_ve, c["ve"])
                inc(v.bn_aggr(out=st2[:], in_=st6[:]), "ve")
            ve_so = c["ve"]
            @block.scalar
            def _(s, ve_so=ve_so):
                s.wait_ge(s_ve, ve_so)
                inc(s.activation(out=rs_sb[:, 0:1], in_=st2[:, 1:2], func=AF.Sqrt,
                                 bias=cst[:, 0:1]), "sc")
            sc_ro = c["sc"]
            @block.vector
            def _(v, ht=ht, sc_ro=sc_ro):
                v.wait_ge(s_sc, sc_ro)
                inc(v.reciprocal(out=rs_sb[:, 0:1], in_=rs_sb[:, 0:1]), "ve")
                _ = v.wait_ge(s_ve, c["ve"])
                inc(v.tensor_scalar(out=epim[:, ht, :], in0=xtile[:],
                                    scalar1=st2[:, 0:1], scalar2=rs_sb[:, 0:1],
                                    op0=ALU.subtract, op1=ALU.mult), "ve")
        for mc in range(8):
            @block.scalar
            def _(s, mc=mc, mm_wo=mm_wo, ve_gate=c["ve"]):
                s.wait_ge(s_mm, mm_wo)
                inc(s.activation(out=sp3[:, (mc % 2) * HT:(mc % 2 + 1) * HT],
                                 in_=pm[mc][:, 0:HT], func=AF.Copy), "sc")
            sc_om = c["sc"]
            @block.tensor
            def _(t, mc=mc, sc_om=sc_om, ve_gate=c["ve"]):
                t.wait_ge(s_sc, sc_om)
                if mc >= 2:
                    t.wait_ge(s_ve, ve_gate)
                for ht in range(2):
                    inc(t.matmul(pm[mc][:, HT + ht * 128: HT + (ht + 1) * 128],
                                 sp3[:, (mc % 2) * HT + ht * 128:(mc % 2) * HT + (ht + 1) * 128],
                                 id_f[:], start=True, stop=True, is_transpose=True), "mm")
            mm_omt = c["mm"]
            @block.vector
            def _(v, mc=mc, mm_omt=mm_omt):
                v.wait_ge(s_mm, mm_omt)
                for ht in range(2):
                    sl = epim[:, ht, mc * 128:(mc + 1) * 128]
                    inc(v.tensor_tensor(out=sl, in0=sl,
                                        in1=pm[mc][:, HT + ht * 128: HT + (ht + 1) * 128],
                                        op=ALU.add), "ve")
        if k == 0:
            tap("t_epim", epim[:], gates=[("ve", c["ve"])])
        for ht in range(2):
            @block.sync
            def _(sync, k=k, ht=ht, ve_gate=c["ve"]):
                sync.wait_ge(s_ve, ve_gate)
                inc(sync.dma_start(out=xtile[:],
                                   in_=x_own[k][ht * 128:(ht + 1) * 128, :]), "x")
            ld_xr = c["x"]
            @block.vector
            def _(v, ht=ht):
                inc(v.bn_stats(out=st6[:, 0:6], in_=epim[:, ht, 0:512]), "ve")
                inc(v.bn_stats(out=st6[:, 6:12], in_=epim[:, ht, 512:1024]), "ve")
                v.wait_ge(s_ve, c["ve"])
                inc(v.bn_aggr(out=st2[:], in_=st6[:]), "ve")
            ve_st3 = c["ve"]
            @block.scalar
            def _(s, ve_st3=ve_st3):
                s.wait_ge(s_ve, ve_st3)
                inc(s.activation(out=rs_sb[:, 2:3], in_=st2[:, 1:2], func=AF.Sqrt,
                                 bias=cst[:, 0:1]), "sc")
            sc_r3 = c["sc"]
            @block.vector
            def _(v, ht=ht, sc_r3=sc_r3, ld_xr=ld_xr):
                v.wait_ge(s_sc, sc_r3)
                v.wait_ge(s_x, ld_xr)
                inc(v.reciprocal(out=rs_sb[:, 2:3], in_=rs_sb[:, 2:3]), "ve")
                _ = v.wait_ge(s_ve, c["ve"])
                inc(v.tensor_scalar(out=epim[:, ht, :], in0=epim[:, ht, :],
                                    scalar1=st2[:, 0:1],
                                    scalar2=rs_sb[:, 2:3], op0=ALU.subtract,
                                    op1=ALU.mult), "ve")
                inc(v.tensor_tensor(out=epim[:, ht, :], in0=epim[:, ht, :],
                                    in1=xtile[:], op=ALU.add), "ve")
            ve_xs1 = c["ve"]
            if k == 0:
                tap_xs1_ht = ht
                if DEBUG_TAPS:
                    @block.scalar
                    def _(s, ht=ht, gv=c["ve"]):
                        s.wait_ge(s_ve, gv)
                        s.dma_start(out=taps["t_xs1"][:, ht, :], in_=epim[:, ht, :]).then_inc(s_tp, 16)
                        tapcnt[0] += 16
                        s.wait_ge(s_tp, tapcnt[0])
            @block.sync
            def _(sync, k=k, ht=ht, ve_xs1=ve_xs1):
                sync.wait_ge(s_ve, ve_xs1)
                inc(sync.dma_start(out=xs1_sp[k][ht * 128:(ht + 1) * 128, :],
                                   in_=epim[:, ht, :]), "st")

    cM = dict(c)
    ctxM.close()

    ctxF = ExitStack()
    sbF = lambda name, shape, dtype=F32: ctxF.enter_context(
        nc.sbuf_tensor(name, list(shape), dtype))
    wfc_sb = sbF("wfc_sb", (128, KC, HF), BF16)
    wpj_sb = sbF("wpj_sb", (128, HKC, D), BF16)
    fstage = sbF("fstage", (128, 2048))
    qs1 = sbF("qs1", (128, 2048))
    qs2 = sbF("qs2", (128, 2048))
    wq_t = sbF("wq_t", (128, D), BF16)
    g_bf = sbF("g_bf", (128, HKC, HT), BF16)
    g2_scr = sbF("g2_scr", (128, HT), BF16)
    ebuf = sbF("ebuf", (128, HT))
    rowsc = sbF("rowsc", (1, 2 * HT))
    sa_bc = sbF("sa_bc", (128, HT))
    nb_bc = sbF("nb_bc", (128, HT))
    nsw_sb = sbF("nsw_sb", (128, 8))
    bo_sb = sbF("bo_sb", (128, 8))
    fo = sbF("fo", (128, 2, HT))
    fo_tok = sbF("fo_tok", (128, 2, D))
    xs1c = sbF("xs1c", (128, 2, D))
    xln_b = sbF("xln_b", (128, D), BF16)
    xlnf = sbF("xlnf", (128, KC, HT), BF16)

    ALLK = tuple(S.keys())
    @block.sync
    def _(sync):
        for key in ALLK:
            sync.wait_ge(S[key], cM[key])
    @block.gpsimd
    def _(g):
        for key in ALLK:
            g.wait_ge(S[key], cM[key])
    @block.tensor
    def _(t):
        for key in ALLK:
            t.wait_ge(S[key], cM[key])
    @block.scalar
    def _(s):
        for key in ALLK:
            s.wait_ge(S[key], cM[key])
    @block.vector
    def _(v):
        for key in ALLK:
            v.wait_ge(S[key], cM[key])

    for kc in range(KC):
        for hh in range(2):
            @block.gpsimd
            def _(g, kc=kc, hh=hh, ve_gate=c["ve"]):
                g.wait_ge(s_ve, ve_gate)
                inc(g.dma_start(out=fstage[:],
                                in_=w_fcT[kc * 128:(kc + 1) * 128,
                                          hh * 2048:(hh + 1) * 2048]), "gl")
            gl_q = c["gl"]
            @block.vector
            def _(v, kc=kc, hh=hh, gl_q=gl_q):
                v.wait_ge(s_gl, gl_q)
                inc(v.tensor_scalar(out=qs1[:], in0=fstage[:], scalar1=thr_sb[:, 0:1],
                                    scalar2=None, op0=ALU.is_ge), "ve")
                inc(v.tensor_scalar(out=qs2[:], in0=fstage[:], scalar1=thr_sb[:, 1:2],
                                    scalar2=None, op0=ALU.is_le), "ve")
                inc(v.tensor_tensor(out=wfc_sb[:, kc, hh * 2048:(hh + 1) * 2048],
                                    in0=qs1[:], in1=qs2[:], op=ALU.subtract), "ve")

    for kc2 in range(HKC):
        @block.gpsimd
        def _(g, kc2=kc2, ve_gate=c["ve"]):
            g.wait_ge(s_ve, ve_gate)
            inc(g.dma_start(out=fstage[:, 0:D],
                            in_=w_projT[kc2 * 128:(kc2 + 1) * 128, :]), "gl")
        gl_q = c["gl"]
        @block.vector
        def _(v, kc2=kc2, gl_q=gl_q, mm_gate=c["mm"]):
            v.wait_ge(s_gl, gl_q)
            if kc2 > 0:
                v.wait_ge(s_mm, mm_gate)
            inc(v.tensor_scalar(out=qs1[:, 0:D], in0=fstage[:, 0:D],
                                scalar1=thr_sb[:, 2:3], scalar2=None, op0=ALU.is_ge), "ve")
            inc(v.tensor_scalar(out=qs2[:, 0:D], in0=fstage[:, 0:D],
                                scalar1=thr_sb[:, 3:4], scalar2=None, op0=ALU.is_le), "ve")
            inc(v.tensor_tensor(out=wq_t[:], in0=qs1[:, 0:D], in1=qs2[:, 0:D],
                                op=ALU.subtract), "ve")
        ve_wq = c["ve"]
        @block.tensor
        def _(t, kc2=kc2, ve_wq=ve_wq):
            t.wait_ge(s_ve, ve_wq)
            for mc in range(8):
                inc(t.matmul(pm[6][:, mc:mc + 1], wq_t[:, mc * 128:(mc + 1) * 128],
                             lnbf_b[:, kc2:kc2 + 1], start=True, stop=True), "mm")
                inc(t.matmul(pm[7][:, mc:mc + 1], wq_t[:, mc * 128:(mc + 1) * 128],
                             lnwf_b[:, kc2:kc2 + 1], start=True, stop=True), "mm")
        mm_bo = c["mm"]
        @block.vector
        def _(v, kc2=kc2, mm_bo=mm_bo):
            v.wait_ge(s_mm, mm_bo)
            if kc2 == 0:
                inc(v.tensor_copy(out=bo_sb[:], in_=pm[6][:, 0:8]), "ve")
                inc(v.tensor_copy(out=nsw_sb[:], in_=pm[7][:, 0:8]), "ve")
            else:
                inc(v.tensor_tensor(out=bo_sb[:], in0=bo_sb[:],
                                    in1=pm[6][:, 0:8], op=ALU.add), "ve")
                inc(v.tensor_tensor(out=nsw_sb[:], in0=nsw_sb[:],
                                    in1=pm[7][:, 0:8], op=ALU.add), "ve")
            v.wait_ge(s_ve, c["ve"])
            inc(v.tensor_scalar(out=wpj_sb[:, kc2, :], in0=wq_t[:],
                                scalar1=lnwf_sb[:, kc2:kc2 + 1], scalar2=None,
                                op0=ALU.mult), "ve")
    @block.vector
    def _(v):
        inc(v.tensor_scalar(out=nsw_sb[:], in0=nsw_sb[:], scalar1=-1.0,
                            scalar2=None, op0=ALU.mult), "ve")
        v.wait_ge(s_ve, c["ve"])
    ve_bo = c["ve"]
    @block.scalar
    def _(s, ve_bo=ve_bo):
        s.wait_ge(s_ve, ve_bo)
    sc_bo = c["sc"]

    if DEBUG_TAPS:
        tap("t_thr", thr_sb[:], gates=[("pc", c["pc"])])
        tap("t_wfc", wfc_sb[:, 0, :].bitcast(F32), gates=[("ve", c["ve"])])
        tap("t_wpj", wpj_sb[:].rearrange("p a b -> p (a b)").bitcast(F32),
            gates=[("ve", c["ve"])])
        if DEBUG_TAPS:
            @block.scalar
            def _(s, gv=c["sc"]):
                s.wait_ge(s_sc, gv)
                s.dma_start(out=taps["t_nswbo"][:, 0:8], in_=nsw_sb[:]).then_inc(s_tp, 16)
                tapcnt[0] += 16
                s.wait_ge(s_tp, tapcnt[0])
        @block.scalar
        def _(s, gv=c["sc"]):
            s.wait_ge(s_sc, gv)
            s.dma_start(out=taps["t_nswbo"][:, 8:16], in_=bo_sb[:]).then_inc(s_tp, 16)
            tapcnt[0] += 16
            s.wait_ge(s_tp, tapcnt[0])

    for h in range(NCH):
        @block.sync
        def _(sync, h=h, ve_gate=c["ve"]):
            if h == 0:
                sync.wait_ge(s_sc, sc_bo)
            sync.wait_ge(s_ve, ve_gate)
            inc(sync.dma_start(out=xs1c[:],
                               in_=xs1_sp[h].rearrange("(a p) d -> p a d", p=128)), "x")
        ld_xs1 = c["x"]
        for ht in range(2):
            @block.vector
            def _(v, ht=ht, ld_xs1=ld_xs1):
                v.wait_ge(s_x, ld_xs1)
                inc(v.bn_stats(out=st6[:, 0:6], in_=xs1c[:, ht, 0:512]), "ve")
                inc(v.bn_stats(out=st6[:, 6:12], in_=xs1c[:, ht, 512:1024]), "ve")
                v.wait_ge(s_ve, c["ve"])
                inc(v.bn_aggr(out=st2[:], in_=st6[:]), "ve")
            ve_sf = c["ve"]
            @block.scalar
            def _(s, ve_sf=ve_sf):
                s.wait_ge(s_ve, ve_sf)
                inc(s.activation(out=rs_sb[:, 0:1], in_=st2[:, 1:2], func=AF.Sqrt,
                                 bias=cst[:, 0:1]), "sc")
            sc_rf = c["sc"]
            @block.vector
            def _(v, ht=ht, sc_rf=sc_rf):
                v.wait_ge(s_sc, sc_rf)
                inc(v.reciprocal(out=rs_sb[:, 0:1], in_=rs_sb[:, 0:1]), "ve")
                _ = v.wait_ge(s_ve, c["ve"])
                inc(v.tensor_scalar(out=xln_b[:], in0=xs1c[:, ht, :],
                                    scalar1=st2[:, 0:1], scalar2=rs_sb[:, 0:1],
                                    op0=ALU.subtract, op1=ALU.mult), "ve")
            ve_xln = c["ve"]
            @block.tensor
            def _(t, ht=ht, ve_xln=ve_xln, sc_gate=c["sc"]):
                t.wait_ge(s_ve, ve_xln)
                t.wait_ge(s_sc, sc_gate)
                pmb = pm[4][:].bitcast(BF16)
                for kc in range(KC):
                    inc(t.matmul(pmb[:, kc * 128:(kc + 1) * 128],
                                 xln_b[:, kc * 128:(kc + 1) * 128], id_b[:],
                                 start=True, stop=True, is_transpose=True), "mm")
            mm_trf = c["mm"]
            @block.scalar
            def _(s, ht=ht, mm_trf=mm_trf):
                s.wait_ge(s_mm, mm_trf)
                inc(s.activation(out=xlnf[:, :, ht * 128:(ht + 1) * 128],
                                 in_=pm[4][:].bitcast(BF16).rearrange(
                                     "p (kc t) -> p kc t", kc=KC),
                                 func=AF.Copy), "sc")
        sc_xlnf = c["sc"]
        if h == 0:
            tap("t_xlnf", xlnf[:].rearrange("p a b -> p (a b)").bitcast(F32),
                gates=[("sc", c["sc"])])

        for mc in range(HKC):
            bank = pm[mc % 4]
            @block.tensor
            def _(t, mc=mc, bank=bank, sc_xlnf=sc_xlnf, sc_gate=c["sc"], ve_gate=c["ve"]):
                t.wait_ge(s_sc, sc_gate)
                t.wait_ge(s_ve, ve_gate)
                for kc in range(KC):
                    inc(t.matmul(bank[:, 0:HT], wfc_sb[:, kc, mc * 128:(mc + 1) * 128],
                                 xlnf[:, kc, :], start=(kc == 0), stop=(kc == KC - 1)),
                        "mm")
            mm_fc = c["mm"]
            @block.scalar
            def _(s, mc=mc, bank=bank, mm_fc=mm_fc):
                s.wait_ge(s_mm, mm_fc)
                inc(s.activation(out=ebuf[:], in_=bank[:, 0:HT], func=AF.Erf,
                                 scale=cst[:, 4:5]), "sc")
            sc_erf = c["sc"]
            @block.vector
            def _(v, mc=mc, bank=bank, sc_erf=sc_erf):
                v.wait_ge(s_sc, sc_erf)
                inc(v.scalar_tensor_tensor(out=g_bf[:, mc, :], in0=ebuf[:],
                                           scalar=1.0, in1=bank[:, 0:HT],
                                           op0=ALU.add, op1=ALU.mult), "ve")
            @block.vector
            def _(v, mc=mc):
                inc(v.tensor_tensor(out=g2_scr[:], in0=g_bf[:, mc, :],
                                    in1=g_bf[:, mc, :], op=ALU.mult), "ve")
            ve_g = c["ve"]
            @block.tensor
            def _(t, mc=mc, ve_g=ve_g):
                t.wait_ge(s_ve, ve_g)
                inc(t.matmul(pm[6][0:1, 0:HT], ones_b[:], g_bf[:, mc, :],
                             start=(mc == 0), stop=(mc == HKC - 1)), "mm")
                inc(t.matmul(pm[7][0:1, 0:HT], ones_b[:], g2_scr[:],
                             start=(mc == 0), stop=(mc == HKC - 1)), "mm")
        mm_st = c["mm"]
        if h == 0:
            tap("t_gbf", g_bf[:].rearrange("p a b -> p (a b)").bitcast(F32),
                gates=[("ve", c["ve"]), ("sc", c["sc"])])
        @block.vector
        def _(v, mm_st=mm_st):
            v.wait_ge(s_mm, mm_st)
            inc(v.tensor_scalar(out=rowsc[0:1, 0:HT], in0=pm[6][0:1, 0:HT],
                                scalar1=0.5 / HF, scalar2=None, op0=ALU.mult), "ve")
            inc(v.tensor_scalar(out=rowsc[0:1, HT:2 * HT], in0=pm[7][0:1, 0:HT],
                                scalar1=0.25 / HF, scalar2=None, op0=ALU.mult), "ve")
            v.wait_ge(s_ve, c["ve"])
            inc(v.tensor_tensor(out=ebuf[0:1, 0:HT], in0=rowsc[0:1, 0:HT],
                                in1=rowsc[0:1, 0:HT], op=ALU.mult), "ve")
            v.wait_ge(s_ve, c["ve"])
            inc(v.tensor_tensor(out=ebuf[0:1, 0:HT], in0=rowsc[0:1, HT:2 * HT],
                                in1=ebuf[0:1, 0:HT], op=ALU.subtract), "ve")
        ve_var = c["ve"]
        @block.scalar
        def _(s, ve_var=ve_var):
            s.wait_ge(s_ve, ve_var)
            inc(s.activation(out=ebuf[0:1, 0:HT], in_=ebuf[0:1, 0:HT],
                             func=AF.Sqrt, bias=cst[0:1, 0:1]), "sc")
        sc_rstdf = c["sc"]
        @block.vector
        def _(v, sc_rstdf=sc_rstdf):
            v.wait_ge(s_sc, sc_rstdf)
            inc(v.reciprocal(out=ebuf[0:1, 0:HT], in_=ebuf[0:1, 0:HT]), "ve")
            v.wait_ge(s_ve, c["ve"])
            inc(v.tensor_tensor(out=rowsc[0:1, 0:HT], in0=rowsc[0:1, 0:HT],
                                in1=ebuf[0:1, 0:HT], op=ALU.mult), "ve")
            inc(v.tensor_scalar(out=rowsc[0:1, HT:2 * HT], in0=ebuf[0:1, 0:HT],
                                scalar1=0.5, scalar2=None, op0=ALU.mult), "ve")
        ve_rows = c["ve"]
        @block.sync
        def _(sync, h=h, ve_rows=ve_rows):
            sync.wait_ge(s_ve, ve_rows)
            inc(sync.dma_start(out=row_dram[h], in_=rowsc[0:1, :].rearrange(
                "o (a t) -> o a t", a=2)), "st")
        st_rows = c["st"]
        @block.sync
        def _(sync, h=h, st_rows=st_rows, ve_gate=c["ve"]):
            sync.wait_ge(s_st, st_rows)
            inc(sync.dma_start(out=nb_bc[:], in_=row_dram[h][0:1, :].to_broadcast([128, HT])), "rw")
            inc(sync.dma_start(out=sa_bc[:], in_=row_dram[h][1:2, :].to_broadcast([128, HT])), "rw")
        ld_rows = c["rw"]

        if h == 0:
            tap("t_rows", rowsc[0:1, :], gates=[("ve", c["ve"])])
        for mc2 in range(8):
            bank = pm[mc2 % 4]
            @block.tensor
            def _(t, mc2=mc2, bank=bank, ve_gate=c["ve"], sc_gate=c["sc"]):
                t.wait_ge(s_ve, ve_gate)
                t.wait_ge(s_sc, sc_gate)
                for kc2 in range(HKC):
                    inc(t.matmul(bank[:, 0:HT], wpj_sb[:, kc2, mc2 * 128:(mc2 + 1) * 128],
                                 g_bf[:, kc2, :], start=(kc2 == 0),
                                 stop=(kc2 == HKC - 1)), "mm")
            mm_pj = c["mm"]
            @block.vector
            def _(v, mc2=mc2, bank=bank, mm_pj=mm_pj, ld_rows=ld_rows):
                v.wait_ge(s_mm, mm_pj)
                v.wait_ge(s_rw, ld_rows)
                inc(v.tensor_tensor(out=fo[:, mc2 % 2, :], in0=bank[:, 0:HT],
                                    in1=sa_bc[:], op=ALU.mult), "ve")
                inc(v.scalar_tensor_tensor(out=fo[:, mc2 % 2, :], in0=nb_bc[:],
                                           scalar=nsw_sb[:, mc2:mc2 + 1],
                                           in1=fo[:, mc2 % 2, :], op0=ALU.mult,
                                           op1=ALU.add), "ve")
                inc(v.tensor_scalar(out=fo[:, mc2 % 2, :], in0=fo[:, mc2 % 2, :],
                                    scalar1=bo_sb[:, mc2:mc2 + 1], scalar2=None,
                                    op0=ALU.add), "ve")
            ve_fo = c["ve"]
            @block.tensor
            def _(t, mc2=mc2, bank=bank, ve_fo=ve_fo):
                t.wait_ge(s_ve, ve_fo)
                for ht in range(2):
                    inc(t.matmul(bank[:, HT + ht * 128:HT + (ht + 1) * 128],
                                 fo[:, mc2 % 2, ht * 128:(ht + 1) * 128], id_f[:],
                                 start=True, stop=True, is_transpose=True), "mm")
            mm_fot = c["mm"]
            @block.scalar
            def _(s, mc2=mc2, bank=bank, mm_fot=mm_fot):
                s.wait_ge(s_mm, mm_fot)
                for ht in range(2):
                    inc(s.activation(out=fo_tok[:, ht, mc2 * 128:(mc2 + 1) * 128],
                                     in_=bank[:, HT + ht * 128:HT + (ht + 1) * 128],
                                     func=AF.Copy), "sc")
        sc_fot = c["sc"]
        if h == 0:
            tap("t_fotok", fo_tok[:], gates=[("sc", c["sc"])])
        for ht in range(2):
            @block.vector
            def _(v, ht=ht, sc_fot=sc_fot):
                v.wait_ge(s_sc, sc_fot)
                inc(v.bn_stats(out=st6[:, 0:6], in_=fo_tok[:, ht, 0:512]), "ve")
                inc(v.bn_stats(out=st6[:, 6:12], in_=fo_tok[:, ht, 512:1024]), "ve")
                v.wait_ge(s_ve, c["ve"])
                inc(v.bn_aggr(out=st2[:], in_=st6[:]), "ve")
            ve_s4 = c["ve"]
            @block.scalar
            def _(s, ve_s4=ve_s4):
                s.wait_ge(s_ve, ve_s4)
                inc(s.activation(out=rs_sb[:, 3:4], in_=st2[:, 1:2], func=AF.Sqrt,
                                 bias=cst[:, 0:1]), "sc")
            sc_r4 = c["sc"]
            @block.vector
            def _(v, ht=ht, sc_r4=sc_r4):
                v.wait_ge(s_sc, sc_r4)
                inc(v.reciprocal(out=rs_sb[:, 3:4], in_=rs_sb[:, 3:4]), "ve")
                _ = v.wait_ge(s_ve, c["ve"])
                inc(v.tensor_scalar(out=fstage[:, 0:D], in0=fo_tok[:, ht, :],
                                    scalar1=st2[:, 0:1], scalar2=rs_sb[:, 3:4],
                                    op0=ALU.subtract, op1=ALU.mult), "ve")
                inc(v.tensor_tensor(out=fstage[:, D:2 * D], in0=fstage[:, 0:D],
                                    in1=xs1c[:, ht, :], op=ALU.add), "ve")
            ve_fin = c["ve"]
            @block.sync
            def _(sync, h=h, ht=ht, ve_fin=ve_fin):
                sync.wait_ge(s_ve, ve_fin)
                inc(sync.dma_start(out=y_out[h][ht * 128:(ht + 1) * 128, :],
                                   in_=fstage[:, D:2 * D]), "st")

    st_total = c["st"]
    @block.sync
    def _(sync):
        sync.wait_ge(s_st, st_total)

    ctxF.close()
    ctx.close()
    return nc


_NC_CACHE = None


def kernel(**inputs):
    global _NC_CACHE
    f32 = np.float32
    x = np.ascontiguousarray(inputs["L0_attn_x"], dtype=f32)
    W_in = np.asarray(inputs["L0_attn_W_in"], f32)
    ln_g = np.asarray(inputs["L0_attn_ln_g"], f32)
    ln_b = np.asarray(inputs["L0_attn_ln_b"], f32)
    conv_w = np.asarray(inputs["L0_attn_conv_w"], f32)
    conv_b = np.asarray(inputs["L0_attn_conv_b"], f32)
    W_x = np.asarray(inputs["L0_attn_W_x"], f32)
    W_dt = np.asarray(inputs["L0_attn_W_dt"], f32)
    b_dt = np.asarray(inputs["L0_attn_b_dt"], f32)
    A_log = np.asarray(inputs["L0_attn_A_log"], f32)
    Dp = np.asarray(inputs["L0_attn_Dp"], f32)
    W_out = np.asarray(inputs["L0_attn_W_out"], f32)
    W_fc = np.asarray(inputs["L0_ffn_W_fc"], f32)
    lnw = np.asarray(inputs["L0_ffn_ln_w"], f32)
    lnb = np.asarray(inputs["L0_ffn_ln_b"], f32)
    W_proj = np.asarray(inputs["L0_ffn_W_proj"], f32)

    CC = np.ascontiguousarray
    def dlay(vec, nk):
        return CC(vec.reshape(nk, 128).T)

    in_maps = []
    for core in range(8):
        p, q = core // 2, core % 2
        ch = slice(q * EH, (q + 1) * EH)
        w_inT_slice = CC(np.concatenate([W_in[ch], W_in[DIN + q * EH: DIN + (q + 1) * EH]],
                                        axis=0).T)
        m = {
            "x": CC(x[p]),
            "w_inT": w_inT_slice,
            "ln_g": dlay(ln_g, KC),
            "ln_b": dlay(ln_b, KC),
            "conv_w": CC(conv_w[ch, 0, :].reshape(DC, 128, 4).transpose(1, 0, 2).reshape(128, DC * 4)),
            "conv_b": dlay(conv_b[ch], DC),
            "w_xT": CC(W_x[:, ch].T),
            "w_dtT": CC(W_dt[ch].T),
            "b_dt": dlay(b_dt[ch], DC),
            "a_log": CC(A_log[ch].reshape(DC, 128, NST).transpose(1, 0, 2).reshape(128, DC * NST)),
            "dp": dlay(Dp[ch], DC),
            "w_outT": CC(W_out.T),
            "w_fcT": CC(W_fc.T),
            "lnw_f": dlay(lnw, HKC),
            "lnb_f": dlay(lnb, HKC),
            "w_projT": CC(W_proj.T),
            "ident": np.eye(128, dtype=f32),
            "x_own": CC(x[p].reshape(NCH, 2, HT, D)[:, q]),
        }
        in_maps.append(m)

    if _NC_CACHE is None:
        _NC_CACHE = build_nc()
    nc = _NC_CACHE
    res = run_bass_kernel_spmd(nc, in_maps, core_ids=list(range(8)))

    out = np.empty((4, TSEQ, D), dtype=f32)
    for core in range(8):
        p, q = core // 2, core % 2
        yc = res.results[core]["y"]
        for k in range(NCH):
            t0 = k * TT + q * HT
            out[p, t0:t0 + HT, :] = yc[k]
    if DEBUG_TAPS:
        kernel.LAST_TAPS = [dict(r) for r in res.results]
    return out


# version 08
# speedup vs baseline: 1.2954x; 1.1836x over previous
import numpy as np
from contextlib import ExitStack

import concourse.bass as bass
import concourse.mybir as mybir
from concourse.bass_utils import run_bass_kernel_spmd

F32 = mybir.dt.float32
BF16 = mybir.dt.bfloat16
ALU = mybir.AluOpType
AF = mybir.ActivationFunctionType
AX = mybir.AxisListType

D = 1024
DIN = 2048
EH = 1024
NST = 16
RNK = 64
TSEQ = 4096
TT = 512
HT = 256
NCH = TSEQ // TT
KC = D // 128
DC = EH // 128
MCXZ = 2 * EH // 128
NKO = DIN // 128
HF = 4096
HKC = HF // 128
DEBUG_TAPS = False
EPS1 = 1e-5
EPS2 = 1e-6
EPSF = 1e-5
AMEPS = 1e-6
RSQ2 = 0.7071067811865476
PAIRS = [[0, 1], [2, 3], [4, 5], [6, 7]]


def build_nc():
    nc = bass.Bass("TRN2", target_bir_lowering=False)

    def ein(name, shape, dtype=F32):
        return nc.dram_tensor(name, list(shape), dtype, kind="ExternalInput").ap()

    x_in = ein("x", (TSEQ, D))
    w_inT = ein("w_inT", (D, 2 * EH))
    ln_g = ein("ln_g", (128, KC))
    ln_b = ein("ln_b", (128, KC))
    conv_w = ein("conv_w", (128, DC * 4))
    conv_b = ein("conv_b", (128, DC))
    w_xT = ein("w_xT", (EH, 96))
    w_dtT = ein("w_dtT", (RNK, EH))
    b_dt = ein("b_dt", (128, DC))
    a_log = ein("a_log", (128, DC * NST))
    dp_in = ein("dp", (128, DC))
    w_outT = ein("w_outT", (DIN, D))
    w_fcT = ein("w_fcT", (D, HF))
    lnw_f = ein("lnw_f", (128, HKC))
    lnb_f = ein("lnb_f", (128, HKC))
    w_projT = ein("w_projT", (HF, D))
    ident = ein("ident", (128, 128))
    x_own = ein("x_own", (NCH, HT, D))

    y_out = nc.dram_tensor("y", [NCH, HT, D], F32, kind="ExternalOutput").ap()
    taps = {}
    if DEBUG_TAPS:
        for nm, shape, dt_ in (("t_hf", (128, KC * TT // 2), F32),
                               ("t_xi", (128, DC, TT + 4), F32),
                               ("t_sz", (128, DC, TT), F32),
                               ("t_u", (128, DC, TT), F32),
                               ("t_dt", (128, DC, TT), F32),
                               ("t_xdbl", (96, TT), F32),
                               ("t_yacc", (128, DC, TT), F32),
                               ("t_y2h", (128, DC * TT // 2), F32),
                               ("t_om", (128, 2, HT), F32),
                               ("t_epim", (128, 2, D), F32),
                               ("t_xs1", (128, 2, D), F32),
                               ("t_thr", (128, 4), F32),
                               ("t_wfc", (128, HF // 2), F32),
                               ("t_wpj", (128, HKC * D // 2), F32),
                               ("t_nswbo", (128, 16), F32),
                               ("t_xlnf", (128, KC * HT // 2), F32),
                               ("t_gbf", (128, HKC * HT // 2), F32),
                               ("t_rows", (1, 2 * HT), F32),
                               ("t_fotok", (128, 2, D), F32)):
            taps[nm] = nc.dram_tensor(nm, list(shape), dt_, kind="ExternalOutput").ap()

    e1_in = nc.dram_tensor("e1_in", [NCH, 96, TT], F32, kind="Internal").ap()
    e1_out = nc.dram_tensor("e1_out", [NCH, 96, TT], F32, kind="Internal").ap()
    e2_in = nc.dram_tensor("e2_in", [NCH, EH, TT], BF16, kind="Internal").ap()
    e2_out = nc.dram_tensor("e2_out", [NCH, DIN, TT], BF16, kind="Internal").ap()
    wout_bf = nc.dram_tensor("wout_bf", [NKO, 128, D], BF16, kind="Internal").ap()
    xs1_sp = nc.dram_tensor("xs1_sp", [NCH, HT, D], F32, kind="Internal").ap()
    row_dram = nc.dram_tensor("row_dram", [NCH, 2, HT], F32, kind="Internal").ap()
    thr_dram = nc.dram_tensor("thr_dram", [1, 4], F32, kind="Internal").ap()

    ctx = ExitStack()
    ctxM = ExitStack()
    sb = lambda st, name, shape, dtype=F32: st.enter_context(
        nc.sbuf_tensor(name, list(shape), dtype))
    psum = lambda name, shape: ctx.enter_context(
        nc.psum_tensor(name, list(shape), F32))
    sem = lambda name: ctx.enter_context(nc.semaphore(name))

    a_sb = sb(ctx, "a_sb", (128, DC * NST))
    g_sb = sb(ctx, "g_sb", (128, KC))
    lnb_sb = sb(ctx, "lnb_sb", (128, KC))
    convw_sb = sb(ctx, "convw_sb", (128, DC * 4))
    convb_sb = sb(ctx, "convb_sb", (128, DC))
    bdt_sb = sb(ctx, "bdt_sb", (128, DC))
    dp_sb = sb(ctx, "dp_sb", (128, DC))
    winb_sb = sb(ctx, "winb_sb", (128, MCXZ))
    id_f = sb(ctx, "id_f", (128, 128))
    id_b = sb(ctx, "id_b", (128, 128), BF16)
    ones_f = sb(ctx, "ones_f", (128, 1))
    ones_b = sb(ctx, "ones_b", (128, 1), BF16)
    thr_sb = sb(ctx, "thr_sb", (128, 4))
    lnwf_sb = sb(ctx, "lnwf_sb", (128, HKC))
    lnbf_sb = sb(ctx, "lnbf_sb", (128, HKC))
    lnwf_b = sb(ctx, "lnwf_b", (128, HKC), BF16)
    lnbf_b = sb(ctx, "lnbf_b", (128, HKC), BF16)
    gs_sb = sb(ctx, "gs_sb", (128, 2))
    tiny = sb(ctx, "tiny", (1, 8))
    st6 = sb(ctx, "st6", (128, 12))
    st2 = sb(ctx, "st2", (128, 2))
    rs_sb = sb(ctx, "rs_sb", (128, 4))
    cst = sb(ctx, "cst", (128, 6))

    w_in_sb = sb(ctxM, "w_in_sb", (128, KC, 2 * EH), BF16)
    w_x_sb = sb(ctxM, "w_x_sb", (128, KC * 96), BF16)
    w_dt_sb = sb(ctxM, "w_dt_sb", (RNK, EH), BF16)
    big = sb(ctxM, "big", (128, 2 * D))
    xtile = sb(ctxM, "xtile", (128, D))
    h_tok = sb(ctxM, "h_tok", (128, D), BF16)
    h_f = sb(ctxM, "h_f", (128, KC, TT), BF16)
    xi = sb(ctxM, "xi", (128, DC, TT + 4))
    sz = sb(ctxM, "sz", (128, DC, TT))
    u_sb = sb(ctxM, "u_sb", (128, DC, TT))
    ubf = sb(ctxM, "ubf", (128, TT), BF16)
    dt_sb = sb(ctxM, "dt_sb", (128, DC, TT))
    dtu_sb = sb(ctxM, "dtu_sb", (128, DC, TT))
    dtin_bf = sb(ctxM, "dtin_bf", (RNK, TT), BF16)
    sp1 = sb(ctxM, "sp1", (128, TT))
    sp2 = sb(ctxM, "sp2", (128, TT))
    sp3 = sb(ctxM, "sp3", (128, TT))
    bbc = sb(ctxM, "bbc", (128, 2, TT))
    cbc = sb(ctxM, "cbc", (128, 2, TT))
    da2 = sb(ctxM, "da2", (128, 2, 2, TT + 1))
    db2 = sb(ctxM, "db2", (128, 2, 2, TT + 1))
    yp_scr = sb(ctxM, "yp_scr", (128, TT))
    yacc = sb(ctxM, "yacc", (128, DC, TT))
    h_carry = sb(ctxM, "h_carry", (128, NST, DC))
    y2h = sb(ctxM, "y2h", (128, DC, TT), BF16)
    wo_st = sb(ctxM, "wo_st", (128, 2, D), BF16)
    epim = big[:].rearrange("p (h d) -> p h d", h=2)

    pm = [psum(f"pm{i}", (128, TT)) for i in range(8)]

    s_tp = sem("s_tp")
    s_ld = sem("s_ld"); s_gl = sem("s_gl"); s_mm = sem("s_mm")
    s_sc = sem("s_sc"); s_ve = sem("s_ve"); s_cc = sem("s_cc"); s_st = sem("s_st")
    s_gp = sem("s_gp")
    s_x = sem("s_x"); s_xd = sem("s_xd"); s_bb = sem("s_bb"); s_cb = sem("s_cb")
    s_wo = sem("s_wo"); s_y2 = sem("s_y2"); s_wi = sem("s_wi"); s_pc = sem("s_pc")
    s_rw = sem("s_rw")
    S = {"ld": s_ld, "gl": s_gl, "mm": s_mm, "sc": s_sc, "ve": s_ve,
         "cc": s_cc, "st": s_st, "x": s_x, "xd": s_xd, "bb": s_bb, "cb": s_cb,
         "wo": s_wo, "y2": s_y2, "wi": s_wi, "pc": s_pc, "rw": s_rw, "gp": s_gp}
    c = {k: 0 for k in S}

    block = ctx.enter_context(nc.Block())

    DMA_KEYS = ("ld", "gl", "st", "x", "xd", "bb", "cb", "wo", "y2", "wi",
                "pc", "rw")
    def inc(instr, key, n=None):
        if n is None:
            n = 16 if key in DMA_KEYS else 1
        instr.then_inc(S[key], n)
        c[key] += n
        return c[key]

    tapcnt = [0]
    def tap(name, src_ap, gates=()):
        if not DEBUG_TAPS:
            return
        @block.scalar
        def _(s, name=name, src_ap=src_ap, gates=tuple(gates)):
            for key, val in gates:
                s.wait_ge(S[key], val)
            s.dma_start(out=taps[name], in_=src_ap).then_inc(s_tp, 16)
            tapcnt[0] += 16
            s.wait_ge(s_tp, tapcnt[0])



    @block.sync
    def _(sync):
        for dst, srcap in ((id_f, ident), (g_sb, ln_g), (lnb_sb, ln_b),
                           (convw_sb, conv_w), (convb_sb, conv_b),
                           (bdt_sb, b_dt), (dp_sb, dp_in),
                           (lnwf_sb, lnw_f), (lnbf_sb, lnb_f)):
            inc(sync.dma_start(out=dst[:], in_=srcap), "pc")
        inc(sync.dma_start(out=sp1[0:RNK, 0:TT], in_=w_dtT[:, 0:TT]), "pc")
        inc(sync.dma_start(out=sp2[0:RNK, 0:TT], in_=w_dtT[:, TT:EH]), "pc")
        inc(sync.dma_start(out=big[:, 0:KC * 96].rearrange("p (k c) -> p k c", k=KC),
                           in_=w_xT.rearrange("(k p) c -> p k c", p=128)), "pc")
        inc(sync.dma_start(out=big[:, 1024:1024 + DC * NST], in_=a_log), "pc")
    ld_consts = c["pc"]

    @block.vector
    def _(v):
        v.wait_ge(s_pc, ld_consts)
        inc(v.memset(ones_f[:], 1.0), "ve")
        inc(v.memset(cst[:, 0:1], EPS1), "ve")
        inc(v.memset(cst[:, 1:2], EPS2), "ve")
        inc(v.memset(cst[:, 2:3], 1.0), "ve")
        inc(v.memset(cst[:, 3:4], -1.0), "ve")
        inc(v.memset(cst[:, 4:5], RSQ2), "ve")
        inc(v.memset(ones_b[:], 1.0), "ve")
        inc(v.memset(h_carry[:], 0.0), "ve")
        inc(v.memset(da2[:, :, :, 0:1].opt(), 0.0), "ve")
        inc(v.memset(xi[:, :, 0:4].opt(), 0.0), "ve")
        inc(v.tensor_copy(out=id_b[:], in_=id_f[:]), "ve")
        inc(v.tensor_copy(out=w_dt_sb[:, 0:TT], in_=sp1[0:RNK, 0:TT]), "ve")
        inc(v.tensor_copy(out=w_dt_sb[:, TT:EH], in_=sp2[0:RNK, 0:TT]), "ve")
        inc(v.tensor_copy(out=w_x_sb[:], in_=big[:, 0:KC * 96]), "ve")
        inc(v.tensor_copy(out=lnwf_b[:], in_=lnwf_sb[:]), "ve")
        inc(v.tensor_copy(out=lnbf_b[:], in_=lnbf_sb[:]), "ve")
    ve_consts = c["ve"]

    @block.scalar
    def _(s):
        s.wait_ge(s_pc, ld_consts)
        inc(s.activation(out=a_sb[:], in_=big[:, 1024:1024 + DC * NST], func=AF.Exp), "sc")
    sc_aexp = c["sc"]
    @block.vector
    def _(v):
        v.wait_ge(s_sc, sc_aexp)
        inc(v.tensor_scalar(out=a_sb[:], in0=a_sb[:], scalar1=-1.0, scalar2=None,
                            op0=ALU.mult), "ve")
    ve_aneg = c["ve"]

    win_cast_at = {}
    for kc in range(KC):
        buf = big[:, 0:2048]
        gate = win_cast_at.get(kc - 1)
        @block.sync
        def _(sync, buf=buf, kc=kc, gate=gate):
            if kc == 0:
                sync.wait_ge(s_ve, ve_consts)
                sync.wait_ge(s_sc, sc_aexp)
            if gate is not None:
                sync.wait_ge(s_sc, gate)
            inc(sync.dma_start(out=buf, in_=w_inT[kc * 128:(kc + 1) * 128, :]), "wi")
        ld_here = c["wi"]
        @block.tensor
        def _(t, kc=kc, buf=buf, ld_here=ld_here, ve_gate=c["ve"]):
            t.wait_ge(s_wi, ld_here)
            if kc > 0:
                t.wait_ge(s_ve, ve_gate)
            for mc in range(MCXZ):
                inc(t.matmul(pm[6][:, mc:mc + 1], buf[:, mc * 128:(mc + 1) * 128],
                             lnb_sb[:, kc:kc + 1], start=True, stop=True), "mm")
        mm_here = c["mm"]
        @block.vector
        def _(v, kc=kc, mm_here=mm_here):
            v.wait_ge(s_mm, mm_here)
            if kc == 0:
                inc(v.tensor_copy(out=winb_sb[:], in_=pm[6][:, 0:MCXZ]), "ve")
            else:
                inc(v.tensor_tensor(out=winb_sb[:], in0=winb_sb[:],
                                    in1=pm[6][:, 0:MCXZ], op=ALU.add), "ve")
            v.wait_ge(s_ve, c["ve"])
        @block.scalar
        def _(s, kc=kc, buf=buf, mm_here=mm_here):
            s.wait_ge(s_mm, mm_here)
            inc(s.activation(out=w_in_sb[:, kc, :], in_=buf, func=AF.Copy,
                             scale=g_sb[:, kc:kc + 1]), "sc")
        win_cast_at[kc] = c["sc"]
    sc_winb = c["sc"]

    wout_st_at = {}
    for kc in range(NKO):
        buf = big[:, (kc % 2) * 1024:(kc % 2) * 1024 + 1024]
        gate = wout_st_at.get(kc - 2)
        @block.gpsimd
        def _(g, kc=kc, buf=buf, gate=gate):
            if kc == 0:
                g.wait_ge(s_sc, win_cast_at[KC - 1])
            if gate is not None:
                g.wait_ge(s_st, gate)
            inc(g.dma_start(out=buf, in_=w_outT[kc * 128:(kc + 1) * 128, :]), "gl")
        gl_here = c["gl"]
        @block.vector
        def _(v, kc=kc, buf=buf, gl_here=gl_here, ve_gate=c["ve"]):
            v.wait_ge(s_gl, gl_here)
            inc(v.tensor_copy(out=wo_st[:, kc % 2, :], in_=buf), "ve")
        ve_here = c["ve"]
        @block.sync
        def _(sync, kc=kc, ve_here=ve_here):
            sync.wait_ge(s_ve, ve_here)
            inc(sync.dma_start(out=wout_bf[kc], in_=wo_st[:, kc % 2, :]), "st")
        wout_st_at[kc] = c["st"]

    @block.vector
    def _(v):
        inc(v.memset(gs_sb[:], 0.0), "ve")
    gamma_prev = c["ve"]
    gl_gamma_last = c["gl"]
    for wi, (wT, nk, cols) in enumerate(((w_fcT, KC, HF), (w_projT, HKC, D))):
        nh = 2 if cols > 2048 else 1
        hw = cols // nh
        for kc in range(nk):
            for hh in range(nh):
                @block.gpsimd
                def _(g, wT=wT, kc=kc, hh=hh, hw=hw, gate=gamma_prev):
                    g.wait_ge(s_ve, gate)
                    inc(g.dma_start(out=big[:, 0:hw],
                                    in_=wT[kc * 128:(kc + 1) * 128,
                                           hh * hw:(hh + 1) * hw]), "gl")
                gl_here = c["gl"]
                @block.vector
                def _(v, wi=wi, hw=hw, gl_here=gl_here):
                    v.wait_ge(s_gl, gl_here)
                    inc(v.tensor_reduce(out=st2[:, 0:1], in_=big[:, 0:hw], axis=AX.X,
                                        op=ALU.add, apply_absolute_value=True), "ve")
                    v.wait_ge(s_ve, c["ve"])
                    inc(v.tensor_tensor(out=gs_sb[:, wi:wi + 1], in0=gs_sb[:, wi:wi + 1],
                                        in1=st2[:, 0:1], op=ALU.add), "ve")
                gamma_prev = c["ve"]
    gl_gamma_last = c["gl"]
    ve_gamma = c["ve"]
    @block.tensor
    def _(t):
        t.wait_ge(s_ve, ve_gamma)
        inc(t.matmul(pm[7][0:1, 0:1], gs_sb[:, 0:1], ones_f[:], start=True, stop=True), "mm")
        inc(t.matmul(pm[7][0:1, 1:2], gs_sb[:, 1:2], ones_f[:], start=True, stop=True), "mm")
    mm_gam = c["mm"]
    @block.vector
    def _(v):
        v.wait_ge(s_mm, mm_gam)
        inc(v.tensor_scalar(out=tiny[0:1, 0:1], in0=pm[7][0:1, 0:1],
                            scalar1=0.5 / (HF * D), scalar2=0.5 * AMEPS,
                            op0=ALU.mult, op1=ALU.add), "ve")
        v.wait_ge(s_ve, c["ve"])
        inc(v.tensor_scalar(out=tiny[0:1, 1:2], in0=tiny[0:1, 0:1],
                            scalar1=-1.0, scalar2=None, op0=ALU.mult), "ve")
        inc(v.tensor_scalar(out=tiny[0:1, 2:3], in0=pm[7][0:1, 1:2],
                            scalar1=0.5 / (HF * D), scalar2=0.5 * AMEPS,
                            op0=ALU.mult, op1=ALU.add), "ve")
        v.wait_ge(s_ve, c["ve"])
        inc(v.tensor_scalar(out=tiny[0:1, 3:4], in0=tiny[0:1, 2:3],
                            scalar1=-1.0, scalar2=None, op0=ALU.mult), "ve")
    ve_thr = c["ve"]
    @block.sync
    def _(sync):
        sync.wait_ge(s_ve, ve_thr)
        inc(sync.dma_start(out=thr_dram, in_=tiny[0:1, 0:4]), "st")
    st_thr = c["st"]
    @block.sync
    def _(sync):
        sync.wait_ge(s_st, st_thr)
        inc(sync.dma_start(out=thr_sb[:], in_=thr_dram.to_broadcast([128, 4])), "pc")
    ld_thr = c["pc"]

    st_xs1_prev = c["st"]
    for k in range(NCH):
        for tt in range(4):
            @block.sync
            def _(sync, k=k, tt=tt, gate_ve=c["ve"]):
                sync.wait_ge(s_ve, gate_ve)
                inc(sync.dma_start(out=xtile[:],
                                   in_=x_in[k * TT + tt * 128: k * TT + (tt + 1) * 128, :]), "x")
            ld_x = c["x"]
            mtile = big[:, (tt % 2) * D:(tt % 2 + 1) * D]
            @block.vector
            def _(v, ld_x=ld_x, k=k, tt=tt, gl_gate=gl_gamma_last,
                  st_xs1_prev=st_xs1_prev):
                v.wait_ge(s_x, ld_x)
                if tt == 0:
                    v.wait_ge(s_st, st_xs1_prev)
                if k == 0 and tt == 0:
                    v.wait_ge(s_ve, ve_aneg)
                    v.wait_ge(s_gl, gl_gate)
                inc(v.bn_stats(out=st6[:, 0:6], in_=xtile[:, 0:512]), "ve")
                inc(v.bn_stats(out=st6[:, 6:12], in_=xtile[:, 512:1024]), "ve")
                v.wait_ge(s_ve, c["ve"])
                inc(v.bn_aggr(out=st2[:], in_=st6[:]), "ve")
            ve_stat1 = c["ve"]
            @block.scalar
            def _(s, ve_stat1=ve_stat1):
                s.wait_ge(s_ve, ve_stat1)
                inc(s.activation(out=rs_sb[:, 0:1], in_=st2[:, 1:2], func=AF.Sqrt,
                                 bias=cst[:, 0:1]), "sc")
            sc_r1 = c["sc"]
            @block.vector
            def _(v, mtile=mtile, sc_r1=sc_r1):
                v.wait_ge(s_sc, sc_r1)
                inc(v.reciprocal(out=rs_sb[:, 0:1], in_=rs_sb[:, 0:1]), "ve")
                _ = v.wait_ge(s_ve, c["ve"])
                inc(v.tensor_scalar(out=mtile, in0=xtile[:], scalar1=st2[:, 0:1],
                                    scalar2=rs_sb[:, 0:1], op0=ALU.subtract,
                                    op1=ALU.mult), "ve")
                inc(v.bn_stats(out=st6[:, 0:6], in_=mtile[:, 0:512]), "ve")
                inc(v.bn_stats(out=st6[:, 6:12], in_=mtile[:, 512:1024]), "ve")
                v.wait_ge(s_ve, c["ve"])
                inc(v.bn_aggr(out=st2[:], in_=st6[:]), "ve")
            ve_stat2 = c["ve"]
            @block.scalar
            def _(s, ve_stat2=ve_stat2):
                s.wait_ge(s_ve, ve_stat2)
                inc(s.activation(out=rs_sb[:, 1:2], in_=st2[:, 1:2], func=AF.Sqrt,
                                 bias=cst[:, 1:2]), "sc")
            sc_r2 = c["sc"]
            @block.vector
            def _(v, mtile=mtile, sc_r2=sc_r2):
                v.wait_ge(s_sc, sc_r2)
                inc(v.reciprocal(out=rs_sb[:, 1:2], in_=rs_sb[:, 1:2]), "ve")
                _ = v.wait_ge(s_ve, c["ve"])
                inc(v.tensor_scalar(out=h_tok[:], in0=mtile, scalar1=st2[:, 0:1],
                                    scalar2=rs_sb[:, 1:2], op0=ALU.subtract,
                                    op1=ALU.mult), "ve")
            ve_h = c["ve"]
            @block.tensor
            def _(t, tt=tt, ve_h=ve_h, sc_gate=c["sc"], ve_gate=c["ve"]):
                t.wait_ge(s_ve, max(ve_h, ve_gate))
                t.wait_ge(s_sc, sc_gate)
                pmb = pm[4][:].bitcast(BF16)
                for kc in range(KC):
                    inc(t.matmul(pmb[:, kc * 128:(kc + 1) * 128],
                                 h_tok[:, kc * 128:(kc + 1) * 128], id_b[:],
                                 start=True, stop=True, is_transpose=True), "mm")
            mm_tr = c["mm"]
            @block.scalar
            def _(s, tt=tt, mm_tr=mm_tr):
                s.wait_ge(s_mm, mm_tr)
                inc(s.activation(out=h_f[:, :, tt * 128:(tt + 1) * 128],
                                 in_=pm[4][:].bitcast(BF16).rearrange(
                                     "p (kc t) -> p kc t", kc=KC),
                                 func=AF.Copy), "sc")
        sc_hf = c["sc"]
        if k == 0:
            tap("t_hf", h_f[:].rearrange("p a b -> p (a b)").bitcast(F32),
                gates=[("sc", c["sc"])])

        sc_win0 = c["sc"]
        ve_halo = c["ve"]
        mm_win_at = []
        for mc in range(MCXZ):
            bank = pm[mc % 4]
            @block.tensor
            def _(t, mc=mc, bank=bank, sc_hf=sc_hf, ve_gate=ve_halo,
                  sc_win0=sc_win0):
                if mc == 0:
                    t.wait_ge(s_sc, sc_hf)
                    t.wait_ge(s_ve, ve_gate)
                if mc >= 4:
                    t.wait_ge(s_sc, sc_win0 + (mc - 4) + 1)
                for kc in range(KC):
                    inc(t.matmul(bank[:], w_in_sb[:, kc, mc * 128:(mc + 1) * 128],
                                 h_f[:, kc, :], start=(kc == 0), stop=(kc == KC - 1)),
                        "mm")
            mm_win_at.append(c["mm"])
        @block.scalar
        def _(s, mm_win_at=tuple(mm_win_at), ve_halo=ve_halo):
            s.wait_ge(s_ve, ve_halo)
            for mc in range(MCXZ):
                s.wait_ge(s_mm, mm_win_at[mc])
                bank = pm[mc % 4]
                if mc < DC:
                    inc(s.activation(out=xi[:, mc, 4:4 + TT], in_=bank[:],
                                     func=AF.Identity,
                                     bias=winb_sb[:, mc:mc + 1]), "sc")
                else:
                    inc(s.activation(out=sz[:, mc - DC, :], in_=bank[:], func=AF.Silu,
                                     bias=winb_sb[:, mc:mc + 1]), "sc")
        assert c["sc"] == sc_win0 + MCXZ
        sc_xz = c["sc"]
        if k == 0:
            tap("t_xi", xi[:], gates=[("sc", c["sc"])])
            tap("t_sz", sz[:], gates=[("sc", c["sc"])])

        ve_conv_at = []
        for dc in range(DC):
            @block.vector
            def _(v, dc=dc, sc_win0=sc_win0):
                v.wait_ge(s_sc, sc_win0 + dc + 1)
                inc(v.tensor_scalar(out=sp1[:], in0=xi[:, dc, 1:1 + TT],
                                    scalar1=convw_sb[:, dc * 4:dc * 4 + 1],
                                    scalar2=convb_sb[:, dc:dc + 1],
                                    op0=ALU.mult, op1=ALU.add), "ve")
                inc(v.scalar_tensor_tensor(out=sp2[:], in0=xi[:, dc, 2:2 + TT],
                                           scalar=convw_sb[:, dc * 4 + 1:dc * 4 + 2],
                                           in1=sp1[:], op0=ALU.mult, op1=ALU.add), "ve")
                inc(v.scalar_tensor_tensor(out=sp1[:], in0=xi[:, dc, 3:3 + TT],
                                           scalar=convw_sb[:, dc * 4 + 2:dc * 4 + 3],
                                           in1=sp2[:], op0=ALU.mult, op1=ALU.add), "ve")
                inc(v.scalar_tensor_tensor(out=dtu_sb[:, dc, :], in0=xi[:, dc, 4:4 + TT],
                                           scalar=convw_sb[:, dc * 4 + 3:dc * 4 + 4],
                                           in1=sp1[:], op0=ALU.mult, op1=ALU.add), "ve")
                inc(v.tensor_copy(out=xi[:, dc, 1:4], in_=xi[:, dc, 1 + TT:4 + TT]), "ve")
            ve_conv_at.append(c["ve"])
        sc_u0 = c["sc"]
        @block.scalar
        def _(s, ve_conv_at=tuple(ve_conv_at)):
            for dc in range(DC):
                s.wait_ge(s_ve, ve_conv_at[dc])
                inc(s.activation(out=u_sb[:, dc, :], in_=dtu_sb[:, dc, :],
                                 func=AF.Silu), "sc")
        assert c["sc"] == sc_u0 + DC

        for dc in range(DC):
            @block.vector
            def _(v, dc=dc, sc_u0=sc_u0, mm_gate=c["mm"]):
                v.wait_ge(s_sc, sc_u0 + dc + 1)
                if dc > 0:
                    v.wait_ge(s_mm, mm_gate)
                inc(v.tensor_copy(out=ubf[:], in_=u_sb[:, dc, :]), "ve")
            ve_ubf = c["ve"]
            @block.tensor
            def _(t, dc=dc, ve_ubf=ve_ubf, sc_gate=c["sc"]):
                t.wait_ge(s_ve, ve_ubf)
                if dc == 0:
                    t.wait_ge(s_sc, sc_gate)
                inc(t.matmul(pm[6][0:96, :], w_x_sb[:, dc * 96:(dc + 1) * 96],
                             ubf[:], start=(dc == 0), stop=(dc == DC - 1)),
                    "mm")
        mm_xp = c["mm"]
        @block.vector
        def _(v, mm_xp=mm_xp):
            v.wait_ge(s_mm, mm_xp)
            inc(v.tensor_copy(out=sp1[0:96, :], in_=pm[6][0:96, :]), "ve")
        ve_xp = c["ve"]
        @block.sync
        def _(sync, k=k, ve_xp=ve_xp):
            sync.wait_ge(s_ve, ve_xp)
            inc(sync.dma_start(out=e1_in[k], in_=sp1[0:96, :]), "st")
        st_e1 = c["st"]
        @block.gpsimd
        def _(g, k=k, st_e1=st_e1):
            g.wait_ge(s_st, st_e1)
            inc(g.collective_compute("AllReduce", ALU.add, replica_groups=PAIRS,
                                     ins=[e1_in[k]], outs=[e1_out[k]]), "cc")
        cc1 = c["cc"]
        @block.sync
        def _(sync, k=k, cc1=cc1):
            sync.wait_ge(s_cc, cc1)
            inc(sync.dma_start(out=sp1[0:RNK, :], in_=e1_out[k][0:RNK, :]), "xd")
        ld_xdbl = c["xd"]
        if k == 0:
            tap("t_xdbl", sp1[0:96, :], gates=[("xd", c["xd"])])

        @block.vector
        def _(v, ld_xdbl=ld_xdbl):
            v.wait_ge(s_xd, ld_xdbl)
            inc(v.tensor_copy(out=dtin_bf[:], in_=sp1[0:RNK, :]), "ve")
        ve_dtin = c["ve"]
        for dc in range(DC):
            bank = pm[dc % 2]
            @block.tensor
            def _(t, dc=dc, bank=bank, ve_dtin=ve_dtin, ve_gate=c["ve"], sc_gate=c["sc"]):
                t.wait_ge(s_ve, max(ve_dtin, ve_gate))
                t.wait_ge(s_sc, sc_gate)
                inc(t.matmul(bank[:], w_dt_sb[:, dc * 128:(dc + 1) * 128],
                             dtin_bf[:], start=True, stop=True), "mm")
            mm_dt = c["mm"]
            @block.vector
            def _(v, dc=dc, bank=bank, mm_dt=mm_dt):
                v.wait_ge(s_mm, mm_dt)
                inc(v.tensor_scalar(out=dt_sb[:, dc, :], in0=bank[:],
                                    scalar1=bdt_sb[:, dc:dc + 1],
                                    scalar2=0.0, op0=ALU.add, op1=ALU.max), "ve")
                inc(v.tensor_scalar(out=dtu_sb[:, dc, :], in0=bank[:],
                                    scalar1=bdt_sb[:, dc:dc + 1],
                                    scalar2=-1.0, op0=ALU.add, op1=ALU.mult), "ve")
                inc(v.scalar_tensor_tensor(out=dtu_sb[:, dc, :],
                                           in0=bank[:],
                                           scalar=bdt_sb[:, dc:dc + 1],
                                           in1=dtu_sb[:, dc, :],
                                           op0=ALU.add, op1=ALU.max), "ve")
        ve_av = c["ve"]
        @block.scalar
        def _(s, ve_av=ve_av):
            s.wait_ge(s_ve, ve_av)
            inc(s.activation(out=big[:, 0:2048],
                             in_=dtu_sb[:, 0:4, :].rearrange("p a b -> p (a b)"),
                             func=AF.Exp, scale=cst[:, 3:4]), "sc")
            inc(s.activation(out=sp2[:],
                             in_=dtu_sb[:, 4, :], func=AF.Exp, scale=cst[:, 3:4]), "sc")
            inc(s.activation(out=sp3[:],
                             in_=dtu_sb[:, 5, :], func=AF.Exp, scale=cst[:, 3:4]), "sc")
            inc(s.activation(out=sp1[:],
                             in_=dtu_sb[:, 6, :], func=AF.Exp, scale=cst[:, 3:4]), "sc")
            inc(s.activation(out=h_tok[:].bitcast(F32),
                             in_=dtu_sb[:, 7, :], func=AF.Exp, scale=cst[:, 3:4]), "sc")
            inc(s.activation(out=dtu_sb[:, 0:4, :].rearrange("p a b -> p (a b)"),
                             in_=big[:, 0:2048], func=AF.Ln, bias=cst[:, 2:3]), "sc")
            inc(s.activation(out=dtu_sb[:, 4, :], in_=sp2[:], func=AF.Ln,
                             bias=cst[:, 2:3]), "sc")
            inc(s.activation(out=dtu_sb[:, 5, :], in_=sp3[:], func=AF.Ln,
                             bias=cst[:, 2:3]), "sc")
            inc(s.activation(out=dtu_sb[:, 6, :], in_=sp1[:], func=AF.Ln,
                             bias=cst[:, 2:3]), "sc")
            inc(s.activation(out=dtu_sb[:, 7, :], in_=h_tok[:].bitcast(F32)[:, 0:TT],
                             func=AF.Ln, bias=cst[:, 2:3]), "sc")
        sc_sp = c["sc"]
        for dc in range(DC):
            @block.vector
            def _(v, dc=dc, sc_sp=sc_sp):
                if dc == 0:
                    v.wait_ge(s_sc, sc_sp)
                inc(v.tensor_tensor(out=dt_sb[:, dc, :], in0=dt_sb[:, dc, :],
                                    in1=dtu_sb[:, dc, :], op=ALU.add), "ve")
                inc(v.tensor_tensor(out=dtu_sb[:, dc, :], in0=dt_sb[:, dc, :],
                                    in1=u_sb[:, dc, :], op=ALU.mult), "ve")
        ve_dt = c["ve"]
        if k == 0:
            tap("t_dt", dt_sb[:], gates=[("ve", c["ve"])])
            tap("t_u", u_sb[:], gates=[("ve", c["ve"])])

        NPAIR = NST * (DC // 2)
        vb = c["ve"]; gb = c["gp"]; scb = c["sc"]
        bb_at = []; cb_at = []
        for n in range(NST):
            @block.sync
            def _(sync, k=k, n=n, vb=vb, gb=gb):
                if n >= 2:
                    jp = 4 * n - 5
                    sync.wait_ge(s_gp, gb + 2 * jp)
                    sync.wait_ge(s_ve, vb + 3 * jp + 3)
                inc(sync.dma_start(out=bbc[:, n % 2, :],
                                   in_=e1_out[k][64 + n:65 + n, :].to_broadcast([128, TT])), "bb")
                inc(sync.dma_start(out=cbc[:, n % 2, :],
                                   in_=e1_out[k][80 + n:81 + n, :].to_broadcast([128, TT])), "cb")
            bb_at.append(c["bb"]); cb_at.append(c["cb"])
        @block.scalar
        def _(s, vb=vb):
            s.wait_ge(s_ve, vb)
            for j in range(NPAIR):
                n, p_ = divmod(j, 4)
                d0 = 2 * p_; slot = j % 2
                if j >= 2:
                    s.wait_ge(s_ve, vb + 3 * (j - 2) + 1)
                inc(s.activation(out=da2[:, slot, 0, 1:1 + TT], in_=dt_sb[:, d0, :],
                                 func=AF.Exp,
                                 scale=a_sb[:, d0 * NST + n:d0 * NST + n + 1]), "sc")
                inc(s.activation(out=da2[:, slot, 1, 1:1 + TT], in_=dt_sb[:, d0 + 1, :],
                                 func=AF.Exp,
                                 scale=a_sb[:, (d0 + 1) * NST + n:(d0 + 1) * NST + n + 1]), "sc")
                inc(s.activation(out=db2[:, slot, :, 0:1],
                                 in_=h_carry[:, n, d0:d0 + 2].unsqueeze(2),
                                 func=AF.Copy), "sc")
        assert c["sc"] == scb + 3 * NPAIR
        @block.gpsimd
        def _(g, vb=vb, bb_at=tuple(bb_at)):
            def yacc_upd(jm):
                nm, pm_ = divmod(jm, 4)
                dm = 2 * pm_; sm = jm % 2
                g.wait_ge(s_ve, vb + 3 * jm + 3)
                if nm == 0:
                    inc(g.tensor_copy(out=yacc[:, dm:dm + 2, :],
                                      in_=db2[:, sm, :, 1:1 + TT]), "gp")
                else:
                    inc(g.tensor_tensor(out=yacc[:, dm:dm + 2, :],
                                        in0=yacc[:, dm:dm + 2, :],
                                        in1=db2[:, sm, :, 1:1 + TT],
                                        op=ALU.add), "gp")
            for j in range(NPAIR):
                n, p_ = divmod(j, 4)
                d0 = 2 * p_; slot = j % 2
                if p_ == 0:
                    g.wait_ge(s_bb, bb_at[n])
                if j == 0:
                    g.wait_ge(s_ve, vb)
                if j >= 2:
                    g.wait_ge(s_ve, vb + 3 * (j - 2) + 3)
                inc(g.tensor_tensor(
                    out=db2[:, slot, :, 1:1 + TT],
                    in0=dtu_sb[:, d0:d0 + 2, :],
                    in1=bbc[:, n % 2, :].unsqueeze(1).to_broadcast([128, 2, TT]),
                    op=ALU.mult), "gp")
                if j >= 1:
                    yacc_upd(j - 1)
            yacc_upd(NPAIR - 1)
        assert c["gp"] == gb + 2 * NPAIR
        @block.vector
        def _(v, vb=vb, gb=gb, scb=scb, cb_at=tuple(cb_at)):
            for j in range(NPAIR):
                n, p_ = divmod(j, 4)
                d0 = 2 * p_; slot = j % 2
                v.wait_ge(s_sc, scb + 3 * j + 3)
                v.wait_ge(s_gp, gb + max(1, 2 * j))
                if p_ == 0:
                    v.wait_ge(s_cb, cb_at[n])
                inc(v.tensor_tensor_scan(
                    out=db2[:, slot, :, :].rearrange("p a b -> p (a b)"),
                    data0=da2[:, slot, :, :].rearrange("p a b -> p (a b)"),
                    data1=db2[:, slot, :, :].rearrange("p a b -> p (a b)"),
                    initial=0.0, op0=ALU.mult, op1=ALU.add), "ve")
                inc(v.tensor_copy(out=h_carry[:, n, d0:d0 + 2],
                                  in_=db2[:, slot, :, TT:TT + 1].rearrange(
                                      "p a b -> p (a b)")), "ve")
                inc(v.tensor_tensor(
                    out=db2[:, slot, :, 1:1 + TT],
                    in0=db2[:, slot, :, 1:1 + TT],
                    in1=cbc[:, n % 2, :].unsqueeze(1).to_broadcast([128, 2, TT]),
                    op=ALU.mult), "ve")
        assert c["ve"] == vb + 3 * NPAIR

        if k == 0:
            tap("t_yacc", yacc[:], gates=[("ve", c["ve"])])
        gp_lat = c["gp"]
        for dc in range(DC):
            @block.vector
            def _(v, dc=dc, gp_lat=gp_lat):
                if dc == 0:
                    v.wait_ge(s_gp, gp_lat)
                inc(v.scalar_tensor_tensor(out=yp_scr[:], in0=u_sb[:, dc, :],
                                           scalar=dp_sb[:, dc:dc + 1],
                                           in1=yacc[:, dc, :], op0=ALU.mult,
                                           op1=ALU.add), "ve")
                inc(v.tensor_tensor(out=y2h[:, dc, :], in0=yp_scr[:],
                                    in1=sz[:, dc, :], op=ALU.mult), "ve")
        ve_y2 = c["ve"]
        if k == 0:
            tap("t_y2h", y2h[:].rearrange("p a b -> p (a b)").bitcast(F32),
                gates=[("ve", c["ve"])])
        @block.sync
        def _(sync, k=k, ve_y2=ve_y2):
            sync.wait_ge(s_ve, ve_y2)
            inc(sync.dma_start(out=e2_in[k].rearrange("(dc p) t -> p dc t", p=128),
                               in_=y2h[:]), "st")
        st_e2 = c["st"]
        @block.gpsimd
        def _(g, k=k, st_e2=st_e2):
            g.wait_ge(s_st, st_e2)
            inc(g.collective_compute("AllGather", ALU.bypass, replica_groups=PAIRS,
                                     ins=[e2_in[k]], outs=[e2_out[k]]), "cc")
        cc2 = c["cc"]

        y2full = h_f[:].rearrange("p a b -> p (a b)").rearrange(
            "p (kc t) -> p kc t", kc=NKO)
        qkey = "ld" if k % 2 == 0 else "gl"
        mm_before_wo = c["mm"]
        def emit_y2_load(eng, k=k, cc2=cc2, mm_before=mm_before_wo):
            eng.wait_ge(s_cc, cc2)
            eng.wait_ge(s_mm, mm_before)
            par = eng.snap(eng.partition_id() % 2, min_val=0, max_val=1)
            inc(eng.dma_start(
                out=y2full,
                in_=e2_out[k].rearrange("(kc p) t -> p kc t", p=128)[:, :, bass.ts(par, HT)]),
                "y2")
        if k % 2 == 0:
            block.sync(emit_y2_load)
        else:
            block.gpsimd(emit_y2_load)
        ld_y2 = c["y2"]
        for kc in range(NKO):
            @block.sync
            def _(sync, kc=kc, mm_gate=c["mm"]):
                if kc >= 2:
                    sync.wait_ge(s_mm, mm_gate)
                inc(sync.dma_start(out=wo_st[:, kc % 2, :], in_=wout_bf[kc]), "wo")
            ld_wo = c["wo"]
            @block.tensor
            def _(t, kc=kc, ld_wo=ld_wo, ld_y2=ld_y2, sc_gate=c["sc"], ve_gate=c["ve"], qkey=qkey):
                t.wait_ge(s_wo, ld_wo)
                if kc == 0:
                    t.wait_ge(s_y2, ld_y2)
                    t.wait_ge(s_sc, sc_gate)
                    t.wait_ge(s_ve, ve_gate)
                for mc in range(8):
                    inc(t.matmul(pm[mc][:, 0:HT],
                                 wo_st[:, kc % 2, mc * 128:(mc + 1) * 128],
                                 y2full[:, kc, :], start=(kc == 0),
                                 stop=(kc == NKO - 1)), "mm")
        mm_wo = c["mm"]

        for ht in range(2):
            @block.sync
            def _(sync, k=k, ht=ht, gate_ve=c["ve"]):
                sync.wait_ge(s_ve, gate_ve)
                inc(sync.dma_start(out=xtile[:],
                                   in_=x_own[k][ht * 128:(ht + 1) * 128, :]), "x")
            ld_xo = c["x"]
            @block.vector
            def _(v, ht=ht, ld_xo=ld_xo):
                v.wait_ge(s_x, ld_xo)
                inc(v.bn_stats(out=st6[:, 0:6], in_=xtile[:, 0:512]), "ve")
                inc(v.bn_stats(out=st6[:, 6:12], in_=xtile[:, 512:1024]), "ve")
                v.wait_ge(s_ve, c["ve"])
                inc(v.bn_aggr(out=st2[:], in_=st6[:]), "ve")
            ve_so = c["ve"]
            @block.scalar
            def _(s, ve_so=ve_so):
                s.wait_ge(s_ve, ve_so)
                inc(s.activation(out=rs_sb[:, 0:1], in_=st2[:, 1:2], func=AF.Sqrt,
                                 bias=cst[:, 0:1]), "sc")
            sc_ro = c["sc"]
            @block.vector
            def _(v, ht=ht, sc_ro=sc_ro):
                v.wait_ge(s_sc, sc_ro)
                inc(v.reciprocal(out=rs_sb[:, 0:1], in_=rs_sb[:, 0:1]), "ve")
                _ = v.wait_ge(s_ve, c["ve"])
                inc(v.tensor_scalar(out=epim[:, ht, :], in0=xtile[:],
                                    scalar1=st2[:, 0:1], scalar2=rs_sb[:, 0:1],
                                    op0=ALU.subtract, op1=ALU.mult), "ve")
        for mc in range(8):
            @block.scalar
            def _(s, mc=mc, mm_wo=mm_wo, ve_gate=c["ve"]):
                s.wait_ge(s_mm, mm_wo)
                inc(s.activation(out=sp3[:, (mc % 2) * HT:(mc % 2 + 1) * HT],
                                 in_=pm[mc][:, 0:HT], func=AF.Copy), "sc")
            sc_om = c["sc"]
            @block.tensor
            def _(t, mc=mc, sc_om=sc_om, ve_gate=c["ve"]):
                t.wait_ge(s_sc, sc_om)
                if mc >= 2:
                    t.wait_ge(s_ve, ve_gate)
                for ht in range(2):
                    inc(t.matmul(pm[mc][:, HT + ht * 128: HT + (ht + 1) * 128],
                                 sp3[:, (mc % 2) * HT + ht * 128:(mc % 2) * HT + (ht + 1) * 128],
                                 id_f[:], start=True, stop=True, is_transpose=True), "mm")
            mm_omt = c["mm"]
            @block.vector
            def _(v, mc=mc, mm_omt=mm_omt):
                v.wait_ge(s_mm, mm_omt)
                for ht in range(2):
                    sl = epim[:, ht, mc * 128:(mc + 1) * 128]
                    inc(v.tensor_tensor(out=sl, in0=sl,
                                        in1=pm[mc][:, HT + ht * 128: HT + (ht + 1) * 128],
                                        op=ALU.add), "ve")
        if k == 0:
            tap("t_epim", epim[:], gates=[("ve", c["ve"])])
        for ht in range(2):
            @block.sync
            def _(sync, k=k, ht=ht, ve_gate=c["ve"]):
                sync.wait_ge(s_ve, ve_gate)
                inc(sync.dma_start(out=xtile[:],
                                   in_=x_own[k][ht * 128:(ht + 1) * 128, :]), "x")
            ld_xr = c["x"]
            @block.vector
            def _(v, ht=ht):
                inc(v.bn_stats(out=st6[:, 0:6], in_=epim[:, ht, 0:512]), "ve")
                inc(v.bn_stats(out=st6[:, 6:12], in_=epim[:, ht, 512:1024]), "ve")
                v.wait_ge(s_ve, c["ve"])
                inc(v.bn_aggr(out=st2[:], in_=st6[:]), "ve")
            ve_st3 = c["ve"]
            @block.scalar
            def _(s, ve_st3=ve_st3):
                s.wait_ge(s_ve, ve_st3)
                inc(s.activation(out=rs_sb[:, 2:3], in_=st2[:, 1:2], func=AF.Sqrt,
                                 bias=cst[:, 0:1]), "sc")
            sc_r3 = c["sc"]
            @block.vector
            def _(v, ht=ht, sc_r3=sc_r3, ld_xr=ld_xr):
                v.wait_ge(s_sc, sc_r3)
                v.wait_ge(s_x, ld_xr)
                inc(v.reciprocal(out=rs_sb[:, 2:3], in_=rs_sb[:, 2:3]), "ve")
                _ = v.wait_ge(s_ve, c["ve"])
                inc(v.tensor_scalar(out=epim[:, ht, :], in0=epim[:, ht, :],
                                    scalar1=st2[:, 0:1],
                                    scalar2=rs_sb[:, 2:3], op0=ALU.subtract,
                                    op1=ALU.mult), "ve")
                inc(v.tensor_tensor(out=epim[:, ht, :], in0=epim[:, ht, :],
                                    in1=xtile[:], op=ALU.add), "ve")
            ve_xs1 = c["ve"]
            if k == 0:
                tap_xs1_ht = ht
                if DEBUG_TAPS:
                    @block.scalar
                    def _(s, ht=ht, gv=c["ve"]):
                        s.wait_ge(s_ve, gv)
                        s.dma_start(out=taps["t_xs1"][:, ht, :], in_=epim[:, ht, :]).then_inc(s_tp, 16)
                        tapcnt[0] += 16
                        s.wait_ge(s_tp, tapcnt[0])
            @block.sync
            def _(sync, k=k, ht=ht, ve_xs1=ve_xs1):
                sync.wait_ge(s_ve, ve_xs1)
                inc(sync.dma_start(out=xs1_sp[k][ht * 128:(ht + 1) * 128, :],
                                   in_=epim[:, ht, :]), "st")
        st_xs1_prev = c["st"]

    cM = dict(c)
    ctxM.close()

    ctxF = ExitStack()
    sbF = lambda name, shape, dtype=F32: ctxF.enter_context(
        nc.sbuf_tensor(name, list(shape), dtype))
    wfc_sb = sbF("wfc_sb", (128, KC, HF), BF16)
    wpj_sb = sbF("wpj_sb", (128, HKC, D), BF16)
    fstage = sbF("fstage", (128, 2048))
    qs1 = sbF("qs1", (128, 2048))
    qs2 = sbF("qs2", (128, 2048))
    wq_t = sbF("wq_t", (128, D), BF16)
    g_bf = sbF("g_bf", (128, HKC, HT), BF16)
    g2_scr = sbF("g2_scr", (128, HT), BF16)
    ebuf = sbF("ebuf", (128, HT))
    rowsc = sbF("rowsc", (1, 2 * HT))
    sa_bc = sbF("sa_bc", (128, HT))
    nb_bc = sbF("nb_bc", (128, HT))
    nsw_sb = sbF("nsw_sb", (128, 8))
    bo_sb = sbF("bo_sb", (128, 8))
    fo = sbF("fo", (128, 2, HT))
    fo_tok = sbF("fo_tok", (128, 2, D))
    xs1c = sbF("xs1c", (128, 2, D))
    xln_b = sbF("xln_b", (128, D), BF16)
    xlnf = sbF("xlnf", (128, KC, HT), BF16)

    ALLK = tuple(S.keys())
    @block.sync
    def _(sync):
        for key in ALLK:
            sync.wait_ge(S[key], cM[key])
    @block.gpsimd
    def _(g):
        for key in ALLK:
            g.wait_ge(S[key], cM[key])
    @block.tensor
    def _(t):
        for key in ALLK:
            t.wait_ge(S[key], cM[key])
    @block.scalar
    def _(s):
        for key in ALLK:
            s.wait_ge(S[key], cM[key])
    @block.vector
    def _(v):
        for key in ALLK:
            v.wait_ge(S[key], cM[key])

    for kc in range(KC):
        for hh in range(2):
            @block.gpsimd
            def _(g, kc=kc, hh=hh, ve_gate=c["ve"]):
                g.wait_ge(s_ve, ve_gate)
                inc(g.dma_start(out=fstage[:],
                                in_=w_fcT[kc * 128:(kc + 1) * 128,
                                          hh * 2048:(hh + 1) * 2048]), "gl")
            gl_q = c["gl"]
            @block.vector
            def _(v, kc=kc, hh=hh, gl_q=gl_q):
                v.wait_ge(s_gl, gl_q)
                inc(v.tensor_scalar(out=qs1[:], in0=fstage[:], scalar1=thr_sb[:, 0:1],
                                    scalar2=None, op0=ALU.is_ge), "ve")
                inc(v.tensor_scalar(out=qs2[:], in0=fstage[:], scalar1=thr_sb[:, 1:2],
                                    scalar2=None, op0=ALU.is_le), "ve")
                inc(v.tensor_tensor(out=wfc_sb[:, kc, hh * 2048:(hh + 1) * 2048],
                                    in0=qs1[:], in1=qs2[:], op=ALU.subtract), "ve")

    for kc2 in range(HKC):
        @block.gpsimd
        def _(g, kc2=kc2, ve_gate=c["ve"]):
            g.wait_ge(s_ve, ve_gate)
            inc(g.dma_start(out=fstage[:, 0:D],
                            in_=w_projT[kc2 * 128:(kc2 + 1) * 128, :]), "gl")
        gl_q = c["gl"]
        @block.vector
        def _(v, kc2=kc2, gl_q=gl_q, mm_gate=c["mm"]):
            v.wait_ge(s_gl, gl_q)
            if kc2 > 0:
                v.wait_ge(s_mm, mm_gate)
            inc(v.tensor_scalar(out=qs1[:, 0:D], in0=fstage[:, 0:D],
                                scalar1=thr_sb[:, 2:3], scalar2=None, op0=ALU.is_ge), "ve")
            inc(v.tensor_scalar(out=qs2[:, 0:D], in0=fstage[:, 0:D],
                                scalar1=thr_sb[:, 3:4], scalar2=None, op0=ALU.is_le), "ve")
            inc(v.tensor_tensor(out=wq_t[:], in0=qs1[:, 0:D], in1=qs2[:, 0:D],
                                op=ALU.subtract), "ve")
        ve_wq = c["ve"]
        @block.tensor
        def _(t, kc2=kc2, ve_wq=ve_wq):
            t.wait_ge(s_ve, ve_wq)
            for mc in range(8):
                inc(t.matmul(pm[6][:, mc:mc + 1], wq_t[:, mc * 128:(mc + 1) * 128],
                             lnbf_b[:, kc2:kc2 + 1], start=True, stop=True), "mm")
                inc(t.matmul(pm[7][:, mc:mc + 1], wq_t[:, mc * 128:(mc + 1) * 128],
                             lnwf_b[:, kc2:kc2 + 1], start=True, stop=True), "mm")
        mm_bo = c["mm"]
        @block.vector
        def _(v, kc2=kc2, mm_bo=mm_bo):
            v.wait_ge(s_mm, mm_bo)
            if kc2 == 0:
                inc(v.tensor_copy(out=bo_sb[:], in_=pm[6][:, 0:8]), "ve")
                inc(v.tensor_copy(out=nsw_sb[:], in_=pm[7][:, 0:8]), "ve")
            else:
                inc(v.tensor_tensor(out=bo_sb[:], in0=bo_sb[:],
                                    in1=pm[6][:, 0:8], op=ALU.add), "ve")
                inc(v.tensor_tensor(out=nsw_sb[:], in0=nsw_sb[:],
                                    in1=pm[7][:, 0:8], op=ALU.add), "ve")
            v.wait_ge(s_ve, c["ve"])
            inc(v.tensor_scalar(out=wpj_sb[:, kc2, :], in0=wq_t[:],
                                scalar1=lnwf_sb[:, kc2:kc2 + 1], scalar2=None,
                                op0=ALU.mult), "ve")
    @block.vector
    def _(v):
        inc(v.tensor_scalar(out=nsw_sb[:], in0=nsw_sb[:], scalar1=-1.0,
                            scalar2=None, op0=ALU.mult), "ve")
        v.wait_ge(s_ve, c["ve"])
    ve_bo = c["ve"]
    @block.scalar
    def _(s, ve_bo=ve_bo):
        s.wait_ge(s_ve, ve_bo)
    sc_bo = c["sc"]

    if DEBUG_TAPS:
        tap("t_thr", thr_sb[:], gates=[("pc", c["pc"])])
        tap("t_wfc", wfc_sb[:, 0, :].bitcast(F32), gates=[("ve", c["ve"])])
        tap("t_wpj", wpj_sb[:].rearrange("p a b -> p (a b)").bitcast(F32),
            gates=[("ve", c["ve"])])
        if DEBUG_TAPS:
            @block.scalar
            def _(s, gv=c["sc"]):
                s.wait_ge(s_sc, gv)
                s.dma_start(out=taps["t_nswbo"][:, 0:8], in_=nsw_sb[:]).then_inc(s_tp, 16)
                tapcnt[0] += 16
                s.wait_ge(s_tp, tapcnt[0])
        @block.scalar
        def _(s, gv=c["sc"]):
            s.wait_ge(s_sc, gv)
            s.dma_start(out=taps["t_nswbo"][:, 8:16], in_=bo_sb[:]).then_inc(s_tp, 16)
            tapcnt[0] += 16
            s.wait_ge(s_tp, tapcnt[0])

    for h in range(NCH):
        @block.sync
        def _(sync, h=h, ve_gate=c["ve"]):
            if h == 0:
                sync.wait_ge(s_sc, sc_bo)
            sync.wait_ge(s_ve, ve_gate)
            inc(sync.dma_start(out=xs1c[:],
                               in_=xs1_sp[h].rearrange("(a p) d -> p a d", p=128)), "x")
        ld_xs1 = c["x"]
        for ht in range(2):
            @block.vector
            def _(v, ht=ht, ld_xs1=ld_xs1):
                v.wait_ge(s_x, ld_xs1)
                inc(v.bn_stats(out=st6[:, 0:6], in_=xs1c[:, ht, 0:512]), "ve")
                inc(v.bn_stats(out=st6[:, 6:12], in_=xs1c[:, ht, 512:1024]), "ve")
                v.wait_ge(s_ve, c["ve"])
                inc(v.bn_aggr(out=st2[:], in_=st6[:]), "ve")
            ve_sf = c["ve"]
            @block.scalar
            def _(s, ve_sf=ve_sf):
                s.wait_ge(s_ve, ve_sf)
                inc(s.activation(out=rs_sb[:, 0:1], in_=st2[:, 1:2], func=AF.Sqrt,
                                 bias=cst[:, 0:1]), "sc")
            sc_rf = c["sc"]
            @block.vector
            def _(v, ht=ht, sc_rf=sc_rf):
                v.wait_ge(s_sc, sc_rf)
                inc(v.reciprocal(out=rs_sb[:, 0:1], in_=rs_sb[:, 0:1]), "ve")
                _ = v.wait_ge(s_ve, c["ve"])
                inc(v.tensor_scalar(out=xln_b[:], in0=xs1c[:, ht, :],
                                    scalar1=st2[:, 0:1], scalar2=rs_sb[:, 0:1],
                                    op0=ALU.subtract, op1=ALU.mult), "ve")
            ve_xln = c["ve"]
            @block.tensor
            def _(t, ht=ht, ve_xln=ve_xln, sc_gate=c["sc"]):
                t.wait_ge(s_ve, ve_xln)
                t.wait_ge(s_sc, sc_gate)
                pmb = pm[4][:].bitcast(BF16)
                for kc in range(KC):
                    inc(t.matmul(pmb[:, kc * 128:(kc + 1) * 128],
                                 xln_b[:, kc * 128:(kc + 1) * 128], id_b[:],
                                 start=True, stop=True, is_transpose=True), "mm")
            mm_trf = c["mm"]
            @block.scalar
            def _(s, ht=ht, mm_trf=mm_trf):
                s.wait_ge(s_mm, mm_trf)
                inc(s.activation(out=xlnf[:, :, ht * 128:(ht + 1) * 128],
                                 in_=pm[4][:].bitcast(BF16).rearrange(
                                     "p (kc t) -> p kc t", kc=KC),
                                 func=AF.Copy), "sc")
        sc_xlnf = c["sc"]
        if h == 0:
            tap("t_xlnf", xlnf[:].rearrange("p a b -> p (a b)").bitcast(F32),
                gates=[("sc", c["sc"])])

        for mc in range(HKC):
            bank = pm[mc % 4]
            @block.tensor
            def _(t, mc=mc, bank=bank, sc_xlnf=sc_xlnf, sc_gate=c["sc"], ve_gate=c["ve"]):
                t.wait_ge(s_sc, sc_gate)
                t.wait_ge(s_ve, ve_gate)
                for kc in range(KC):
                    inc(t.matmul(bank[:, 0:HT], wfc_sb[:, kc, mc * 128:(mc + 1) * 128],
                                 xlnf[:, kc, :], start=(kc == 0), stop=(kc == KC - 1)),
                        "mm")
            mm_fc = c["mm"]
            @block.scalar
            def _(s, mc=mc, bank=bank, mm_fc=mm_fc):
                s.wait_ge(s_mm, mm_fc)
                inc(s.activation(out=ebuf[:], in_=bank[:, 0:HT], func=AF.Erf,
                                 scale=cst[:, 4:5]), "sc")
            sc_erf = c["sc"]
            @block.vector
            def _(v, mc=mc, bank=bank, sc_erf=sc_erf):
                v.wait_ge(s_sc, sc_erf)
                inc(v.scalar_tensor_tensor(out=g_bf[:, mc, :], in0=ebuf[:],
                                           scalar=1.0, in1=bank[:, 0:HT],
                                           op0=ALU.add, op1=ALU.mult), "ve")
            @block.vector
            def _(v, mc=mc):
                inc(v.tensor_tensor(out=g2_scr[:], in0=g_bf[:, mc, :],
                                    in1=g_bf[:, mc, :], op=ALU.mult), "ve")
            ve_g = c["ve"]
            @block.tensor
            def _(t, mc=mc, ve_g=ve_g):
                t.wait_ge(s_ve, ve_g)
                inc(t.matmul(pm[6][0:1, 0:HT], ones_b[:], g_bf[:, mc, :],
                             start=(mc == 0), stop=(mc == HKC - 1)), "mm")
                inc(t.matmul(pm[7][0:1, 0:HT], ones_b[:], g2_scr[:],
                             start=(mc == 0), stop=(mc == HKC - 1)), "mm")
        mm_st = c["mm"]
        if h == 0:
            tap("t_gbf", g_bf[:].rearrange("p a b -> p (a b)").bitcast(F32),
                gates=[("ve", c["ve"]), ("sc", c["sc"])])
        @block.vector
        def _(v, mm_st=mm_st):
            v.wait_ge(s_mm, mm_st)
            inc(v.tensor_scalar(out=rowsc[0:1, 0:HT], in0=pm[6][0:1, 0:HT],
                                scalar1=0.5 / HF, scalar2=None, op0=ALU.mult), "ve")
            inc(v.tensor_scalar(out=rowsc[0:1, HT:2 * HT], in0=pm[7][0:1, 0:HT],
                                scalar1=0.25 / HF, scalar2=None, op0=ALU.mult), "ve")
            v.wait_ge(s_ve, c["ve"])
            inc(v.tensor_tensor(out=ebuf[0:1, 0:HT], in0=rowsc[0:1, 0:HT],
                                in1=rowsc[0:1, 0:HT], op=ALU.mult), "ve")
            v.wait_ge(s_ve, c["ve"])
            inc(v.tensor_tensor(out=ebuf[0:1, 0:HT], in0=rowsc[0:1, HT:2 * HT],
                                in1=ebuf[0:1, 0:HT], op=ALU.subtract), "ve")
        ve_var = c["ve"]
        @block.scalar
        def _(s, ve_var=ve_var):
            s.wait_ge(s_ve, ve_var)
            inc(s.activation(out=ebuf[0:1, 0:HT], in_=ebuf[0:1, 0:HT],
                             func=AF.Sqrt, bias=cst[0:1, 0:1]), "sc")
        sc_rstdf = c["sc"]
        @block.vector
        def _(v, sc_rstdf=sc_rstdf):
            v.wait_ge(s_sc, sc_rstdf)
            inc(v.reciprocal(out=ebuf[0:1, 0:HT], in_=ebuf[0:1, 0:HT]), "ve")
            v.wait_ge(s_ve, c["ve"])
            inc(v.tensor_tensor(out=rowsc[0:1, 0:HT], in0=rowsc[0:1, 0:HT],
                                in1=ebuf[0:1, 0:HT], op=ALU.mult), "ve")
            inc(v.tensor_scalar(out=rowsc[0:1, HT:2 * HT], in0=ebuf[0:1, 0:HT],
                                scalar1=0.5, scalar2=None, op0=ALU.mult), "ve")
        ve_rows = c["ve"]
        @block.sync
        def _(sync, h=h, ve_rows=ve_rows):
            sync.wait_ge(s_ve, ve_rows)
            inc(sync.dma_start(out=row_dram[h], in_=rowsc[0:1, :].rearrange(
                "o (a t) -> o a t", a=2)), "st")
        st_rows = c["st"]
        @block.sync
        def _(sync, h=h, st_rows=st_rows, ve_gate=c["ve"]):
            sync.wait_ge(s_st, st_rows)
            inc(sync.dma_start(out=nb_bc[:], in_=row_dram[h][0:1, :].to_broadcast([128, HT])), "rw")
            inc(sync.dma_start(out=sa_bc[:], in_=row_dram[h][1:2, :].to_broadcast([128, HT])), "rw")
        ld_rows = c["rw"]

        if h == 0:
            tap("t_rows", rowsc[0:1, :], gates=[("ve", c["ve"])])
        for mc2 in range(8):
            bank = pm[mc2 % 4]
            @block.tensor
            def _(t, mc2=mc2, bank=bank, ve_gate=c["ve"], sc_gate=c["sc"]):
                t.wait_ge(s_ve, ve_gate)
                t.wait_ge(s_sc, sc_gate)
                for kc2 in range(HKC):
                    inc(t.matmul(bank[:, 0:HT], wpj_sb[:, kc2, mc2 * 128:(mc2 + 1) * 128],
                                 g_bf[:, kc2, :], start=(kc2 == 0),
                                 stop=(kc2 == HKC - 1)), "mm")
            mm_pj = c["mm"]
            @block.vector
            def _(v, mc2=mc2, bank=bank, mm_pj=mm_pj, ld_rows=ld_rows):
                v.wait_ge(s_mm, mm_pj)
                v.wait_ge(s_rw, ld_rows)
                inc(v.tensor_tensor(out=fo[:, mc2 % 2, :], in0=bank[:, 0:HT],
                                    in1=sa_bc[:], op=ALU.mult), "ve")
                inc(v.scalar_tensor_tensor(out=fo[:, mc2 % 2, :], in0=nb_bc[:],
                                           scalar=nsw_sb[:, mc2:mc2 + 1],
                                           in1=fo[:, mc2 % 2, :], op0=ALU.mult,
                                           op1=ALU.add), "ve")
                inc(v.tensor_scalar(out=fo[:, mc2 % 2, :], in0=fo[:, mc2 % 2, :],
                                    scalar1=bo_sb[:, mc2:mc2 + 1], scalar2=None,
                                    op0=ALU.add), "ve")
            ve_fo = c["ve"]
            @block.tensor
            def _(t, mc2=mc2, bank=bank, ve_fo=ve_fo):
                t.wait_ge(s_ve, ve_fo)
                for ht in range(2):
                    inc(t.matmul(bank[:, HT + ht * 128:HT + (ht + 1) * 128],
                                 fo[:, mc2 % 2, ht * 128:(ht + 1) * 128], id_f[:],
                                 start=True, stop=True, is_transpose=True), "mm")
            mm_fot = c["mm"]
            @block.scalar
            def _(s, mc2=mc2, bank=bank, mm_fot=mm_fot):
                s.wait_ge(s_mm, mm_fot)
                for ht in range(2):
                    inc(s.activation(out=fo_tok[:, ht, mc2 * 128:(mc2 + 1) * 128],
                                     in_=bank[:, HT + ht * 128:HT + (ht + 1) * 128],
                                     func=AF.Copy), "sc")
        sc_fot = c["sc"]
        if h == 0:
            tap("t_fotok", fo_tok[:], gates=[("sc", c["sc"])])
        for ht in range(2):
            @block.vector
            def _(v, ht=ht, sc_fot=sc_fot):
                v.wait_ge(s_sc, sc_fot)
                inc(v.bn_stats(out=st6[:, 0:6], in_=fo_tok[:, ht, 0:512]), "ve")
                inc(v.bn_stats(out=st6[:, 6:12], in_=fo_tok[:, ht, 512:1024]), "ve")
                v.wait_ge(s_ve, c["ve"])
                inc(v.bn_aggr(out=st2[:], in_=st6[:]), "ve")
            ve_s4 = c["ve"]
            @block.scalar
            def _(s, ve_s4=ve_s4):
                s.wait_ge(s_ve, ve_s4)
                inc(s.activation(out=rs_sb[:, 3:4], in_=st2[:, 1:2], func=AF.Sqrt,
                                 bias=cst[:, 0:1]), "sc")
            sc_r4 = c["sc"]
            @block.vector
            def _(v, ht=ht, sc_r4=sc_r4):
                v.wait_ge(s_sc, sc_r4)
                inc(v.reciprocal(out=rs_sb[:, 3:4], in_=rs_sb[:, 3:4]), "ve")
                _ = v.wait_ge(s_ve, c["ve"])
                inc(v.tensor_scalar(out=fstage[:, 0:D], in0=fo_tok[:, ht, :],
                                    scalar1=st2[:, 0:1], scalar2=rs_sb[:, 3:4],
                                    op0=ALU.subtract, op1=ALU.mult), "ve")
                inc(v.tensor_tensor(out=fstage[:, D:2 * D], in0=fstage[:, 0:D],
                                    in1=xs1c[:, ht, :], op=ALU.add), "ve")
            ve_fin = c["ve"]
            @block.sync
            def _(sync, h=h, ht=ht, ve_fin=ve_fin):
                sync.wait_ge(s_ve, ve_fin)
                inc(sync.dma_start(out=y_out[h][ht * 128:(ht + 1) * 128, :],
                                   in_=fstage[:, D:2 * D]), "st")

    st_total = c["st"]
    @block.sync
    def _(sync):
        sync.wait_ge(s_st, st_total)

    ctxF.close()
    ctx.close()
    return nc


_NC_CACHE = None


def kernel(**inputs):
    global _NC_CACHE
    f32 = np.float32
    x = np.ascontiguousarray(inputs["L0_attn_x"], dtype=f32)
    W_in = np.asarray(inputs["L0_attn_W_in"], f32)
    ln_g = np.asarray(inputs["L0_attn_ln_g"], f32)
    ln_b = np.asarray(inputs["L0_attn_ln_b"], f32)
    conv_w = np.asarray(inputs["L0_attn_conv_w"], f32)
    conv_b = np.asarray(inputs["L0_attn_conv_b"], f32)
    W_x = np.asarray(inputs["L0_attn_W_x"], f32)
    W_dt = np.asarray(inputs["L0_attn_W_dt"], f32)
    b_dt = np.asarray(inputs["L0_attn_b_dt"], f32)
    A_log = np.asarray(inputs["L0_attn_A_log"], f32)
    Dp = np.asarray(inputs["L0_attn_Dp"], f32)
    W_out = np.asarray(inputs["L0_attn_W_out"], f32)
    W_fc = np.asarray(inputs["L0_ffn_W_fc"], f32)
    lnw = np.asarray(inputs["L0_ffn_ln_w"], f32)
    lnb = np.asarray(inputs["L0_ffn_ln_b"], f32)
    W_proj = np.asarray(inputs["L0_ffn_W_proj"], f32)

    CC = np.ascontiguousarray
    def dlay(vec, nk):
        return CC(vec.reshape(nk, 128).T)

    in_maps = []
    for core in range(8):
        p, q = core // 2, core % 2
        ch = slice(q * EH, (q + 1) * EH)
        w_inT_slice = CC(np.concatenate([W_in[ch], W_in[DIN + q * EH: DIN + (q + 1) * EH]],
                                        axis=0).T)
        m = {
            "x": CC(x[p]),
            "w_inT": w_inT_slice,
            "ln_g": dlay(ln_g, KC),
            "ln_b": dlay(ln_b, KC),
            "conv_w": CC(conv_w[ch, 0, :].reshape(DC, 128, 4).transpose(1, 0, 2).reshape(128, DC * 4)),
            "conv_b": dlay(conv_b[ch], DC),
            "w_xT": CC(W_x[:, ch].T),
            "w_dtT": CC(W_dt[ch].T),
            "b_dt": dlay(b_dt[ch], DC),
            "a_log": CC(A_log[ch].reshape(DC, 128, NST).transpose(1, 0, 2).reshape(128, DC * NST)),
            "dp": dlay(Dp[ch], DC),
            "w_outT": CC(W_out.T),
            "w_fcT": CC(W_fc.T),
            "lnw_f": dlay(lnw, HKC),
            "lnb_f": dlay(lnb, HKC),
            "w_projT": CC(W_proj.T),
            "ident": np.eye(128, dtype=f32),
            "x_own": CC(x[p].reshape(NCH, 2, HT, D)[:, q]),
        }
        in_maps.append(m)

    if _NC_CACHE is None:
        _NC_CACHE = build_nc()
    nc = _NC_CACHE
    res = run_bass_kernel_spmd(nc, in_maps, core_ids=list(range(8)))

    out = np.empty((4, TSEQ, D), dtype=f32)
    for core in range(8):
        p, q = core // 2, core % 2
        yc = res.results[core]["y"]
        for k in range(NCH):
            t0 = k * TT + q * HT
            out[p, t0:t0 + HT, :] = yc[k]
    if DEBUG_TAPS:
        kernel.LAST_TAPS = [dict(r) for r in res.results]
    return out


# version 09
# speedup vs baseline: 1.3590x; 1.0491x over previous
import numpy as np
from contextlib import ExitStack

import concourse.bass as bass
import concourse.mybir as mybir
from concourse.bass_utils import run_bass_kernel_spmd

F32 = mybir.dt.float32
BF16 = mybir.dt.bfloat16
ALU = mybir.AluOpType
AF = mybir.ActivationFunctionType
AX = mybir.AxisListType

D = 1024
DIN = 2048
EH = 1024
NST = 16
RNK = 64
TSEQ = 4096
TT = 512
HT = 256
NCH = TSEQ // TT
KC = D // 128
DC = EH // 128
MCXZ = 2 * EH // 128
NKO = DIN // 128
HF = 4096
HKC = HF // 128
DEBUG_TAPS = False
EPS1 = 1e-5
EPS2 = 1e-6
EPSF = 1e-5
AMEPS = 1e-6
RSQ2 = 0.7071067811865476
PAIRS = [[0, 1], [2, 3], [4, 5], [6, 7]]


def build_nc():
    nc = bass.Bass("TRN2", target_bir_lowering=False)

    def ein(name, shape, dtype=F32):
        return nc.dram_tensor(name, list(shape), dtype, kind="ExternalInput").ap()

    x_in = ein("x", (TSEQ, D))
    w_inT = ein("w_inT", (D, 2 * EH))
    ln_g = ein("ln_g", (128, KC))
    ln_b = ein("ln_b", (128, KC))
    conv_w = ein("conv_w", (128, DC * 4))
    conv_b = ein("conv_b", (128, DC))
    w_xT = ein("w_xT", (EH, 96))
    w_dtT = ein("w_dtT", (RNK, EH))
    b_dt = ein("b_dt", (128, DC))
    a_log = ein("a_log", (128, DC * NST))
    dp_in = ein("dp", (128, DC))
    w_outT = ein("w_outT", (DIN, D))
    w_fcT = ein("w_fcT", (D, HF))
    lnw_f = ein("lnw_f", (128, HKC))
    lnb_f = ein("lnb_f", (128, HKC))
    w_projT = ein("w_projT", (HF, D))
    ident = ein("ident", (128, 128))
    x_own = ein("x_own", (NCH, HT, D))

    y_out = nc.dram_tensor("y", [NCH, HT, D], F32, kind="ExternalOutput").ap()
    taps = {}
    if DEBUG_TAPS:
        for nm, shape, dt_ in (("t_hf", (128, KC * TT // 2), F32),
                               ("t_xi", (128, DC, TT + 4), F32),
                               ("t_sz", (128, DC, TT), F32),
                               ("t_u", (128, DC, TT), F32),
                               ("t_dt", (128, DC, TT), F32),
                               ("t_xdbl", (96, TT), F32),
                               ("t_yacc", (128, DC, TT), F32),
                               ("t_y2h", (128, DC * TT // 2), F32),
                               ("t_om", (128, 2, HT), F32),
                               ("t_epim", (128, 2, D), F32),
                               ("t_xs1", (128, 2, D), F32),
                               ("t_thr", (128, 4), F32),
                               ("t_wfc", (128, HF // 2), F32),
                               ("t_wpj", (128, HKC * D // 2), F32),
                               ("t_nswbo", (128, 16), F32),
                               ("t_xlnf", (128, KC * HT // 2), F32),
                               ("t_gbf", (128, HKC * HT // 2), F32),
                               ("t_rows", (1, 2 * HT), F32),
                               ("t_fotok", (128, 2, D), F32)):
            taps[nm] = nc.dram_tensor(nm, list(shape), dt_, kind="ExternalOutput").ap()

    e1_in = nc.dram_tensor("e1_in", [NCH, 96, TT], F32, kind="Internal").ap()
    e1_out = nc.dram_tensor("e1_out", [NCH, 96, TT], F32, kind="Internal").ap()
    e2_in = nc.dram_tensor("e2_in", [NCH, EH, TT], BF16, kind="Internal").ap()
    e2_out = nc.dram_tensor("e2_out", [NCH, DIN, TT], BF16, kind="Internal").ap()
    wout_bf = nc.dram_tensor("wout_bf", [NKO, 128, D], BF16, kind="Internal").ap()
    xs1_sp = nc.dram_tensor("xs1_sp", [NCH, HT, D], F32, kind="Internal").ap()
    row_dram = nc.dram_tensor("row_dram", [NCH, 2, HT], F32, kind="Internal").ap()
    thr_dram = nc.dram_tensor("thr_dram", [1, 4], F32, kind="Internal").ap()

    ctx = ExitStack()
    ctxM = ExitStack()
    sb = lambda st, name, shape, dtype=F32: st.enter_context(
        nc.sbuf_tensor(name, list(shape), dtype))
    psum = lambda name, shape: ctx.enter_context(
        nc.psum_tensor(name, list(shape), F32))
    sem = lambda name: ctx.enter_context(nc.semaphore(name))

    a_sb = sb(ctx, "a_sb", (128, DC * NST))
    g_sb = sb(ctx, "g_sb", (128, KC))
    lnb_sb = sb(ctx, "lnb_sb", (128, KC))
    convw_sb = sb(ctx, "convw_sb", (128, DC * 4))
    convb_sb = sb(ctx, "convb_sb", (128, DC))
    bdt_sb = sb(ctx, "bdt_sb", (128, DC))
    dp_sb = sb(ctx, "dp_sb", (128, DC))
    winb_sb = sb(ctx, "winb_sb", (128, MCXZ))
    id_f = sb(ctx, "id_f", (128, 128))
    id_b = sb(ctx, "id_b", (128, 128), BF16)
    ones_f = sb(ctx, "ones_f", (128, 1))
    ones_b = sb(ctx, "ones_b", (128, 1), BF16)
    thr_sb = sb(ctx, "thr_sb", (128, 4))
    lnwf_sb = sb(ctx, "lnwf_sb", (128, HKC))
    lnbf_sb = sb(ctx, "lnbf_sb", (128, HKC))
    lnwf_b = sb(ctx, "lnwf_b", (128, HKC), BF16)
    lnbf_b = sb(ctx, "lnbf_b", (128, HKC), BF16)
    gs_sb = sb(ctx, "gs_sb", (128, 2))
    tiny = sb(ctx, "tiny", (1, 8))
    st6 = sb(ctx, "st6", (128, 12))
    st2 = sb(ctx, "st2", (128, 2))
    rs_sb = sb(ctx, "rs_sb", (128, 4))
    cst = sb(ctx, "cst", (128, 6))

    w_in_sb = sb(ctxM, "w_in_sb", (128, KC, 2 * EH), BF16)
    w_x_sb = sb(ctxM, "w_x_sb", (128, KC * 96), BF16)
    w_dt_sb = sb(ctxM, "w_dt_sb", (RNK, EH), BF16)
    big = sb(ctxM, "big", (128, 2 * D))
    xtile = sb(ctxM, "xtile", (128, D))
    h_tok = sb(ctxM, "h_tok", (128, D), BF16)
    h_f = sb(ctxM, "h_f", (128, KC, TT), BF16)
    xi = sb(ctxM, "xi", (128, DC, TT + 4))
    sz = sb(ctxM, "sz", (128, DC, TT))
    u_sb = sb(ctxM, "u_sb", (128, DC, TT))
    ubf = sb(ctxM, "ubf", (128, TT), BF16)
    dt_sb = sb(ctxM, "dt_sb", (128, DC, TT))
    dtu_sb = sb(ctxM, "dtu_sb", (128, DC, TT))
    dtin_bf = sb(ctxM, "dtin_bf", (RNK, TT), BF16)
    sp1 = sb(ctxM, "sp1", (128, TT))
    sp2 = sb(ctxM, "sp2", (128, TT))
    sp3 = sb(ctxM, "sp3", (128, TT))
    bbc = sb(ctxM, "bbc", (128, 2, TT))
    cbc = sb(ctxM, "cbc", (128, 2, TT))
    da2 = sb(ctxM, "da2", (128, 2, 2, TT + 1))
    db2 = sb(ctxM, "db2", (128, 2, 2, TT + 1))
    yp_scr = sb(ctxM, "yp_scr", (128, TT))
    yacc = sb(ctxM, "yacc", (128, DC, TT))
    h_carry = sb(ctxM, "h_carry", (128, NST, DC))
    y2h = sb(ctxM, "y2h", (128, DC, TT), BF16)
    wo_st = sb(ctxM, "wo_st", (128, 2, D), BF16)
    epim = big[:].rearrange("p (h d) -> p h d", h=2)

    pm = [psum(f"pm{i}", (128, TT)) for i in range(8)]

    s_tp = sem("s_tp")
    s_ld = sem("s_ld"); s_gl = sem("s_gl"); s_mm = sem("s_mm")
    s_sc = sem("s_sc"); s_ve = sem("s_ve"); s_cc = sem("s_cc"); s_st = sem("s_st")
    s_gp = sem("s_gp")
    s_x = sem("s_x"); s_xd = sem("s_xd"); s_bb = sem("s_bb"); s_cb = sem("s_cb")
    s_wo = sem("s_wo"); s_y2 = sem("s_y2"); s_wi = sem("s_wi"); s_pc = sem("s_pc")
    s_rw = sem("s_rw")
    S = {"ld": s_ld, "gl": s_gl, "mm": s_mm, "sc": s_sc, "ve": s_ve,
         "cc": s_cc, "st": s_st, "x": s_x, "xd": s_xd, "bb": s_bb, "cb": s_cb,
         "wo": s_wo, "y2": s_y2, "wi": s_wi, "pc": s_pc, "rw": s_rw, "gp": s_gp}
    c = {k: 0 for k in S}

    block = ctx.enter_context(nc.Block())

    DMA_KEYS = ("ld", "gl", "st", "x", "xd", "bb", "cb", "wo", "y2", "wi",
                "pc", "rw")
    def inc(instr, key, n=None):
        if n is None:
            n = 16 if key in DMA_KEYS else 1
        instr.then_inc(S[key], n)
        c[key] += n
        return c[key]

    tapcnt = [0]
    def tap(name, src_ap, gates=()):
        if not DEBUG_TAPS:
            return
        @block.scalar
        def _(s, name=name, src_ap=src_ap, gates=tuple(gates)):
            for key, val in gates:
                s.wait_ge(S[key], val)
            s.dma_start(out=taps[name], in_=src_ap).then_inc(s_tp, 16)
            tapcnt[0] += 16
            s.wait_ge(s_tp, tapcnt[0])



    @block.sync
    def _(sync):
        for dst, srcap in ((id_f, ident), (g_sb, ln_g), (lnb_sb, ln_b),
                           (convw_sb, conv_w), (convb_sb, conv_b),
                           (bdt_sb, b_dt), (dp_sb, dp_in),
                           (lnwf_sb, lnw_f), (lnbf_sb, lnb_f)):
            inc(sync.dma_start(out=dst[:], in_=srcap), "pc")
        inc(sync.dma_start(out=sp1[0:RNK, 0:TT], in_=w_dtT[:, 0:TT]), "pc")
        inc(sync.dma_start(out=sp2[0:RNK, 0:TT], in_=w_dtT[:, TT:EH]), "pc")
        inc(sync.dma_start(out=big[:, 0:KC * 96].rearrange("p (k c) -> p k c", k=KC),
                           in_=w_xT.rearrange("(k p) c -> p k c", p=128)), "pc")
        inc(sync.dma_start(out=big[:, 1024:1024 + DC * NST], in_=a_log), "pc")
    ld_consts = c["pc"]

    @block.vector
    def _(v):
        v.wait_ge(s_pc, ld_consts)
        inc(v.memset(ones_f[:], 1.0), "ve")
        inc(v.memset(cst[:, 0:1], EPS1), "ve")
        inc(v.memset(cst[:, 1:2], EPS2), "ve")
        inc(v.memset(cst[:, 2:3], 1.0), "ve")
        inc(v.memset(cst[:, 3:4], -1.0), "ve")
        inc(v.memset(cst[:, 4:5], RSQ2), "ve")
        inc(v.memset(ones_b[:], 1.0), "ve")
        inc(v.memset(h_carry[:], 0.0), "ve")
        inc(v.memset(da2[:, :, :, 0:1].opt(), 0.0), "ve")
        inc(v.memset(xi[:, :, 0:4].opt(), 0.0), "ve")
        inc(v.tensor_copy(out=id_b[:], in_=id_f[:]), "ve")
        inc(v.tensor_copy(out=w_dt_sb[:, 0:TT], in_=sp1[0:RNK, 0:TT]), "ve")
        inc(v.tensor_copy(out=w_dt_sb[:, TT:EH], in_=sp2[0:RNK, 0:TT]), "ve")
        inc(v.tensor_copy(out=w_x_sb[:], in_=big[:, 0:KC * 96]), "ve")
        inc(v.tensor_copy(out=lnwf_b[:], in_=lnwf_sb[:]), "ve")
        inc(v.tensor_copy(out=lnbf_b[:], in_=lnbf_sb[:]), "ve")
    ve_consts = c["ve"]

    @block.scalar
    def _(s):
        s.wait_ge(s_pc, ld_consts)
        inc(s.activation(out=a_sb[:], in_=big[:, 1024:1024 + DC * NST], func=AF.Exp), "sc")
    sc_aexp = c["sc"]
    @block.vector
    def _(v):
        v.wait_ge(s_sc, sc_aexp)
        inc(v.tensor_scalar(out=a_sb[:], in0=a_sb[:], scalar1=-1.0, scalar2=None,
                            op0=ALU.mult), "ve")
    ve_aneg = c["ve"]

    win_cast_at = {}
    for kc in range(KC):
        buf = big[:, 0:2048]
        gate = win_cast_at.get(kc - 1)
        @block.sync
        def _(sync, buf=buf, kc=kc, gate=gate):
            if kc == 0:
                sync.wait_ge(s_ve, ve_consts)
                sync.wait_ge(s_sc, sc_aexp)
            if gate is not None:
                sync.wait_ge(s_sc, gate)
            inc(sync.dma_start(out=buf, in_=w_inT[kc * 128:(kc + 1) * 128, :]), "wi")
        ld_here = c["wi"]
        @block.tensor
        def _(t, kc=kc, buf=buf, ld_here=ld_here, ve_gate=c["ve"]):
            t.wait_ge(s_wi, ld_here)
            if kc > 0:
                t.wait_ge(s_ve, ve_gate)
            for mc in range(MCXZ):
                inc(t.matmul(pm[6][:, mc:mc + 1], buf[:, mc * 128:(mc + 1) * 128],
                             lnb_sb[:, kc:kc + 1], start=True, stop=True), "mm")
        mm_here = c["mm"]
        @block.vector
        def _(v, kc=kc, mm_here=mm_here):
            v.wait_ge(s_mm, mm_here)
            if kc == 0:
                inc(v.tensor_copy(out=winb_sb[:], in_=pm[6][:, 0:MCXZ]), "ve")
            else:
                inc(v.tensor_tensor(out=winb_sb[:], in0=winb_sb[:],
                                    in1=pm[6][:, 0:MCXZ], op=ALU.add), "ve")
            v.wait_ge(s_ve, c["ve"])
        @block.scalar
        def _(s, kc=kc, buf=buf, mm_here=mm_here):
            s.wait_ge(s_mm, mm_here)
            inc(s.activation(out=w_in_sb[:, kc, :], in_=buf, func=AF.Copy,
                             scale=g_sb[:, kc:kc + 1]), "sc")
        win_cast_at[kc] = c["sc"]
    sc_winb = c["sc"]

    wout_st_at = {}
    for kc in range(NKO):
        buf = big[:, (kc % 2) * 1024:(kc % 2) * 1024 + 1024]
        gate = wout_st_at.get(kc - 2)
        @block.gpsimd
        def _(g, kc=kc, buf=buf, gate=gate):
            if kc == 0:
                g.wait_ge(s_sc, win_cast_at[KC - 1])
            if gate is not None:
                g.wait_ge(s_st, gate)
            inc(g.dma_start(out=buf, in_=w_outT[kc * 128:(kc + 1) * 128, :]), "gl")
        gl_here = c["gl"]
        @block.vector
        def _(v, kc=kc, buf=buf, gl_here=gl_here, ve_gate=c["ve"]):
            v.wait_ge(s_gl, gl_here)
            inc(v.tensor_copy(out=wo_st[:, kc % 2, :], in_=buf), "ve")
        ve_here = c["ve"]
        @block.sync
        def _(sync, kc=kc, ve_here=ve_here):
            sync.wait_ge(s_ve, ve_here)
            inc(sync.dma_start(out=wout_bf[kc], in_=wo_st[:, kc % 2, :]), "st")
        wout_st_at[kc] = c["st"]

    @block.vector
    def _(v):
        inc(v.memset(gs_sb[:], 0.0), "ve")
    gamma_prev = c["ve"]
    gl_gamma_last = c["gl"]
    for wi, (wT, nk, cols) in enumerate(((w_fcT, KC, HF), (w_projT, HKC, D))):
        nh = 2 if cols > 2048 else 1
        hw = cols // nh
        for kc in range(nk):
            for hh in range(nh):
                @block.gpsimd
                def _(g, wT=wT, kc=kc, hh=hh, hw=hw, gate=gamma_prev):
                    g.wait_ge(s_ve, gate)
                    inc(g.dma_start(out=big[:, 0:hw],
                                    in_=wT[kc * 128:(kc + 1) * 128,
                                           hh * hw:(hh + 1) * hw]), "gl")
                gl_here = c["gl"]
                @block.vector
                def _(v, wi=wi, hw=hw, gl_here=gl_here):
                    v.wait_ge(s_gl, gl_here)
                    inc(v.tensor_reduce(out=st2[:, 0:1], in_=big[:, 0:hw], axis=AX.X,
                                        op=ALU.add, apply_absolute_value=True), "ve")
                    v.wait_ge(s_ve, c["ve"])
                    inc(v.tensor_tensor(out=gs_sb[:, wi:wi + 1], in0=gs_sb[:, wi:wi + 1],
                                        in1=st2[:, 0:1], op=ALU.add), "ve")
                gamma_prev = c["ve"]
    gl_gamma_last = c["gl"]
    ve_gamma = c["ve"]
    @block.tensor
    def _(t):
        t.wait_ge(s_ve, ve_gamma)
        inc(t.matmul(pm[7][0:1, 0:1], gs_sb[:, 0:1], ones_f[:], start=True, stop=True), "mm")
        inc(t.matmul(pm[7][0:1, 1:2], gs_sb[:, 1:2], ones_f[:], start=True, stop=True), "mm")
    mm_gam = c["mm"]
    @block.vector
    def _(v):
        v.wait_ge(s_mm, mm_gam)
        inc(v.tensor_scalar(out=tiny[0:1, 0:1], in0=pm[7][0:1, 0:1],
                            scalar1=0.5 / (HF * D), scalar2=0.5 * AMEPS,
                            op0=ALU.mult, op1=ALU.add), "ve")
        v.wait_ge(s_ve, c["ve"])
        inc(v.tensor_scalar(out=tiny[0:1, 1:2], in0=tiny[0:1, 0:1],
                            scalar1=-1.0, scalar2=None, op0=ALU.mult), "ve")
        inc(v.tensor_scalar(out=tiny[0:1, 2:3], in0=pm[7][0:1, 1:2],
                            scalar1=0.5 / (HF * D), scalar2=0.5 * AMEPS,
                            op0=ALU.mult, op1=ALU.add), "ve")
        v.wait_ge(s_ve, c["ve"])
        inc(v.tensor_scalar(out=tiny[0:1, 3:4], in0=tiny[0:1, 2:3],
                            scalar1=-1.0, scalar2=None, op0=ALU.mult), "ve")
    ve_thr = c["ve"]
    @block.sync
    def _(sync):
        sync.wait_ge(s_ve, ve_thr)
        inc(sync.dma_start(out=thr_dram, in_=tiny[0:1, 0:4]), "st")
    st_thr = c["st"]
    @block.sync
    def _(sync):
        sync.wait_ge(s_st, st_thr)
        inc(sync.dma_start(out=thr_sb[:], in_=thr_dram.to_broadcast([128, 4])), "pc")
    ld_thr = c["pc"]

    st_xs1_prev = c["st"]
    for k in range(NCH):
        for tt in range(4):
            @block.sync
            def _(sync, k=k, tt=tt, gate_ve=c["ve"]):
                sync.wait_ge(s_ve, gate_ve)
                inc(sync.dma_start(out=xtile[:],
                                   in_=x_in[k * TT + tt * 128: k * TT + (tt + 1) * 128, :]), "x")
            ld_x = c["x"]
            mtile = big[:, (tt % 2) * D:(tt % 2 + 1) * D]
            @block.vector
            def _(v, ld_x=ld_x, k=k, tt=tt, gl_gate=gl_gamma_last,
                  st_xs1_prev=st_xs1_prev):
                v.wait_ge(s_x, ld_x)
                if tt == 0:
                    v.wait_ge(s_st, st_xs1_prev)
                if k == 0 and tt == 0:
                    v.wait_ge(s_ve, ve_aneg)
                    v.wait_ge(s_gl, gl_gate)
                inc(v.bn_stats(out=st6[:, 0:6], in_=xtile[:, 0:512]), "ve")
                inc(v.bn_stats(out=st6[:, 6:12], in_=xtile[:, 512:1024]), "ve")
                v.wait_ge(s_ve, c["ve"])
                inc(v.bn_aggr(out=st2[:], in_=st6[:]), "ve")
            ve_stat1 = c["ve"]
            @block.scalar
            def _(s, ve_stat1=ve_stat1):
                s.wait_ge(s_ve, ve_stat1)
                inc(s.activation(out=rs_sb[:, 0:1], in_=st2[:, 1:2], func=AF.Sqrt,
                                 bias=cst[:, 0:1]), "sc")
            sc_r1 = c["sc"]
            @block.vector
            def _(v, mtile=mtile, sc_r1=sc_r1):
                v.wait_ge(s_sc, sc_r1)
                inc(v.reciprocal(out=rs_sb[:, 0:1], in_=rs_sb[:, 0:1]), "ve")
                _ = v.wait_ge(s_ve, c["ve"])
                inc(v.tensor_scalar(out=mtile, in0=xtile[:], scalar1=st2[:, 0:1],
                                    scalar2=rs_sb[:, 0:1], op0=ALU.subtract,
                                    op1=ALU.mult), "ve")
                inc(v.bn_stats(out=st6[:, 0:6], in_=mtile[:, 0:512]), "ve")
                inc(v.bn_stats(out=st6[:, 6:12], in_=mtile[:, 512:1024]), "ve")
                v.wait_ge(s_ve, c["ve"])
                inc(v.bn_aggr(out=st2[:], in_=st6[:]), "ve")
            ve_stat2 = c["ve"]
            @block.scalar
            def _(s, ve_stat2=ve_stat2):
                s.wait_ge(s_ve, ve_stat2)
                inc(s.activation(out=rs_sb[:, 1:2], in_=st2[:, 1:2], func=AF.Sqrt,
                                 bias=cst[:, 1:2]), "sc")
            sc_r2 = c["sc"]
            @block.vector
            def _(v, mtile=mtile, sc_r2=sc_r2):
                v.wait_ge(s_sc, sc_r2)
                inc(v.reciprocal(out=rs_sb[:, 1:2], in_=rs_sb[:, 1:2]), "ve")
                _ = v.wait_ge(s_ve, c["ve"])
                inc(v.tensor_scalar(out=h_tok[:], in0=mtile, scalar1=st2[:, 0:1],
                                    scalar2=rs_sb[:, 1:2], op0=ALU.subtract,
                                    op1=ALU.mult), "ve")
            ve_h = c["ve"]
            @block.tensor
            def _(t, tt=tt, ve_h=ve_h, sc_gate=c["sc"], ve_gate=c["ve"]):
                t.wait_ge(s_ve, max(ve_h, ve_gate))
                t.wait_ge(s_sc, sc_gate)
                pmb = pm[4][:].bitcast(BF16)
                for kc in range(KC):
                    inc(t.matmul(pmb[:, kc * 128:(kc + 1) * 128],
                                 h_tok[:, kc * 128:(kc + 1) * 128], id_b[:],
                                 start=True, stop=True, is_transpose=True), "mm")
            mm_tr = c["mm"]
            @block.scalar
            def _(s, tt=tt, mm_tr=mm_tr):
                s.wait_ge(s_mm, mm_tr)
                inc(s.activation(out=h_f[:, :, tt * 128:(tt + 1) * 128],
                                 in_=pm[4][:].bitcast(BF16).rearrange(
                                     "p (kc t) -> p kc t", kc=KC),
                                 func=AF.Copy), "sc")
        sc_hf = c["sc"]
        if k == 0:
            tap("t_hf", h_f[:].rearrange("p a b -> p (a b)").bitcast(F32),
                gates=[("sc", c["sc"])])

        sc_win0 = c["sc"]
        ve_halo = c["ve"]
        mm_win_at = []
        for mc in range(MCXZ):
            bank = pm[mc % 4]
            @block.tensor
            def _(t, mc=mc, bank=bank, sc_hf=sc_hf, ve_gate=ve_halo,
                  sc_win0=sc_win0):
                if mc == 0:
                    t.wait_ge(s_sc, sc_hf)
                    t.wait_ge(s_ve, ve_gate)
                if mc >= 4:
                    t.wait_ge(s_sc, sc_win0 + (mc - 4) + 1)
                for kc in range(KC):
                    inc(t.matmul(bank[:], w_in_sb[:, kc, mc * 128:(mc + 1) * 128],
                                 h_f[:, kc, :], start=(kc == 0), stop=(kc == KC - 1)),
                        "mm")
            mm_win_at.append(c["mm"])
        @block.scalar
        def _(s, mm_win_at=tuple(mm_win_at), ve_halo=ve_halo):
            s.wait_ge(s_ve, ve_halo)
            for mc in range(MCXZ):
                s.wait_ge(s_mm, mm_win_at[mc])
                bank = pm[mc % 4]
                if mc < DC:
                    inc(s.activation(out=xi[:, mc, 4:4 + TT], in_=bank[:],
                                     func=AF.Identity,
                                     bias=winb_sb[:, mc:mc + 1]), "sc")
                else:
                    inc(s.activation(out=sz[:, mc - DC, :], in_=bank[:], func=AF.Silu,
                                     bias=winb_sb[:, mc:mc + 1]), "sc")
        assert c["sc"] == sc_win0 + MCXZ
        sc_xz = c["sc"]
        if k == 0:
            tap("t_xi", xi[:], gates=[("sc", c["sc"])])
            tap("t_sz", sz[:], gates=[("sc", c["sc"])])

        ve_conv_at = []
        for dc in range(DC):
            @block.vector
            def _(v, dc=dc, sc_win0=sc_win0):
                v.wait_ge(s_sc, sc_win0 + dc + 1)
                inc(v.tensor_scalar(out=sp1[:], in0=xi[:, dc, 1:1 + TT],
                                    scalar1=convw_sb[:, dc * 4:dc * 4 + 1],
                                    scalar2=convb_sb[:, dc:dc + 1],
                                    op0=ALU.mult, op1=ALU.add), "ve")
                inc(v.scalar_tensor_tensor(out=sp2[:], in0=xi[:, dc, 2:2 + TT],
                                           scalar=convw_sb[:, dc * 4 + 1:dc * 4 + 2],
                                           in1=sp1[:], op0=ALU.mult, op1=ALU.add), "ve")
                inc(v.scalar_tensor_tensor(out=sp1[:], in0=xi[:, dc, 3:3 + TT],
                                           scalar=convw_sb[:, dc * 4 + 2:dc * 4 + 3],
                                           in1=sp2[:], op0=ALU.mult, op1=ALU.add), "ve")
                inc(v.scalar_tensor_tensor(out=dtu_sb[:, dc, :], in0=xi[:, dc, 4:4 + TT],
                                           scalar=convw_sb[:, dc * 4 + 3:dc * 4 + 4],
                                           in1=sp1[:], op0=ALU.mult, op1=ALU.add), "ve")
                inc(v.tensor_copy(out=xi[:, dc, 1:4], in_=xi[:, dc, 1 + TT:4 + TT]), "ve")
            ve_conv_at.append(c["ve"])
        sc_u0 = c["sc"]
        @block.scalar
        def _(s, ve_conv_at=tuple(ve_conv_at)):
            for dc in range(DC):
                s.wait_ge(s_ve, ve_conv_at[dc])
                inc(s.activation(out=u_sb[:, dc, :], in_=dtu_sb[:, dc, :],
                                 func=AF.Silu), "sc")
        assert c["sc"] == sc_u0 + DC

        for dc in range(DC):
            @block.vector
            def _(v, dc=dc, sc_u0=sc_u0, mm_gate=c["mm"]):
                v.wait_ge(s_sc, sc_u0 + dc + 1)
                if dc > 0:
                    v.wait_ge(s_mm, mm_gate)
                inc(v.tensor_copy(out=ubf[:], in_=u_sb[:, dc, :]), "ve")
            ve_ubf = c["ve"]
            @block.tensor
            def _(t, dc=dc, ve_ubf=ve_ubf, sc_gate=c["sc"]):
                t.wait_ge(s_ve, ve_ubf)
                if dc == 0:
                    t.wait_ge(s_sc, sc_gate)
                inc(t.matmul(pm[6][0:96, :], w_x_sb[:, dc * 96:(dc + 1) * 96],
                             ubf[:], start=(dc == 0), stop=(dc == DC - 1)),
                    "mm")
        mm_xp = c["mm"]
        @block.vector
        def _(v, mm_xp=mm_xp):
            v.wait_ge(s_mm, mm_xp)
            inc(v.tensor_copy(out=sp1[0:96, :], in_=pm[6][0:96, :]), "ve")
        ve_xp = c["ve"]
        @block.sync
        def _(sync, k=k, ve_xp=ve_xp):
            sync.wait_ge(s_ve, ve_xp)
            inc(sync.dma_start(out=e1_in[k], in_=sp1[0:96, :]), "st")
        st_e1 = c["st"]
        @block.gpsimd
        def _(g, k=k, st_e1=st_e1):
            g.wait_ge(s_st, st_e1)
            inc(g.collective_compute("AllReduce", ALU.add, replica_groups=PAIRS,
                                     ins=[e1_in[k]], outs=[e1_out[k]]), "cc")
        cc1 = c["cc"]
        @block.sync
        def _(sync, k=k, cc1=cc1):
            sync.wait_ge(s_cc, cc1)
            inc(sync.dma_start(out=sp1[0:RNK, :], in_=e1_out[k][0:RNK, :]), "xd")
        ld_xdbl = c["xd"]
        if k == 0:
            tap("t_xdbl", sp1[0:96, :], gates=[("xd", c["xd"])])

        @block.vector
        def _(v, ld_xdbl=ld_xdbl):
            v.wait_ge(s_xd, ld_xdbl)
            inc(v.tensor_copy(out=dtin_bf[:], in_=sp1[0:RNK, :]), "ve")
        ve_dtin = c["ve"]
        for dc in range(DC):
            bank = pm[dc % 2]
            @block.tensor
            def _(t, dc=dc, bank=bank, ve_dtin=ve_dtin, ve_gate=c["ve"], sc_gate=c["sc"]):
                t.wait_ge(s_ve, max(ve_dtin, ve_gate))
                t.wait_ge(s_sc, sc_gate)
                inc(t.matmul(bank[:], w_dt_sb[:, dc * 128:(dc + 1) * 128],
                             dtin_bf[:], start=True, stop=True), "mm")
            mm_dt = c["mm"]
            @block.vector
            def _(v, dc=dc, bank=bank, mm_dt=mm_dt):
                v.wait_ge(s_mm, mm_dt)
                inc(v.tensor_scalar(out=dt_sb[:, dc, :], in0=bank[:],
                                    scalar1=bdt_sb[:, dc:dc + 1],
                                    scalar2=0.0, op0=ALU.add, op1=ALU.max), "ve")
                inc(v.tensor_scalar(out=dtu_sb[:, dc, :], in0=bank[:],
                                    scalar1=bdt_sb[:, dc:dc + 1],
                                    scalar2=-1.0, op0=ALU.add, op1=ALU.mult), "ve")
                inc(v.scalar_tensor_tensor(out=dtu_sb[:, dc, :],
                                           in0=bank[:],
                                           scalar=bdt_sb[:, dc:dc + 1],
                                           in1=dtu_sb[:, dc, :],
                                           op0=ALU.add, op1=ALU.max), "ve")
        ve_av = c["ve"]
        @block.scalar
        def _(s, ve_av=ve_av):
            s.wait_ge(s_ve, ve_av)
            inc(s.activation(out=big[:, 0:2048],
                             in_=dtu_sb[:, 0:4, :].rearrange("p a b -> p (a b)"),
                             func=AF.Exp, scale=cst[:, 3:4]), "sc")
            inc(s.activation(out=sp2[:],
                             in_=dtu_sb[:, 4, :], func=AF.Exp, scale=cst[:, 3:4]), "sc")
            inc(s.activation(out=sp3[:],
                             in_=dtu_sb[:, 5, :], func=AF.Exp, scale=cst[:, 3:4]), "sc")
            inc(s.activation(out=sp1[:],
                             in_=dtu_sb[:, 6, :], func=AF.Exp, scale=cst[:, 3:4]), "sc")
            inc(s.activation(out=h_tok[:].bitcast(F32),
                             in_=dtu_sb[:, 7, :], func=AF.Exp, scale=cst[:, 3:4]), "sc")
            inc(s.activation(out=dtu_sb[:, 0:4, :].rearrange("p a b -> p (a b)"),
                             in_=big[:, 0:2048], func=AF.Ln, bias=cst[:, 2:3]), "sc")
            inc(s.activation(out=dtu_sb[:, 4, :], in_=sp2[:], func=AF.Ln,
                             bias=cst[:, 2:3]), "sc")
            inc(s.activation(out=dtu_sb[:, 5, :], in_=sp3[:], func=AF.Ln,
                             bias=cst[:, 2:3]), "sc")
            inc(s.activation(out=dtu_sb[:, 6, :], in_=sp1[:], func=AF.Ln,
                             bias=cst[:, 2:3]), "sc")
            inc(s.activation(out=dtu_sb[:, 7, :], in_=h_tok[:].bitcast(F32)[:, 0:TT],
                             func=AF.Ln, bias=cst[:, 2:3]), "sc")
        sc_sp = c["sc"]
        for dc in range(DC):
            @block.vector
            def _(v, dc=dc, sc_sp=sc_sp):
                if dc == 0:
                    v.wait_ge(s_sc, sc_sp)
                inc(v.tensor_tensor(out=dt_sb[:, dc, :], in0=dt_sb[:, dc, :],
                                    in1=dtu_sb[:, dc, :], op=ALU.add), "ve")
                inc(v.tensor_tensor(out=dtu_sb[:, dc, :], in0=dt_sb[:, dc, :],
                                    in1=u_sb[:, dc, :], op=ALU.mult), "ve")
        ve_dt = c["ve"]
        if k == 0:
            tap("t_dt", dt_sb[:], gates=[("ve", c["ve"])])
            tap("t_u", u_sb[:], gates=[("ve", c["ve"])])

        NPAIR = NST * (DC // 2)
        vb = c["ve"]; gb = c["gp"]; scb = c["sc"]
        bb_at = []; cb_at = []
        for n in range(NST):
            @block.sync
            def _(sync, k=k, n=n, vb=vb):
                if n >= 2:
                    jp = 4 * n - 5
                    sync.wait_ge(s_ve, vb + 5 * jp + 5)
                inc(sync.dma_start(out=bbc[:, n % 2, :],
                                   in_=e1_out[k][64 + n:65 + n, :].to_broadcast([128, TT])), "bb")
                inc(sync.dma_start(out=cbc[:, n % 2, :],
                                   in_=e1_out[k][80 + n:81 + n, :].to_broadcast([128, TT])), "cb")
            bb_at.append(c["bb"]); cb_at.append(c["cb"])
        @block.scalar
        def _(s, vb=vb):
            s.wait_ge(s_ve, vb)
            for j in range(NPAIR):
                n, p_ = divmod(j, 4)
                d0 = 2 * p_; slot = j % 2
                if j >= 2:
                    s.wait_ge(s_ve, vb + 5 * (j - 2) + 2)
                inc(s.activation(out=da2[:, slot, 0, 1:1 + TT], in_=dt_sb[:, d0, :],
                                 func=AF.Exp,
                                 scale=a_sb[:, d0 * NST + n:d0 * NST + n + 1]), "sc")
                inc(s.activation(out=da2[:, slot, 1, 1:1 + TT], in_=dt_sb[:, d0 + 1, :],
                                 func=AF.Exp,
                                 scale=a_sb[:, (d0 + 1) * NST + n:(d0 + 1) * NST + n + 1]), "sc")
                inc(s.activation(out=db2[:, slot, :, 0:1],
                                 in_=h_carry[:, n, d0:d0 + 2].unsqueeze(2),
                                 func=AF.Copy), "sc")
        assert c["sc"] == scb + 3 * NPAIR
        @block.vector
        def _(v, vb=vb, gb=gb, scb=scb, bb_at=tuple(bb_at), cb_at=tuple(cb_at)):
            for j in range(NPAIR):
                n, p_ = divmod(j, 4)
                d0 = 2 * p_; slot = j % 2
                v.wait_ge(s_sc, scb + 3 * j + 3)
                if p_ == 0:
                    v.wait_ge(s_bb, bb_at[n])
                    v.wait_ge(s_cb, cb_at[n])
                if j >= 2:
                    v.wait_ge(s_gp, gb + (j - 2) + 1)
                inc(v.tensor_tensor(
                    out=db2[:, slot, :, 1:1 + TT],
                    in0=dtu_sb[:, d0:d0 + 2, :],
                    in1=bbc[:, n % 2, :].unsqueeze(1).to_broadcast([128, 2, TT]),
                    op=ALU.mult), "ve")
                inc(v.tensor_tensor_scan(
                    out=db2[:, slot, :, :].rearrange("p a b -> p (a b)"),
                    data0=da2[:, slot, :, :].rearrange("p a b -> p (a b)"),
                    data1=db2[:, slot, :, :].rearrange("p a b -> p (a b)"),
                    initial=0.0, op0=ALU.mult, op1=ALU.add), "ve")
                inc(v.tensor_copy(out=h_carry[:, n, d0:d0 + 1],
                                  in_=db2[:, slot, 0, TT:TT + 1]), "ve")
                inc(v.tensor_copy(out=h_carry[:, n, d0 + 1:d0 + 2],
                                  in_=db2[:, slot, 1, TT:TT + 1]), "ve")
                inc(v.tensor_tensor(
                    out=db2[:, slot, :, 1:1 + TT],
                    in0=db2[:, slot, :, 1:1 + TT],
                    in1=cbc[:, n % 2, :].unsqueeze(1).to_broadcast([128, 2, TT]),
                    op=ALU.mult), "ve")
        assert c["ve"] == vb + 5 * NPAIR
        @block.gpsimd
        def _(g, vb=vb):
            for j in range(NPAIR):
                n, p_ = divmod(j, 4)
                d0 = 2 * p_; slot = j % 2
                g.wait_ge(s_ve, vb + 5 * j + 5)
                if n == 0:
                    inc(g.tensor_copy(out=yacc[:, d0:d0 + 2, :],
                                      in_=db2[:, slot, :, 1:1 + TT]), "gp")
                else:
                    inc(g.tensor_tensor(out=yacc[:, d0:d0 + 2, :],
                                        in0=yacc[:, d0:d0 + 2, :],
                                        in1=db2[:, slot, :, 1:1 + TT],
                                        op=ALU.add), "gp")
        assert c["gp"] == gb + NPAIR

        if k == 0:
            tap("t_yacc", yacc[:], gates=[("ve", c["ve"])])
        gp_lat = c["gp"]
        for dc in range(DC):
            @block.vector
            def _(v, dc=dc, gp_lat=gp_lat):
                if dc == 0:
                    v.wait_ge(s_gp, gp_lat)
                inc(v.scalar_tensor_tensor(out=yp_scr[:], in0=u_sb[:, dc, :],
                                           scalar=dp_sb[:, dc:dc + 1],
                                           in1=yacc[:, dc, :], op0=ALU.mult,
                                           op1=ALU.add), "ve")
                inc(v.tensor_tensor(out=y2h[:, dc, :], in0=yp_scr[:],
                                    in1=sz[:, dc, :], op=ALU.mult), "ve")
        ve_y2 = c["ve"]
        if k == 0:
            tap("t_y2h", y2h[:].rearrange("p a b -> p (a b)").bitcast(F32),
                gates=[("ve", c["ve"])])
        @block.sync
        def _(sync, k=k, ve_y2=ve_y2):
            sync.wait_ge(s_ve, ve_y2)
            inc(sync.dma_start(out=e2_in[k].rearrange("(dc p) t -> p dc t", p=128),
                               in_=y2h[:]), "st")
        st_e2 = c["st"]
        @block.gpsimd
        def _(g, k=k, st_e2=st_e2):
            g.wait_ge(s_st, st_e2)
            inc(g.collective_compute("AllGather", ALU.bypass, replica_groups=PAIRS,
                                     ins=[e2_in[k]], outs=[e2_out[k]]), "cc")
        cc2 = c["cc"]

        y2full = h_f[:].rearrange("p a b -> p (a b)").rearrange(
            "p (kc t) -> p kc t", kc=NKO)
        qkey = "ld" if k % 2 == 0 else "gl"
        mm_before_wo = c["mm"]
        def emit_y2_load(eng, k=k, cc2=cc2, mm_before=mm_before_wo):
            eng.wait_ge(s_cc, cc2)
            eng.wait_ge(s_mm, mm_before)
            par = eng.snap(eng.partition_id() % 2, min_val=0, max_val=1)
            inc(eng.dma_start(
                out=y2full,
                in_=e2_out[k].rearrange("(kc p) t -> p kc t", p=128)[:, :, bass.ts(par, HT)]),
                "y2")
        if k % 2 == 0:
            block.sync(emit_y2_load)
        else:
            block.gpsimd(emit_y2_load)
        ld_y2 = c["y2"]
        for kc in range(NKO):
            @block.sync
            def _(sync, kc=kc, mm_gate=c["mm"]):
                if kc >= 2:
                    sync.wait_ge(s_mm, mm_gate)
                inc(sync.dma_start(out=wo_st[:, kc % 2, :], in_=wout_bf[kc]), "wo")
            ld_wo = c["wo"]
            @block.tensor
            def _(t, kc=kc, ld_wo=ld_wo, ld_y2=ld_y2, sc_gate=c["sc"], ve_gate=c["ve"], qkey=qkey):
                t.wait_ge(s_wo, ld_wo)
                if kc == 0:
                    t.wait_ge(s_y2, ld_y2)
                    t.wait_ge(s_sc, sc_gate)
                    t.wait_ge(s_ve, ve_gate)
                for mc in range(8):
                    inc(t.matmul(pm[mc][:, 0:HT],
                                 wo_st[:, kc % 2, mc * 128:(mc + 1) * 128],
                                 y2full[:, kc, :], start=(kc == 0),
                                 stop=(kc == NKO - 1)), "mm")
        mm_wo = c["mm"]

        for ht in range(2):
            @block.sync
            def _(sync, k=k, ht=ht, gate_ve=c["ve"]):
                sync.wait_ge(s_ve, gate_ve)
                inc(sync.dma_start(out=xtile[:],
                                   in_=x_own[k][ht * 128:(ht + 1) * 128, :]), "x")
            ld_xo = c["x"]
            @block.vector
            def _(v, ht=ht, ld_xo=ld_xo):
                v.wait_ge(s_x, ld_xo)
                inc(v.bn_stats(out=st6[:, 0:6], in_=xtile[:, 0:512]), "ve")
                inc(v.bn_stats(out=st6[:, 6:12], in_=xtile[:, 512:1024]), "ve")
                v.wait_ge(s_ve, c["ve"])
                inc(v.bn_aggr(out=st2[:], in_=st6[:]), "ve")
            ve_so = c["ve"]
            @block.scalar
            def _(s, ve_so=ve_so):
                s.wait_ge(s_ve, ve_so)
                inc(s.activation(out=rs_sb[:, 0:1], in_=st2[:, 1:2], func=AF.Sqrt,
                                 bias=cst[:, 0:1]), "sc")
            sc_ro = c["sc"]
            @block.vector
            def _(v, ht=ht, sc_ro=sc_ro):
                v.wait_ge(s_sc, sc_ro)
                inc(v.reciprocal(out=rs_sb[:, 0:1], in_=rs_sb[:, 0:1]), "ve")
                _ = v.wait_ge(s_ve, c["ve"])
                inc(v.tensor_scalar(out=epim[:, ht, :], in0=xtile[:],
                                    scalar1=st2[:, 0:1], scalar2=rs_sb[:, 0:1],
                                    op0=ALU.subtract, op1=ALU.mult), "ve")
        for mc in range(8):
            @block.scalar
            def _(s, mc=mc, mm_wo=mm_wo, ve_gate=c["ve"]):
                s.wait_ge(s_mm, mm_wo)
                inc(s.activation(out=sp3[:, (mc % 2) * HT:(mc % 2 + 1) * HT],
                                 in_=pm[mc][:, 0:HT], func=AF.Copy), "sc")
            sc_om = c["sc"]
            @block.tensor
            def _(t, mc=mc, sc_om=sc_om, ve_gate=c["ve"]):
                t.wait_ge(s_sc, sc_om)
                if mc >= 2:
                    t.wait_ge(s_ve, ve_gate)
                for ht in range(2):
                    inc(t.matmul(pm[mc][:, HT + ht * 128: HT + (ht + 1) * 128],
                                 sp3[:, (mc % 2) * HT + ht * 128:(mc % 2) * HT + (ht + 1) * 128],
                                 id_f[:], start=True, stop=True, is_transpose=True), "mm")
            mm_omt = c["mm"]
            @block.vector
            def _(v, mc=mc, mm_omt=mm_omt):
                v.wait_ge(s_mm, mm_omt)
                for ht in range(2):
                    sl = epim[:, ht, mc * 128:(mc + 1) * 128]
                    inc(v.tensor_tensor(out=sl, in0=sl,
                                        in1=pm[mc][:, HT + ht * 128: HT + (ht + 1) * 128],
                                        op=ALU.add), "ve")
        if k == 0:
            tap("t_epim", epim[:], gates=[("ve", c["ve"])])
        for ht in range(2):
            @block.sync
            def _(sync, k=k, ht=ht, ve_gate=c["ve"]):
                sync.wait_ge(s_ve, ve_gate)
                inc(sync.dma_start(out=xtile[:],
                                   in_=x_own[k][ht * 128:(ht + 1) * 128, :]), "x")
            ld_xr = c["x"]
            @block.vector
            def _(v, ht=ht):
                inc(v.bn_stats(out=st6[:, 0:6], in_=epim[:, ht, 0:512]), "ve")
                inc(v.bn_stats(out=st6[:, 6:12], in_=epim[:, ht, 512:1024]), "ve")
                v.wait_ge(s_ve, c["ve"])
                inc(v.bn_aggr(out=st2[:], in_=st6[:]), "ve")
            ve_st3 = c["ve"]
            @block.scalar
            def _(s, ve_st3=ve_st3):
                s.wait_ge(s_ve, ve_st3)
                inc(s.activation(out=rs_sb[:, 2:3], in_=st2[:, 1:2], func=AF.Sqrt,
                                 bias=cst[:, 0:1]), "sc")
            sc_r3 = c["sc"]
            @block.vector
            def _(v, ht=ht, sc_r3=sc_r3, ld_xr=ld_xr):
                v.wait_ge(s_sc, sc_r3)
                v.wait_ge(s_x, ld_xr)
                inc(v.reciprocal(out=rs_sb[:, 2:3], in_=rs_sb[:, 2:3]), "ve")
                _ = v.wait_ge(s_ve, c["ve"])
                inc(v.tensor_scalar(out=epim[:, ht, :], in0=epim[:, ht, :],
                                    scalar1=st2[:, 0:1],
                                    scalar2=rs_sb[:, 2:3], op0=ALU.subtract,
                                    op1=ALU.mult), "ve")
                inc(v.tensor_tensor(out=epim[:, ht, :], in0=epim[:, ht, :],
                                    in1=xtile[:], op=ALU.add), "ve")
            ve_xs1 = c["ve"]
            if k == 0:
                tap_xs1_ht = ht
                if DEBUG_TAPS:
                    @block.scalar
                    def _(s, ht=ht, gv=c["ve"]):
                        s.wait_ge(s_ve, gv)
                        s.dma_start(out=taps["t_xs1"][:, ht, :], in_=epim[:, ht, :]).then_inc(s_tp, 16)
                        tapcnt[0] += 16
                        s.wait_ge(s_tp, tapcnt[0])
            @block.sync
            def _(sync, k=k, ht=ht, ve_xs1=ve_xs1):
                sync.wait_ge(s_ve, ve_xs1)
                inc(sync.dma_start(out=xs1_sp[k][ht * 128:(ht + 1) * 128, :],
                                   in_=epim[:, ht, :]), "st")
        st_xs1_prev = c["st"]

    cM = dict(c)
    ctxM.close()

    ctxF = ExitStack()
    sbF = lambda name, shape, dtype=F32: ctxF.enter_context(
        nc.sbuf_tensor(name, list(shape), dtype))
    wfc_sb = sbF("wfc_sb", (128, KC, HF), BF16)
    wpj_sb = sbF("wpj_sb", (128, HKC, D), BF16)
    fstage = sbF("fstage", (128, 2048))
    qs1 = sbF("qs1", (128, 2048))
    qs2 = sbF("qs2", (128, 2048))
    wq_t = sbF("wq_t", (128, D), BF16)
    g_bf = sbF("g_bf", (128, HKC, HT), BF16)
    g2_scr = sbF("g2_scr", (128, HT), BF16)
    ebuf = sbF("ebuf", (128, HT))
    rowsc = sbF("rowsc", (1, 2 * HT))
    sa_bc = sbF("sa_bc", (128, HT))
    nb_bc = sbF("nb_bc", (128, HT))
    nsw_sb = sbF("nsw_sb", (128, 8))
    bo_sb = sbF("bo_sb", (128, 8))
    fo = sbF("fo", (128, 2, HT))
    fo_tok = sbF("fo_tok", (128, 2, D))
    xs1c = sbF("xs1c", (128, 2, D))
    xln_b = sbF("xln_b", (128, D), BF16)
    xlnf = sbF("xlnf", (128, KC, HT), BF16)

    ALLK = tuple(S.keys())
    @block.sync
    def _(sync):
        for key in ALLK:
            sync.wait_ge(S[key], cM[key])
    @block.gpsimd
    def _(g):
        for key in ALLK:
            g.wait_ge(S[key], cM[key])
    @block.tensor
    def _(t):
        for key in ALLK:
            t.wait_ge(S[key], cM[key])
    @block.scalar
    def _(s):
        for key in ALLK:
            s.wait_ge(S[key], cM[key])
    @block.vector
    def _(v):
        for key in ALLK:
            v.wait_ge(S[key], cM[key])

    for kc in range(KC):
        for hh in range(2):
            @block.gpsimd
            def _(g, kc=kc, hh=hh, ve_gate=c["ve"]):
                g.wait_ge(s_ve, ve_gate)
                inc(g.dma_start(out=fstage[:],
                                in_=w_fcT[kc * 128:(kc + 1) * 128,
                                          hh * 2048:(hh + 1) * 2048]), "gl")
            gl_q = c["gl"]
            @block.vector
            def _(v, kc=kc, hh=hh, gl_q=gl_q):
                v.wait_ge(s_gl, gl_q)
                inc(v.tensor_scalar(out=qs1[:], in0=fstage[:], scalar1=thr_sb[:, 0:1],
                                    scalar2=None, op0=ALU.is_ge), "ve")
                inc(v.tensor_scalar(out=qs2[:], in0=fstage[:], scalar1=thr_sb[:, 1:2],
                                    scalar2=None, op0=ALU.is_le), "ve")
                inc(v.tensor_tensor(out=wfc_sb[:, kc, hh * 2048:(hh + 1) * 2048],
                                    in0=qs1[:], in1=qs2[:], op=ALU.subtract), "ve")

    for kc2 in range(HKC):
        @block.gpsimd
        def _(g, kc2=kc2, ve_gate=c["ve"]):
            g.wait_ge(s_ve, ve_gate)
            inc(g.dma_start(out=fstage[:, 0:D],
                            in_=w_projT[kc2 * 128:(kc2 + 1) * 128, :]), "gl")
        gl_q = c["gl"]
        @block.vector
        def _(v, kc2=kc2, gl_q=gl_q, mm_gate=c["mm"]):
            v.wait_ge(s_gl, gl_q)
            if kc2 > 0:
                v.wait_ge(s_mm, mm_gate)
            inc(v.tensor_scalar(out=qs1[:, 0:D], in0=fstage[:, 0:D],
                                scalar1=thr_sb[:, 2:3], scalar2=None, op0=ALU.is_ge), "ve")
            inc(v.tensor_scalar(out=qs2[:, 0:D], in0=fstage[:, 0:D],
                                scalar1=thr_sb[:, 3:4], scalar2=None, op0=ALU.is_le), "ve")
            inc(v.tensor_tensor(out=wq_t[:], in0=qs1[:, 0:D], in1=qs2[:, 0:D],
                                op=ALU.subtract), "ve")
        ve_wq = c["ve"]
        @block.tensor
        def _(t, kc2=kc2, ve_wq=ve_wq):
            t.wait_ge(s_ve, ve_wq)
            for mc in range(8):
                inc(t.matmul(pm[6][:, mc:mc + 1], wq_t[:, mc * 128:(mc + 1) * 128],
                             lnbf_b[:, kc2:kc2 + 1], start=True, stop=True), "mm")
                inc(t.matmul(pm[7][:, mc:mc + 1], wq_t[:, mc * 128:(mc + 1) * 128],
                             lnwf_b[:, kc2:kc2 + 1], start=True, stop=True), "mm")
        mm_bo = c["mm"]
        @block.vector
        def _(v, kc2=kc2, mm_bo=mm_bo):
            v.wait_ge(s_mm, mm_bo)
            if kc2 == 0:
                inc(v.tensor_copy(out=bo_sb[:], in_=pm[6][:, 0:8]), "ve")
                inc(v.tensor_copy(out=nsw_sb[:], in_=pm[7][:, 0:8]), "ve")
            else:
                inc(v.tensor_tensor(out=bo_sb[:], in0=bo_sb[:],
                                    in1=pm[6][:, 0:8], op=ALU.add), "ve")
                inc(v.tensor_tensor(out=nsw_sb[:], in0=nsw_sb[:],
                                    in1=pm[7][:, 0:8], op=ALU.add), "ve")
            v.wait_ge(s_ve, c["ve"])
            inc(v.tensor_scalar(out=wpj_sb[:, kc2, :], in0=wq_t[:],
                                scalar1=lnwf_sb[:, kc2:kc2 + 1], scalar2=None,
                                op0=ALU.mult), "ve")
    @block.vector
    def _(v):
        inc(v.tensor_scalar(out=nsw_sb[:], in0=nsw_sb[:], scalar1=-1.0,
                            scalar2=None, op0=ALU.mult), "ve")
        v.wait_ge(s_ve, c["ve"])
    ve_bo = c["ve"]
    @block.scalar
    def _(s, ve_bo=ve_bo):
        s.wait_ge(s_ve, ve_bo)
    sc_bo = c["sc"]

    if DEBUG_TAPS:
        tap("t_thr", thr_sb[:], gates=[("pc", c["pc"])])
        tap("t_wfc", wfc_sb[:, 0, :].bitcast(F32), gates=[("ve", c["ve"])])
        tap("t_wpj", wpj_sb[:].rearrange("p a b -> p (a b)").bitcast(F32),
            gates=[("ve", c["ve"])])
        if DEBUG_TAPS:
            @block.scalar
            def _(s, gv=c["sc"]):
                s.wait_ge(s_sc, gv)
                s.dma_start(out=taps["t_nswbo"][:, 0:8], in_=nsw_sb[:]).then_inc(s_tp, 16)
                tapcnt[0] += 16
                s.wait_ge(s_tp, tapcnt[0])
        @block.scalar
        def _(s, gv=c["sc"]):
            s.wait_ge(s_sc, gv)
            s.dma_start(out=taps["t_nswbo"][:, 8:16], in_=bo_sb[:]).then_inc(s_tp, 16)
            tapcnt[0] += 16
            s.wait_ge(s_tp, tapcnt[0])

    for h in range(NCH):
        @block.sync
        def _(sync, h=h, ve_gate=c["ve"]):
            if h == 0:
                sync.wait_ge(s_sc, sc_bo)
            sync.wait_ge(s_ve, ve_gate)
            inc(sync.dma_start(out=xs1c[:],
                               in_=xs1_sp[h].rearrange("(a p) d -> p a d", p=128)), "x")
        ld_xs1 = c["x"]
        for ht in range(2):
            @block.vector
            def _(v, ht=ht, ld_xs1=ld_xs1):
                v.wait_ge(s_x, ld_xs1)
                inc(v.bn_stats(out=st6[:, 0:6], in_=xs1c[:, ht, 0:512]), "ve")
                inc(v.bn_stats(out=st6[:, 6:12], in_=xs1c[:, ht, 512:1024]), "ve")
                v.wait_ge(s_ve, c["ve"])
                inc(v.bn_aggr(out=st2[:], in_=st6[:]), "ve")
            ve_sf = c["ve"]
            @block.scalar
            def _(s, ve_sf=ve_sf):
                s.wait_ge(s_ve, ve_sf)
                inc(s.activation(out=rs_sb[:, 0:1], in_=st2[:, 1:2], func=AF.Sqrt,
                                 bias=cst[:, 0:1]), "sc")
            sc_rf = c["sc"]
            @block.vector
            def _(v, ht=ht, sc_rf=sc_rf):
                v.wait_ge(s_sc, sc_rf)
                inc(v.reciprocal(out=rs_sb[:, 0:1], in_=rs_sb[:, 0:1]), "ve")
                _ = v.wait_ge(s_ve, c["ve"])
                inc(v.tensor_scalar(out=xln_b[:], in0=xs1c[:, ht, :],
                                    scalar1=st2[:, 0:1], scalar2=rs_sb[:, 0:1],
                                    op0=ALU.subtract, op1=ALU.mult), "ve")
            ve_xln = c["ve"]
            @block.tensor
            def _(t, ht=ht, ve_xln=ve_xln, sc_gate=c["sc"]):
                t.wait_ge(s_ve, ve_xln)
                t.wait_ge(s_sc, sc_gate)
                pmb = pm[4][:].bitcast(BF16)
                for kc in range(KC):
                    inc(t.matmul(pmb[:, kc * 128:(kc + 1) * 128],
                                 xln_b[:, kc * 128:(kc + 1) * 128], id_b[:],
                                 start=True, stop=True, is_transpose=True), "mm")
            mm_trf = c["mm"]
            @block.scalar
            def _(s, ht=ht, mm_trf=mm_trf):
                s.wait_ge(s_mm, mm_trf)
                inc(s.activation(out=xlnf[:, :, ht * 128:(ht + 1) * 128],
                                 in_=pm[4][:].bitcast(BF16).rearrange(
                                     "p (kc t) -> p kc t", kc=KC),
                                 func=AF.Copy), "sc")
        sc_xlnf = c["sc"]
        if h == 0:
            tap("t_xlnf", xlnf[:].rearrange("p a b -> p (a b)").bitcast(F32),
                gates=[("sc", c["sc"])])

        for mc in range(HKC):
            bank = pm[mc % 4]
            @block.tensor
            def _(t, mc=mc, bank=bank, sc_xlnf=sc_xlnf, sc_gate=c["sc"], ve_gate=c["ve"]):
                t.wait_ge(s_sc, sc_gate)
                t.wait_ge(s_ve, ve_gate)
                for kc in range(KC):
                    inc(t.matmul(bank[:, 0:HT], wfc_sb[:, kc, mc * 128:(mc + 1) * 128],
                                 xlnf[:, kc, :], start=(kc == 0), stop=(kc == KC - 1)),
                        "mm")
            mm_fc = c["mm"]
            @block.scalar
            def _(s, mc=mc, bank=bank, mm_fc=mm_fc):
                s.wait_ge(s_mm, mm_fc)
                inc(s.activation(out=ebuf[:], in_=bank[:, 0:HT], func=AF.Erf,
                                 scale=cst[:, 4:5]), "sc")
            sc_erf = c["sc"]
            @block.vector
            def _(v, mc=mc, bank=bank, sc_erf=sc_erf):
                v.wait_ge(s_sc, sc_erf)
                inc(v.scalar_tensor_tensor(out=g_bf[:, mc, :], in0=ebuf[:],
                                           scalar=1.0, in1=bank[:, 0:HT],
                                           op0=ALU.add, op1=ALU.mult), "ve")
            @block.vector
            def _(v, mc=mc):
                inc(v.tensor_tensor(out=g2_scr[:], in0=g_bf[:, mc, :],
                                    in1=g_bf[:, mc, :], op=ALU.mult), "ve")
            ve_g = c["ve"]
            @block.tensor
            def _(t, mc=mc, ve_g=ve_g):
                t.wait_ge(s_ve, ve_g)
                inc(t.matmul(pm[6][0:1, 0:HT], ones_b[:], g_bf[:, mc, :],
                             start=(mc == 0), stop=(mc == HKC - 1)), "mm")
                inc(t.matmul(pm[7][0:1, 0:HT], ones_b[:], g2_scr[:],
                             start=(mc == 0), stop=(mc == HKC - 1)), "mm")
        mm_st = c["mm"]
        if h == 0:
            tap("t_gbf", g_bf[:].rearrange("p a b -> p (a b)").bitcast(F32),
                gates=[("ve", c["ve"]), ("sc", c["sc"])])
        @block.vector
        def _(v, mm_st=mm_st):
            v.wait_ge(s_mm, mm_st)
            inc(v.tensor_scalar(out=rowsc[0:1, 0:HT], in0=pm[6][0:1, 0:HT],
                                scalar1=0.5 / HF, scalar2=None, op0=ALU.mult), "ve")
            inc(v.tensor_scalar(out=rowsc[0:1, HT:2 * HT], in0=pm[7][0:1, 0:HT],
                                scalar1=0.25 / HF, scalar2=None, op0=ALU.mult), "ve")
            v.wait_ge(s_ve, c["ve"])
            inc(v.tensor_tensor(out=ebuf[0:1, 0:HT], in0=rowsc[0:1, 0:HT],
                                in1=rowsc[0:1, 0:HT], op=ALU.mult), "ve")
            v.wait_ge(s_ve, c["ve"])
            inc(v.tensor_tensor(out=ebuf[0:1, 0:HT], in0=rowsc[0:1, HT:2 * HT],
                                in1=ebuf[0:1, 0:HT], op=ALU.subtract), "ve")
        ve_var = c["ve"]
        @block.scalar
        def _(s, ve_var=ve_var):
            s.wait_ge(s_ve, ve_var)
            inc(s.activation(out=ebuf[0:1, 0:HT], in_=ebuf[0:1, 0:HT],
                             func=AF.Sqrt, bias=cst[0:1, 0:1]), "sc")
        sc_rstdf = c["sc"]
        @block.vector
        def _(v, sc_rstdf=sc_rstdf):
            v.wait_ge(s_sc, sc_rstdf)
            inc(v.reciprocal(out=ebuf[0:1, 0:HT], in_=ebuf[0:1, 0:HT]), "ve")
            v.wait_ge(s_ve, c["ve"])
            inc(v.tensor_tensor(out=rowsc[0:1, 0:HT], in0=rowsc[0:1, 0:HT],
                                in1=ebuf[0:1, 0:HT], op=ALU.mult), "ve")
            inc(v.tensor_scalar(out=rowsc[0:1, HT:2 * HT], in0=ebuf[0:1, 0:HT],
                                scalar1=0.5, scalar2=None, op0=ALU.mult), "ve")
        ve_rows = c["ve"]
        @block.sync
        def _(sync, h=h, ve_rows=ve_rows):
            sync.wait_ge(s_ve, ve_rows)
            inc(sync.dma_start(out=row_dram[h], in_=rowsc[0:1, :].rearrange(
                "o (a t) -> o a t", a=2)), "st")
        st_rows = c["st"]
        @block.sync
        def _(sync, h=h, st_rows=st_rows, ve_gate=c["ve"]):
            sync.wait_ge(s_st, st_rows)
            inc(sync.dma_start(out=nb_bc[:], in_=row_dram[h][0:1, :].to_broadcast([128, HT])), "rw")
            inc(sync.dma_start(out=sa_bc[:], in_=row_dram[h][1:2, :].to_broadcast([128, HT])), "rw")
        ld_rows = c["rw"]

        if h == 0:
            tap("t_rows", rowsc[0:1, :], gates=[("ve", c["ve"])])
        for mc2 in range(8):
            bank = pm[mc2 % 4]
            @block.tensor
            def _(t, mc2=mc2, bank=bank, ve_gate=c["ve"], sc_gate=c["sc"]):
                t.wait_ge(s_ve, ve_gate)
                t.wait_ge(s_sc, sc_gate)
                for kc2 in range(HKC):
                    inc(t.matmul(bank[:, 0:HT], wpj_sb[:, kc2, mc2 * 128:(mc2 + 1) * 128],
                                 g_bf[:, kc2, :], start=(kc2 == 0),
                                 stop=(kc2 == HKC - 1)), "mm")
            mm_pj = c["mm"]
            @block.vector
            def _(v, mc2=mc2, bank=bank, mm_pj=mm_pj, ld_rows=ld_rows):
                v.wait_ge(s_mm, mm_pj)
                v.wait_ge(s_rw, ld_rows)
                inc(v.tensor_tensor(out=fo[:, mc2 % 2, :], in0=bank[:, 0:HT],
                                    in1=sa_bc[:], op=ALU.mult), "ve")
                inc(v.scalar_tensor_tensor(out=fo[:, mc2 % 2, :], in0=nb_bc[:],
                                           scalar=nsw_sb[:, mc2:mc2 + 1],
                                           in1=fo[:, mc2 % 2, :], op0=ALU.mult,
                                           op1=ALU.add), "ve")
                inc(v.tensor_scalar(out=fo[:, mc2 % 2, :], in0=fo[:, mc2 % 2, :],
                                    scalar1=bo_sb[:, mc2:mc2 + 1], scalar2=None,
                                    op0=ALU.add), "ve")
            ve_fo = c["ve"]
            @block.tensor
            def _(t, mc2=mc2, bank=bank, ve_fo=ve_fo):
                t.wait_ge(s_ve, ve_fo)
                for ht in range(2):
                    inc(t.matmul(bank[:, HT + ht * 128:HT + (ht + 1) * 128],
                                 fo[:, mc2 % 2, ht * 128:(ht + 1) * 128], id_f[:],
                                 start=True, stop=True, is_transpose=True), "mm")
            mm_fot = c["mm"]
            @block.scalar
            def _(s, mc2=mc2, bank=bank, mm_fot=mm_fot):
                s.wait_ge(s_mm, mm_fot)
                for ht in range(2):
                    inc(s.activation(out=fo_tok[:, ht, mc2 * 128:(mc2 + 1) * 128],
                                     in_=bank[:, HT + ht * 128:HT + (ht + 1) * 128],
                                     func=AF.Copy), "sc")
        sc_fot = c["sc"]
        if h == 0:
            tap("t_fotok", fo_tok[:], gates=[("sc", c["sc"])])
        for ht in range(2):
            @block.vector
            def _(v, ht=ht, sc_fot=sc_fot):
                v.wait_ge(s_sc, sc_fot)
                inc(v.bn_stats(out=st6[:, 0:6], in_=fo_tok[:, ht, 0:512]), "ve")
                inc(v.bn_stats(out=st6[:, 6:12], in_=fo_tok[:, ht, 512:1024]), "ve")
                v.wait_ge(s_ve, c["ve"])
                inc(v.bn_aggr(out=st2[:], in_=st6[:]), "ve")
            ve_s4 = c["ve"]
            @block.scalar
            def _(s, ve_s4=ve_s4):
                s.wait_ge(s_ve, ve_s4)
                inc(s.activation(out=rs_sb[:, 3:4], in_=st2[:, 1:2], func=AF.Sqrt,
                                 bias=cst[:, 0:1]), "sc")
            sc_r4 = c["sc"]
            @block.vector
            def _(v, ht=ht, sc_r4=sc_r4):
                v.wait_ge(s_sc, sc_r4)
                inc(v.reciprocal(out=rs_sb[:, 3:4], in_=rs_sb[:, 3:4]), "ve")
                _ = v.wait_ge(s_ve, c["ve"])
                inc(v.tensor_scalar(out=fstage[:, 0:D], in0=fo_tok[:, ht, :],
                                    scalar1=st2[:, 0:1], scalar2=rs_sb[:, 3:4],
                                    op0=ALU.subtract, op1=ALU.mult), "ve")
                inc(v.tensor_tensor(out=fstage[:, D:2 * D], in0=fstage[:, 0:D],
                                    in1=xs1c[:, ht, :], op=ALU.add), "ve")
            ve_fin = c["ve"]
            @block.sync
            def _(sync, h=h, ht=ht, ve_fin=ve_fin):
                sync.wait_ge(s_ve, ve_fin)
                inc(sync.dma_start(out=y_out[h][ht * 128:(ht + 1) * 128, :],
                                   in_=fstage[:, D:2 * D]), "st")

    st_total = c["st"]
    @block.sync
    def _(sync):
        sync.wait_ge(s_st, st_total)

    ctxF.close()
    ctx.close()
    return nc


_NC_CACHE = None


def kernel(**inputs):
    global _NC_CACHE
    f32 = np.float32
    x = np.ascontiguousarray(inputs["L0_attn_x"], dtype=f32)
    W_in = np.asarray(inputs["L0_attn_W_in"], f32)
    ln_g = np.asarray(inputs["L0_attn_ln_g"], f32)
    ln_b = np.asarray(inputs["L0_attn_ln_b"], f32)
    conv_w = np.asarray(inputs["L0_attn_conv_w"], f32)
    conv_b = np.asarray(inputs["L0_attn_conv_b"], f32)
    W_x = np.asarray(inputs["L0_attn_W_x"], f32)
    W_dt = np.asarray(inputs["L0_attn_W_dt"], f32)
    b_dt = np.asarray(inputs["L0_attn_b_dt"], f32)
    A_log = np.asarray(inputs["L0_attn_A_log"], f32)
    Dp = np.asarray(inputs["L0_attn_Dp"], f32)
    W_out = np.asarray(inputs["L0_attn_W_out"], f32)
    W_fc = np.asarray(inputs["L0_ffn_W_fc"], f32)
    lnw = np.asarray(inputs["L0_ffn_ln_w"], f32)
    lnb = np.asarray(inputs["L0_ffn_ln_b"], f32)
    W_proj = np.asarray(inputs["L0_ffn_W_proj"], f32)

    CC = np.ascontiguousarray
    def dlay(vec, nk):
        return CC(vec.reshape(nk, 128).T)

    in_maps = []
    for core in range(8):
        p, q = core // 2, core % 2
        ch = slice(q * EH, (q + 1) * EH)
        w_inT_slice = CC(np.concatenate([W_in[ch], W_in[DIN + q * EH: DIN + (q + 1) * EH]],
                                        axis=0).T)
        m = {
            "x": CC(x[p]),
            "w_inT": w_inT_slice,
            "ln_g": dlay(ln_g, KC),
            "ln_b": dlay(ln_b, KC),
            "conv_w": CC(conv_w[ch, 0, :].reshape(DC, 128, 4).transpose(1, 0, 2).reshape(128, DC * 4)),
            "conv_b": dlay(conv_b[ch], DC),
            "w_xT": CC(W_x[:, ch].T),
            "w_dtT": CC(W_dt[ch].T),
            "b_dt": dlay(b_dt[ch], DC),
            "a_log": CC(A_log[ch].reshape(DC, 128, NST).transpose(1, 0, 2).reshape(128, DC * NST)),
            "dp": dlay(Dp[ch], DC),
            "w_outT": CC(W_out.T),
            "w_fcT": CC(W_fc.T),
            "lnw_f": dlay(lnw, HKC),
            "lnb_f": dlay(lnb, HKC),
            "w_projT": CC(W_proj.T),
            "ident": np.eye(128, dtype=f32),
            "x_own": CC(x[p].reshape(NCH, 2, HT, D)[:, q]),
        }
        in_maps.append(m)

    if _NC_CACHE is None:
        _NC_CACHE = build_nc()
    nc = _NC_CACHE
    res = run_bass_kernel_spmd(nc, in_maps, core_ids=list(range(8)))

    out = np.empty((4, TSEQ, D), dtype=f32)
    for core in range(8):
        p, q = core // 2, core % 2
        yc = res.results[core]["y"]
        for k in range(NCH):
            t0 = k * TT + q * HT
            out[p, t0:t0 + HT, :] = yc[k]
    if DEBUG_TAPS:
        kernel.LAST_TAPS = [dict(r) for r in res.results]
    return out
